# version 09
# baseline (speedup 1.0000x reference)
import numpy as np
import ml_dtypes
from contextlib import ExitStack

import concourse.bass as bass
import concourse.mybir as mybir
from concourse.bass_utils import run_bass_kernel_spmd

F32 = mybir.dt.float32
BF16 = mybir.dt.bfloat16
F16 = mybir.dt.float16
AF = mybir.ActivationFunctionType
ALU = mybir.AluOpType
AX = mybir.AxisListType
I32 = mybir.dt.int32
BF = ml_dtypes.bfloat16

NCORES = 8
TOK = 2048
D = 1024
C = 128
NCH = TOK // C
HEADS = 4
DK = 256
DV = 512
EPS = 1e-5
LN16 = float(np.log(16.0))

ENGS = ("tensor", "vector", "scalar", "gpsimd", "sync")


class Prog:

    NSLOT = 4

    def __init__(self, nc):
        self.nc = nc
        self.ops = {e: [] for e in ENGS}
        self.nops = {e: 0 for e in ENGS}
        self.qcount = {}
        self.last_write = {}
        self.readers = {}
        self.last_psum = {}
        self.waited = {e: {} for e in ENGS}
        self.need_inc = {e: set() for e in ENGS}

    def _src_key(self, tok):
        return ("E", tok[1]) if tok[0] == "E" else ("Q",) + tuple(tok[1])

    def rec(self, eng, emit, reads=(), writes=(), queue=None, tag=""):
        deps = {}

        def add(tok):
            k = self._src_key(tok)
            v = tok[2]
            if k not in deps or deps[k] < v:
                deps[k] = v

        for r in reads:
            t = self.last_write.get(r)
            if t is not None:
                add(t)
            if r.startswith("ps_"):
                t2 = self.last_psum.get(r)
                if t2 is not None:
                    add(t2)
        for w in writes:
            t = self.last_write.get(w)
            if t is not None:
                add(t)
            for kk, vv in self.readers.get(w, {}).items():
                add(("E", kk[1], vv) if kk[0] == "E" else ("Q", (kk[1], kk[2]), vv))
            if w.startswith("ps_"):
                t2 = self.last_psum.get(w)
                if t2 is not None:
                    add(t2)

        if queue is not None:
            idx = None
            n = self.qcount.get(queue, 0) + 1
            self.qcount[queue] = n
            slot = (n - 1) % self.NSLOT
            gen = (n - 1) // self.NSLOT + 1
            my = ("Q", (queue, slot), gen)
            if gen > 1:
                add(("Q", (queue, slot), gen - 1))
        else:
            self.nops[eng] += 1
            idx = self.nops[eng]
            my = ("E", eng, idx)
        myk = self._src_key(my)

        waits = []
        for k, v in deps.items():
            if k[0] == "E" and k[1] == "tensor" and eng == "tensor":
                continue
            got = self.waited[eng].get(k, 0)
            if v <= got:
                continue
            waits.append((k, v))
            self.waited[eng][k] = v
            if k[0] == "E":
                self.need_inc[k[1]].add(v)

        for r in reads:
            self.readers.setdefault(r, {})[myk] = max(
                self.readers.get(r, {}).get(myk, 0), my[2]
            )
            if r.startswith("ps_"):
                self.last_psum[r] = my
        for w in writes:
            self.last_write[w] = my
            self.readers[w] = {}
            if w.startswith("ps_"):
                self.last_psum[w] = my

        self.ops[eng].append(
            dict(emit=emit, waits=waits, queue=queue, idx=idx, tag=tag)
        )
        return my

    def _queue_toks(self):
        toks = []
        for q, n in self.qcount.items():
            for slot in range(min(n, self.NSLOT)):
                gen = (n - 1 - slot) // self.NSLOT + 1
                toks.append(("Q", (q, slot), gen))
        return toks

    def barrier(self):
        toks = []
        for e in ENGS:
            if self.nops[e]:
                toks.append(("E", e, self.nops[e]))
        toks += self._queue_toks()
        for e in ENGS:
            for t in toks:
                k = self._src_key(t)
                if k == ("E", e):
                    continue
                v = t[2]
                if self.waited[e].get(k, 0) < v:
                    self.waited[e][k] = v
                    if k[0] == "E":
                        self.need_inc[k[1]].add(v)
                    self.ops[e].append(dict(emit=None, waits=[(k, v)], queue=None,
                                            idx=None, tag="barrier"))

    def finalize(self, es, block):
        sems = {}
        for e in ENGS:
            sems[("E", e)] = es.enter_context(self.nc.semaphore(f"sem_{e}"))
        for q in self.qcount:
            for slot in range(self.NSLOT):
                sems[("Q", q, slot)] = es.enter_context(
                    self.nc.semaphore(f"sem_{q}_{slot}"))
        inc_prefix = {}
        for e in ENGS:
            pref = {}
            cnt = 0
            marks = self.need_inc[e]
            for i in range(1, self.nops[e] + 1):
                if i in marks:
                    cnt += 1
                pref[i] = cnt
            inc_prefix[e] = pref

        def val_of(k, v):
            if k[0] == "Q":
                return 16 * v
            return inc_prefix[k[1]].get(v, 0)

        qseen = {}

        def body(e):
            def run(eng):
                for op in self.ops[e]:
                    for (k, v) in op["waits"]:
                        eng.wait_ge(sems[k], val_of(k, v))
                    if op["emit"] is None:
                        continue
                    inst = op["emit"](eng)
                    if op["queue"] is not None:
                        q = op["queue"]
                        n = qseen.get(q, 0)
                        qseen[q] = n + 1
                        inst.then_inc(sems[("Q", q, n % self.NSLOT)], 16)
                    elif op["idx"] in self.need_inc[e]:
                        inst.then_inc(sems[("E", e)], 1)
            return run

        for e in ENGS:
            if self.ops[e]:
                getattr(block, e)(body(e))


def _tile_consts():
    i = np.arange(C)
    trib = ((i[:, None] <= i[None, :]).astype(np.float32) / 16.0).astype(BF)
    trir = ((i[:, None] > i[None, :]).astype(np.float32) / 16.0).astype(BF)
    mask = np.tile((i[:, None] <= i[None, :]).astype(np.float32), (1, 4))
    identb = np.eye(C, dtype=np.float32).astype(BF)
    identf = np.eye(C, dtype=np.float32)
    onesb = np.ones((C, C), dtype=np.float32).astype(BF)
    esel = np.zeros((8, 512), dtype=np.float32)
    for p in range(512):
        esel[p // 64, p] = 1.0
    return dict(trib=trib, trir=trir, mask=mask, identb=identb, identf=identf,
                onesb=onesb, esel=esel.astype(BF))


def _cmat(core):
    m_ = np.zeros((8, 8), dtype=np.float32)
    for r in range(8):
        for m in range(8):
            if r < m < core:
                m_[m, r] = 1.0
    for r in range(8):
        if r >= core:
            m_[:, r] = 0.0
    return m_.astype(BF)


def _wmask(core):
    m = np.zeros((8,), np.float32)
    m[:core] = 1.0
    return m



def build_phase_a():
    nc = bass.Bass("TRN2", target_bir_lowering=False, debug=False)
    ap = {}
    ap["x"] = nc.dram_tensor("x", [TOK, D], F32, kind="ExternalInput").ap()
    for n, shp in [("wq", [D, D]), ("wk", [D, D]), ("wv", [D, 2 * D]),
                   ("wgk1", [D, 16]), ("wgk2", [16, D]), ("bgk2", [1, D])]:
        ap[n] = nc.dram_tensor(n, shp, F32, kind="ExternalInput").ap()
    for n, shp in [("ctrib", [C, C]), ("ctrir", [C, C]), ("cidentb", [C, C]),
                   ("conesb", [C, C])]:
        ap[n] = nc.dram_tensor(n, shp, BF16, kind="ExternalInput").ap()
    ap["cmask"] = nc.dram_tensor("cmask", [C, 4 * C], F32, kind="ExternalInput").ap()
    ap["cidentf"] = nc.dram_tensor("cidentf", [C, C], F32, kind="ExternalInput").ap()
    ap["cln16"] = nc.dram_tensor("cln16", [C, 1], F32, kind="ExternalInput").ap()
    ap["o16_d"] = nc.dram_tensor("o16_d", [NCH, C, 2 * D], F16, kind="ExternalOutput").ap()
    ap["qgT_d"] = nc.dram_tensor("qgT_d", [NCH, C, D], BF16, kind="ExternalOutput").ap()
    ap["base_d"] = nc.dram_tensor("base_d", [NCH, C, D], F32, kind="ExternalOutput").ap()
    ap["xnT_d"] = nc.dram_tensor("xnT_d", [NCH, C, D], BF16, kind="ExternalOutput").ap()
    ap["T_d"] = nc.dram_tensor("T_d", [C, 8 * DV], BF16, kind="ExternalOutput").ap()
    ap["btot_d"] = nc.dram_tensor("btot_d", [1, D], F32, kind="ExternalOutput").ap()

    es = ExitStack()
    with es:
        def sb(n, s, t):
            return es.enter_context(nc.sbuf_tensor(n, s, t))[:]

        def psm(n, s, t):
            return es.enter_context(nc.psum_tensor(n, s, t))[:]

        w_q = sb("w_q", [128, 8 * D], BF16)
        w_k = sb("w_k", [128, 8 * D], BF16)
        w_v = sb("w_v", [128, 8 * 2 * D], BF16)
        w_gk1 = sb("w_gk1", [128, 8 * 16], BF16)
        w_gk2 = sb("w_gk2", [16, D], BF16)
        s_bgk2 = sb("s_bgk2", [1, D], BF16)
        c_trib = sb("c_trib", [C, C], BF16)
        c_trir = sb("c_trir", [C, C], BF16)
        c_identb = sb("c_identb", [C, C], BF16)
        c_ones = sb("c_ones", [C, C], BF16)
        c_mask4 = sb("c_mask4", [C, 4 * C], F32)
        c_ln16 = sb("c_ln16", [C, 1], F32)
        s_Sb = sb("s_Sb", [128, 8 * DV], BF16)
        s_bpreT = sb("s_bpreT", [C, 8], F32)
        s_btrow = sb("s_btrow", [1, D], F32)
        c_identf = sb("c_identf", [C, C], F32)

        spec = {
            "x": ([C, D], F32, 2), "st": ([C, 8], F32, 1), "xn": ([C, D], F32, 1),
            "xnb": ([C, D], BF16, 1), "xnT": ([C, D], BF16, 3), "f1": ([C, D], F32, 1),
            "ghi": ([C, D], BF16, 2), "glo": ([C, D], BF16, 2),
            "z1e": ([C, 16], BF16, 1), "z1b": ([16, C], BF16, 1),
            "lam": ([C, D], BF16, 2), "lami": ([C, D], BF16, 2),
            "lrev": ([C, D], BF16, 2),
            "qh": ([C, D], BF16, 2),
            "bnd": ([C, 8], F32, 2), "expbp": ([C, 8], F32, 4),
            "kt": ([C, D], BF16, 2), "kh": ([C, D], BF16, 3),
            "v": ([C, 2 * D], BF16, 3),
            "qhT": ([C, D], BF16, 3), "ktT": ([C, D], BF16, 2), "qgT": ([C, D], BF16, 2),
            "pt": ([C, 4 * C], BF16, 2), "o16": ([C, 2 * D], F16, 2),
            "dc": ([C, 8], F32, 5),
        }
        tiles = {}
        for nm, (shp, dt2, nb) in spec.items():
            tiles[nm] = [sb(f"t_{nm}_{b}", shp, dt2) for b in range(nb)]

        def T(nm, i):
            nb = spec[nm][2]
            return tiles[nm][i % nb], f"t_{nm}_{i % nb}"

        ptb1 = psm("ptb1", [128, 1024], BF16)
        ptb4 = psm("ptb4", [128, 1024], BF16)
        pc = psm("pc", [128, 1024], F32)
        pp = psm("pp", [128, 1024], F32)
        psc = psm("psc", [128, 512], F32)
        psc2 = psm("psc2", [128, 512], F32)
        PTB1, PTB4, PC, PP, PSC, PSC2 = ("ps_ptb1", "ps_ptb4", "ps_pc", "ps_pp",
                                         "ps_psc", "ps_psc2")

        p = Prog(nc)
        R = p.rec

        def ld(engq, dst, src, rd, wr):
            R(engq[0], lambda e, d=dst, s=src: e.dma_start(out=d, in_=s),
              reads=rd, writes=wr, queue=engq[1])

        gq = ("gpsimd", "qg")
        sq = ("sync", "qs")
        for kt in range(8):
            ld(gq, w_q[:, kt * D:(kt + 1) * D],
               ap["wq"][kt * 128:(kt + 1) * 128, :], ["d:wq"], ["w_q"])
            ld(gq, w_k[:, kt * D:(kt + 1) * D],
               ap["wk"][kt * 128:(kt + 1) * 128, :], ["d:wk"], ["w_k"])
            ld(gq, w_v[:, kt * 2 * D:(kt + 1) * 2 * D],
               ap["wv"][kt * 128:(kt + 1) * 128, :], ["d:wv"], ["w_v"])
        ld(gq, w_gk1.rearrange("p (kt n) -> p kt n", kt=8),
           ap["wgk1"].rearrange("(kt p) n -> p kt n", p=128), ["d:wgk1"], ["w_gk1"])
        ld(gq, w_gk2, ap["wgk2"], ["d:wgk2"], ["w_gk2"])
        ld(gq, s_bgk2, ap["bgk2"], ["d:bgk2"], ["s_bgk2"])
        ld(gq, c_trib, ap["ctrib"], ["d:ctrib"], ["c_trib"])
        ld(gq, c_trir, ap["ctrir"], ["d:ctrir"], ["c_trir"])
        ld(gq, c_identb, ap["cidentb"], ["d:cidentb"], ["c_identb"])
        ld(gq, c_ones, ap["conesb"], ["d:conesb"], ["c_ones"])
        ld(sq, c_mask4, ap["cmask"], ["d:cmask"], ["c_mask4"])
        ld(sq, c_identf, ap["cidentf"], ["d:cidentf"], ["c_identf"])
        ld(sq, c_ln16, ap["cln16"], ["d:cln16"], ["c_ln16"])
        R("vector", lambda e: e.memset(s_Sb, 0.0), writes=["s_Sb"])
        R("vector", lambda e: e.memset(s_bpreT, 0.0), writes=["s_bpreT"])

        def mm(out, lhsT, rhs, start, stop, rd, wr):
            R("tensor", lambda e, o=out, l=lhsT, r=rhs, s1=start, s2=stop:
              e.matmul(o, l, r, start=s1, stop=s2), reads=rd, writes=wr)

        def tr(out, in_, ident, rd, wr):
            R("tensor", lambda e, o=out, i=in_, d=ident: e.transpose(o, i, d),
              reads=rd, writes=wr)

        def S0(i):
            d, dn = T("x", i)
            R("sync", lambda e, dd=d, sl=ap["x"][i * C:(i + 1) * C, :]:
              e.dma_start(out=dd, in_=sl), reads=["d:x"], writes=[dn], queue="qs")

        def S1a(i):
            x, xn_ = T("x", i)
            st, stn = T("st", i)
            xn, xnn = T("xn", i)
            xnb, xnbn = T("xnb", i)
            xnT, xnTn = T("xnT", i)
            f1, f1n = T("f1", i)
            R("vector", lambda e: e.reduce_sum(st[:, 0:1], x, axis=AX.X),
              reads=[xn_], writes=[stn])
            R("scalar", lambda e: e.activation(out=f1, in_=x, func=AF.Square,
                                               accum_out=st[:, 1:2]),
              reads=[xn_], writes=[f1n, stn])
            R("vector", lambda e: e.tensor_scalar(out=st[:, 2:3], in0=st[:, 0:1],
                                                  scalar1=1.0 / D, scalar2=None, op0=ALU.mult),
              reads=[stn], writes=[stn])
            R("vector", lambda e: e.tensor_scalar(out=st[:, 3:4], in0=st[:, 1:2],
                                                  scalar1=1.0 / D, scalar2=None, op0=ALU.mult),
              reads=[stn], writes=[stn])
            R("vector", lambda e: e.tensor_mul(st[:, 4:5], st[:, 2:3], st[:, 2:3]),
              reads=[stn], writes=[stn])
            R("vector", lambda e: e.tensor_sub(st[:, 4:5], st[:, 3:4], st[:, 4:5]),
              reads=[stn], writes=[stn])
            R("vector", lambda e: e.tensor_scalar(out=st[:, 4:5], in0=st[:, 4:5],
                                                  scalar1=1.0, scalar2=EPS, op0=ALU.mult,
                                                  op1=ALU.add),
              reads=[stn], writes=[stn])
            R("scalar", lambda e: e.activation(out=st[:, 5:6], in_=st[:, 4:5], func=AF.Ln),
              reads=[stn], writes=[stn])
            R("scalar", lambda e: e.activation(out=st[:, 6:7], in_=st[:, 5:6], func=AF.Exp,
                                               scale=-0.5),
              reads=[stn], writes=[stn])
            R("vector", lambda e: e.tensor_scalar(out=xn, in0=x, scalar1=st[:, 2:3],
                                                  scalar2=st[:, 6:7], op0=ALU.subtract,
                                                  op1=ALU.mult),
              reads=[xn_, stn], writes=[xnn])
            R("scalar", lambda e: e.copy(xnb, xn), reads=[xnn], writes=[xnbn])
            R("vector", lambda e: e.tensor_add(x, x, xn), reads=[xn_, xnn], writes=[xn_])
            R("sync", lambda e, cc=i: e.dma_start(out=ap["base_d"][cc], in_=x),
              reads=[xn_], writes=["d:base"], queue="qs")
            for j in range(8):
                tr(ptb1[:, j * C:(j + 1) * C], xnb[:, j * C:(j + 1) * C], c_identb,
                   [xnbn, "c_identb"], [PTB1])
            R("scalar", lambda e: e.copy(xnT, ptb1), reads=[PTB1], writes=[xnTn])
            R("sync", lambda e, cc=i: e.dma_start(out=ap["xnT_d"][cc], in_=xnT),
              reads=[xnTn], writes=["d:xnT"], queue="qs")

        def S1b(i):
            xnT, xnTn = T("xnT", i)
            xn, xnn = T("xn", i)
            f1, f1n = T("f1", i)
            ghi, ghin = T("ghi", i)
            glo, glon = T("glo", i)
            z1e, z1en = T("z1e", i)
            z1b, z1bn = T("z1b", i)
            for j in range(8):
                mm(pc[:, 0:16], xnT[:, j * C:(j + 1) * C],
                   w_gk1[:, j * 16:(j + 1) * 16], j == 0, j == 7, [xnTn, "w_gk1"], [PC])
            R("vector", lambda e: e.tensor_copy(z1e, pc[:, 0:16]), reads=[PC], writes=[z1en])
            tr(ptb1[0:16, 0:C], z1e, c_identb, [z1en, "c_identb"], [PTB1])
            R("scalar", lambda e: e.copy(z1b, ptb1[0:16, 0:C]), reads=[PTB1], writes=[z1bn])
            for nh in range(2):
                sl = slice(nh * 512, (nh + 1) * 512)
                mm(pc[:, sl], z1b, w_gk2[:, sl], True, False, [z1bn, "w_gk2"], [PC])
                mm(pc[:, sl], c_ones[0:1, :], s_bgk2[:, sl], False, True,
                   ["c_ones", "s_bgk2"], [PC])
            R("scalar", lambda e: e.activation(out=f1, in_=pc, func=AF.Abs),
              reads=[PC], writes=[f1n])
            R("scalar", lambda e: e.activation(out=f1, in_=f1, func=AF.Exp, scale=-1.0),
              reads=[f1n], writes=[f1n])
            R("scalar", lambda e: e.activation(out=f1, in_=f1, func=AF.Ln, bias=1.0),
              reads=[f1n], writes=[f1n])
            R("vector", lambda e: e.tensor_scalar(out=xn, in0=pc, scalar1=0.0,
                                                  scalar2=None, op0=ALU.min),
              reads=[PC], writes=[xnn])
            R("vector", lambda e: e.tensor_sub(xn, xn, f1), reads=[xnn, f1n], writes=[xnn])
            R("scalar", lambda e: e.copy(ghi, xn), reads=[xnn], writes=[ghin])
            R("vector", lambda e: e.tensor_sub(glo, xn, ghi), reads=[xnn, ghin], writes=[glon])

        def S2a(i):
            ghi, ghin = T("ghi", i)
            glo, glon = T("glo", i)
            lam, lamn = T("lam", i)
            lami, lamin = T("lami", i)
            for nh in range(2):
                sl = slice(nh * 512, (nh + 1) * 512)
                mm(pc[:, sl], c_trib, ghi[:, sl], True, False, ["c_trib", ghin], [PC])
                mm(pc[:, sl], c_trib, glo[:, sl], False, True, ["c_trib", glon], [PC])
            R("scalar", lambda e: e.activation(out=lam, in_=pc, func=AF.Exp, bias=c_ln16),
              reads=[PC, "c_ln16"], writes=[lamn])
            R("scalar", lambda e: e.activation(out=lami, in_=pc, func=AF.Exp, scale=-1.0),
              reads=[PC], writes=[lamin])

        def S2b(i):
            ghi, ghin = T("ghi", i)
            glo, glon = T("glo", i)
            lrev, lrevn = T("lrev", i)
            for nh in range(2):
                sl = slice(nh * 512, (nh + 1) * 512)
                mm(pc[:, sl], c_trir, ghi[:, sl], True, False, ["c_trir", ghin], [PC])
                mm(pc[:, sl], c_trir, glo[:, sl], False, True, ["c_trir", glon], [PC])
            R("scalar", lambda e: e.activation(out=lrev, in_=pc, func=AF.Exp),
              reads=[PC], writes=[lrevn])

        def S2c(i):
            ghi, ghin = T("ghi", i)
            glo, glon = T("glo", i)
            dc, dcn = T("dc", i)
            bnd, bndn = T("bnd", i)
            expbp, expbpn = T("expbp", i)
            for j in range(8):
                mm(pc[:, j:j + 1], ghi[:, j * C:(j + 1) * C], c_ones[:, 0:1],
                   j == 0, False, [ghin, "c_ones"], [PC])
                mm(pc[:, j:j + 1], glo[:, j * C:(j + 1) * C], c_ones[:, 0:1],
                   False, j == 7, [glon, "c_ones"], [PC])
            R("scalar", lambda e: e.activation(out=dc, in_=pc[:, 0:8], func=AF.Exp),
              reads=[PC], writes=[dcn])
            R("vector", lambda e: e.tensor_copy(bnd, pc[:, 0:8]), reads=[PC], writes=[bndn])
            R("scalar", lambda e: e.activation(out=expbp, in_=s_bpreT, func=AF.Exp),
              reads=["s_bpreT"], writes=[expbpn])
            R("vector", lambda e: e.tensor_add(s_bpreT, s_bpreT, bnd),
              reads=["s_bpreT", bndn], writes=["s_bpreT"])

        def S3q(i):
            xnT, xnTn = T("xnT", i)
            lam, lamn = T("lam", i)
            qh, qhn = T("qh", i)
            for j in range(8):
                for nh in range(2):
                    sl = slice(nh * 512, (nh + 1) * 512)
                    mm(pp[:, sl], xnT[:, j * C:(j + 1) * C],
                       w_q[:, j * D + nh * 512: j * D + (nh + 1) * 512],
                       j == 0, j == 7, [xnTn, "w_q"], [PP])
            R("vector", lambda e: e.tensor_mul(qh, pp, lam), reads=[PP, lamn], writes=[qhn])

        def S3k(i):
            xnT, xnTn = T("xnT", i)
            lami, lamin = T("lami", i)
            lrev, lrevn = T("lrev", i)
            kt, ktn = T("kt", i)
            kh, khn = T("kh", i)
            for j in range(8):
                for nh in range(2):
                    sl = slice(nh * 512, (nh + 1) * 512)
                    mm(pp[:, sl], xnT[:, j * C:(j + 1) * C],
                       w_k[:, j * D + nh * 512: j * D + (nh + 1) * 512],
                       j == 0, j == 7, [xnTn, "w_k"], [PP])
            R("vector", lambda e: e.tensor_mul(kt, pp, lami), reads=[PP, lamin], writes=[ktn])
            R("vector", lambda e: e.tensor_mul(kh, pp, lrev), reads=[PP, lrevn], writes=[khn])

        def S3v(i, half):
            xnT, xnTn = T("xnT", i)
            v, vn = T("v", i)
            for j in range(8):
                for nh in range(2):
                    mm(pp[:, nh * 512:(nh + 1) * 512], xnT[:, j * C:(j + 1) * C],
                       w_v[:, j * 2 * D + half * D + nh * 512:
                           j * 2 * D + half * D + (nh + 1) * 512],
                       j == 0, j == 7, [xnTn, "w_v"], [PP])
            R("scalar", lambda e, h=half: e.copy(v[:, h * D:(h + 1) * D], pp),
              reads=[PP], writes=[vn])

        def S4(i):
            for src, dst in (("qh", "qhT"), ("kt", "ktT")):
                sT, sn = T(src, i)
                dT, dn = T(dst, i)
                for j in range(8):
                    tr(ptb4[:, j * C:(j + 1) * C], sT[:, j * C:(j + 1) * C], c_identb,
                       [sn, "c_identb"], [PTB4])
                R("scalar", lambda e, d=dT, ps=ptb4: e.copy(d, ps), reads=[PTB4], writes=[dn])
            qhT, qhTn = T("qhT", i)
            qgT, qgTn = T("qgT", i)
            expbp, expbpn = T("expbp", i)
            for j in range(8):
                R("vector", lambda e, jj=j: e.tensor_scalar(
                    out=qgT[:, jj * C:(jj + 1) * C], in0=qhT[:, jj * C:(jj + 1) * C],
                    scalar1=expbp[:, jj:jj + 1], scalar2=None, op0=ALU.mult),
                  reads=[qhTn, expbpn], writes=[qgTn])
            R("sync", lambda e, cc=i: e.dma_start(out=ap["qgT_d"][cc], in_=qgT),
              reads=[qgTn], writes=["d:qgT"], queue="qs")

        def S5a(i):
            qhT, qhTn = T("qhT", i)
            ktT, ktTn = T("ktT", i)
            pt, ptn = T("pt", i)
            for h in range(4):
                for ji, j in enumerate((2 * h, 2 * h + 1)):
                    mm(psc2[:, h * C:(h + 1) * C], ktT[:, j * C:(j + 1) * C],
                       qhT[:, j * C:(j + 1) * C], (h == 0 and ji == 0),
                       (h == 3 and ji == 1), [ktTn, qhTn], [PSC2])
            R("vector", lambda e: e.tensor_mul(pt, psc2, c_mask4),
              reads=[PSC2, "c_mask4"], writes=[ptn])

        def S5b(i, h):
            qhT, qhTn = T("qhT", i)
            kh, khn = T("kh", i)
            v, vn = T("v", i)
            pt, ptn = T("pt", i)
            o16, o16n = T("o16", i)
            dc, dcn = T("dc", i)
            mm(psc[:, 0:DV], pt[:, h * C:(h + 1) * C], v[:, h * DV:(h + 1) * DV],
               True, False, [ptn, vn], [PSC])
            for ji, j in enumerate((2 * h, 2 * h + 1)):
                mm(psc[:, 0:DV], qhT[:, j * C:(j + 1) * C], s_Sb[:, j * DV:(j + 1) * DV],
                   False, ji == 1, [qhTn, "s_Sb"], [PSC])
            R("scalar", lambda e: e.copy(o16[:, h * DV:(h + 1) * DV], psc[:, 0:DV]),
              reads=[PSC], writes=[o16n])
            for ih in range(2):
                sg = 2 * h + ih
                mm(psc[:, 0:DV], kh[:, h * DK + ih * C: h * DK + (ih + 1) * C],
                   v[:, h * DV:(h + 1) * DV], True, True, [khn, vn], [PSC])
                R("vector", lambda e, g=sg: e.scalar_tensor_tensor(
                    out=s_Sb[:, g * DV:(g + 1) * DV], in0=s_Sb[:, g * DV:(g + 1) * DV],
                    scalar=dc[:, g:g + 1], in1=psc[:, 0:DV],
                    op0=ALU.mult, op1=ALU.add),
                  reads=["s_Sb", dcn, PSC], writes=["s_Sb"])
            if h == 3:
                R("sync", lambda e, cc=i: e.dma_start(out=ap["o16_d"][cc], in_=o16),
                  reads=[o16n], writes=["d:o16"], queue="qs")

        S0(0)
        for i in range(NCH + 6):
            if 0 <= i + 1 < NCH:
                S0(i + 1)
            if 0 <= i - 2 < NCH:
                S3q(i - 2)
            if 0 <= i - 5 < NCH:
                S5b(i - 5, 0)
            if 0 <= i - 3 < NCH:
                S3k(i - 3)
            if 0 <= i - 5 < NCH:
                S5b(i - 5, 1)
            if 0 <= i - 3 < NCH:
                S3v(i - 3, 0)
            if 0 <= i - 5 < NCH:
                S5b(i - 5, 2)
            if 0 <= i - 3 < NCH:
                S3v(i - 3, 1)
            if 0 <= i - 5 < NCH:
                S5b(i - 5, 3)
            if 0 <= i - 4 < NCH:
                S5a(i - 4)
            if 0 <= i - 1 < NCH:
                S2a(i - 1)
            if 0 <= i - 3 < NCH:
                S4(i - 3)
            if 0 <= i - 1 < NCH:
                S2b(i - 1)
            if 0 <= i - 1 < NCH:
                S2c(i - 1)
            if 0 <= i < NCH:
                S1a(i)
            if 0 <= i < NCH:
                S1b(i)

        R("sync", lambda e: e.dma_start(out=ap["T_d"], in_=s_Sb),
          reads=["s_Sb"], writes=["d:T"], queue="qs")
        for j in range(8):
            tr(pc[0:1, j * C:(j + 1) * C], s_bpreT[:, j:j + 1], c_identf,
               ["s_bpreT", "c_identf"], [PC])
        R("vector", lambda e: e.tensor_copy(s_btrow, pc[0:1, :]),
          reads=[PC], writes=["s_btrow"])
        R("sync", lambda e: e.dma_start(out=ap["btot_d"], in_=s_btrow),
          reads=["s_btrow"], writes=["d:btot"], queue="qs")
        p.barrier()

        block = es.enter_context(nc.Block())
        p.finalize(es, block)
    return nc


def build_phase_b():
    nc = bass.Bass("TRN2", target_bir_lowering=False, debug=False)
    ap = {}
    for n, shp, dt in [("o16_d", [NCH, C, 2 * D], F16), ("qgT_d", [NCH, C, D], BF16),
                       ("base_d", [NCH, C, D], F32), ("xnT_d", [NCH, C, D], BF16),
                       ("T_all", [NCORES * C, 8 * DV], BF16), ("btot_all", [8, D], F32),
                       ("x0", [1, D], F32),
                       ("wg", [D, 2 * D], F32), ("wo", [2 * D, D], F32),
                       ("alay", [D, 512], F32), ("bwT", [512, D], F32),
                       ("wrT", [D, 8], F32), ("gnw", [128, 4], F32),
                       ("cmatx", [8, 8], BF16), ("wmrow", [1, 8], BF16),
                       ("conesb", [C, C], BF16),
                       ("cidentb", [C, C], BF16), ("cidentf", [C, C], F32),
                       ("cesel", [8, 512], BF16)]:
        ap[n] = nc.dram_tensor(n, shp, dt, kind="ExternalInput").ap()
    ap["y"] = nc.dram_tensor("y", [TOK, D], F32, kind="ExternalOutput").ap()

    es = ExitStack()
    with es:
        def sb(n, s, t):
            return es.enter_context(nc.sbuf_tensor(n, s, t))[:]

        def psm(n, s, t):
            return es.enter_context(nc.psum_tensor(n, s, t))[:]

        w_g = sb("w_g", [128, 8 * 2 * D], BF16)
        w_wo = sb("w_wo", [128, 16 * D], BF16)
        w_alay = sb("w_alay", [128, 8 * 512], BF16)
        w_bwT = sb("w_bwT", [128, 4 * D], BF16)
        w_wrT = sb("w_wrT", [128, 8 * 8], F32)
        s_gnw = sb("s_gnw", [128, 4], F32)
        c_identb = sb("c_identb", [C, C], BF16)
        c_identf = sb("c_identf", [C, C], F32)
        c_esel = sb("c_esel", [8, 512], BF16)
        c_cmat = sb("c_cmat", [8, 8], BF16)
        c_ones = sb("c_ones", [C, C], BF16)
        s_wmrow = sb("s_wmrow", [1, 8], BF16)
        s_x0 = sb("s_x0", [1, D], F32)
        s_x0b = sb("s_x0b", [1, D], F32)
        g_st = sb("g_st", [1, 8], F32)
        g_junk = sb("g_junk", [1, D], F32)
        s_x0T = sb("s_x0T", [128, 8], F32)
        g_lg8 = sb("g_lg8", [8, 1], F32)
        g_lg = sb("g_lg", [1, 8], F32)
        g_d = sb("g_d", [1, 8], F32)
        g_ind = sb("g_ind", [1, 8], F32)
        g_l2 = sb("g_l2", [1, 8], F32)
        g_keep = sb("g_keep", [1, 8], F32)
        g_e = sb("g_e", [1, 8], F32)
        g_gt = sb("g_gt", [1, 8], F32)
        g_gc8 = sb("g_gc8", [8, 1], F32)
        g_gcb = sb("g_gcb", [8, 4], BF16)
        s_gc = sb("s_gc", [128, 4], F32)
        b_bt = sb("b_bt", [8, D], F32)
        b_bthi = sb("b_bthi", [8, D], BF16)
        b_btlo = sb("b_btlo", [8, D], BF16)
        b_w = sb("b_w", [128, 64], F32)
        b_Tr = [sb("b_Tr_0", [128, 8 * DV], BF16)]
        s_Sin = sb("s_Sin", [128, 8 * DV], F32)
        s_Sinb = sb("s_Sinb", [128, 8 * DV], BF16)

        spec = {
            "xnT": ([C, D], BF16, 3), "o16": ([C, 2 * D], F16, 3),
            "qgT": ([C, D], BF16, 3), "base": ([C, D], F32, 4),
            "ob": ([C, 2 * D], BF16, 2), "rs": ([C, 4], F32, 2),
            "sd": ([C, 4], F32, 2), "rst": ([C, 4], F32, 2),
            "junk": ([C, DV], F32, 1), "sg": ([C, 2 * D], BF16, 2),
            "ogT": ([C, 2 * D], BF16, 2),
            "mid": ([C, 512], BF16, 2), "midT": ([C, 512], BF16, 2),
        }
        tiles = {}
        for nm, (shp, dt2, nb) in spec.items():
            tiles[nm] = [sb(f"b_{nm}_{b}", shp, dt2) for b in range(nb)]

        def T(nm, i):
            nb = spec[nm][2]
            return tiles[nm][i % nb], f"b_{nm}_{i % nb}"

        pgA = psm("pgA", [128, 1024], F32)
        pd = psm("pd", [128, 1024], F32)
        pm = psm("pm", [128, 512], F32)
        pwo = psm("pwo", [128, 1024], F32)
        ptbB = psm("ptbB", [128, 1024], BF16)
        PGA, PD, PM, PWO, PTBB = "ps_pgA", "ps_pd", "ps_pm", "ps_pwo", "ps_ptbB"

        p = Prog(nc)
        R = p.rec

        def mm(out, lhsT, rhs, start, stop, rd, wr):
            R("tensor", lambda e, o=out, l=lhsT, r=rhs, s1=start, s2=stop:
              e.matmul(o, l, r, start=s1, stop=s2), reads=rd, writes=wr)

        def tr(out, in_, ident, rd, wr):
            R("tensor", lambda e, o=out, i=in_, d=ident: e.transpose(o, i, d),
              reads=rd, writes=wr)

        def ldg(dst, src, rd, wr):
            R("gpsimd", lambda e, d=dst, s=src: e.dma_start(out=d, in_=s),
              reads=rd, writes=wr, queue="qg")

        for kt in range(8):
            ldg(w_g[:, kt * 2 * D:(kt + 1) * 2 * D],
                ap["wg"][kt * 128:(kt + 1) * 128, :], ["d:wg"], ["w_g"])
        for kt in range(16):
            ldg(w_wo[:, kt * D:(kt + 1) * D],
                ap["wo"][kt * 128:(kt + 1) * 128, :], ["d:wo"], ["w_wo"])
        ldg(w_alay.rearrange("p (kt n) -> p kt n", kt=8),
            ap["alay"].rearrange("(kt p) n -> p kt n", p=128), ["d:alay"], ["w_alay"])
        ldg(w_bwT.rearrange("p (kt n) -> p kt n", kt=4),
            ap["bwT"].rearrange("(kt p) n -> p kt n", p=128), ["d:bwT"], ["w_bwT"])
        ldg(c_identb, ap["cidentb"], ["d:cidentb"], ["c_identb"])
        ldg(c_esel, ap["cesel"], ["d:cesel"], ["c_esel"])
        ldg(c_cmat, ap["cmatx"], ["d:cmatx"], ["c_cmat"])
        ldg(c_ones, ap["conesb"], ["d:conesb"], ["c_ones"])
        ldg(s_wmrow, ap["wmrow"], ["d:wmrow"], ["s_wmrow"])
        R("sync", lambda e: e.dma_start(
            out=w_wrT.rearrange("p (kt n) -> p kt n", kt=8),
            in_=ap["wrT"].rearrange("(kt p) n -> p kt n", p=128)),
          reads=["d:wrT"], writes=["w_wrT"], queue="qs")
        R("sync", lambda e: e.dma_start(out=s_gnw, in_=ap["gnw"]),
          reads=["d:gnw"], writes=["s_gnw"], queue="qs")
        R("sync", lambda e: e.dma_start(out=c_identf, in_=ap["cidentf"]),
          reads=["d:cidentf"], writes=["c_identf"], queue="qs")
        R("sync", lambda e: e.dma_start(out=s_x0, in_=ap["x0"]),
          reads=["d:x0"], writes=["s_x0"], queue="qs")
        R("sync", lambda e: e.dma_start(out=b_bt, in_=ap["btot_all"]),
          reads=["d:btot"], writes=["b_bt"], queue="qs")

        R("vector", lambda e: e.reduce_sum(g_st[:, 0:1], s_x0, axis=AX.X),
          reads=["s_x0"], writes=["g_st"])
        R("scalar", lambda e: e.activation(out=g_junk, in_=s_x0, func=AF.Square,
                                           accum_out=g_st[:, 1:2]),
          reads=["s_x0"], writes=["g_junk", "g_st"])
        R("vector", lambda e: e.tensor_scalar(out=g_st[:, 2:3], in0=g_st[:, 0:1],
                                              scalar1=1.0 / D, scalar2=None, op0=ALU.mult),
          reads=["g_st"], writes=["g_st"])
        R("vector", lambda e: e.tensor_scalar(out=g_st[:, 3:4], in0=g_st[:, 1:2],
                                              scalar1=1.0 / D, scalar2=None, op0=ALU.mult),
          reads=["g_st"], writes=["g_st"])
        R("vector", lambda e: e.tensor_mul(g_st[:, 4:5], g_st[:, 2:3], g_st[:, 2:3]),
          reads=["g_st"], writes=["g_st"])
        R("vector", lambda e: e.tensor_sub(g_st[:, 4:5], g_st[:, 3:4], g_st[:, 4:5]),
          reads=["g_st"], writes=["g_st"])
        R("vector", lambda e: e.tensor_scalar(out=g_st[:, 4:5], in0=g_st[:, 4:5],
                                              scalar1=1.0, scalar2=EPS, op0=ALU.mult,
                                              op1=ALU.add),
          reads=["g_st"], writes=["g_st"])
        R("scalar", lambda e: e.activation(out=g_st[:, 5:6], in_=g_st[:, 4:5], func=AF.Ln),
          reads=["g_st"], writes=["g_st"])
        R("scalar", lambda e: e.activation(out=g_st[:, 6:7], in_=g_st[:, 5:6], func=AF.Exp,
                                           scale=-0.5),
          reads=["g_st"], writes=["g_st"])
        R("vector", lambda e: e.tensor_scalar(out=s_x0b, in0=s_x0, scalar1=g_st[:, 2:3],
                                              scalar2=g_st[:, 6:7], op0=ALU.subtract,
                                              op1=ALU.mult),
          reads=["s_x0", "g_st"], writes=["s_x0b"])
        for j in range(8):
            tr(pm[:, j:j + 1], s_x0b[:, j * C:(j + 1) * C], c_identf[0:1, 0:1],
               ["s_x0b", "c_identf"], [PM])
        R("scalar", lambda e: e.copy(s_x0T, pm[:, 0:8]), reads=[PM], writes=["s_x0T"])
        for j in range(8):
            mm(pd[0:8, 0:1], w_wrT[:, j * 8:(j + 1) * 8], s_x0T[:, j:j + 1],
               j == 0, j == 7, ["w_wrT", "s_x0T"], [PD])
        R("scalar", lambda e: e.copy(g_lg8, pd[0:8, 0:1]), reads=[PD], writes=["g_lg8"])
        tr(pm[0:1, 0:8], g_lg8, c_identf[0:8, 0:8], ["g_lg8", "c_identf"], [PM])
        R("scalar", lambda e: e.copy(g_lg, pm[0:1, 0:8]), reads=[PM], writes=["g_lg"])
        R("vector", lambda e: e.reduce_max(g_st[:, 7:8], g_lg, axis=AX.X),
          reads=["g_lg"], writes=["g_st"])
        R("vector", lambda e: e.tensor_scalar(out=g_d, in0=g_lg, scalar1=g_st[:, 7:8],
                                              scalar2=None, op0=ALU.subtract),
          reads=["g_lg", "g_st"], writes=["g_d"])
        R("vector", lambda e: e.tensor_scalar(out=g_ind, in0=g_d, scalar1=0.0,
                                              scalar2=None, op0=ALU.is_ge),
          reads=["g_d"], writes=["g_ind"])
        R("vector", lambda e: e.scalar_tensor_tensor(out=g_l2, in0=g_ind, scalar=-1e30,
                                                     in1=g_lg, op0=ALU.mult, op1=ALU.add),
          reads=["g_ind", "g_lg"], writes=["g_l2"])
        R("vector", lambda e: e.reduce_max(g_st[:, 0:1], g_l2, axis=AX.X),
          reads=["g_l2"], writes=["g_st"])
        R("vector", lambda e: e.tensor_scalar(out=g_keep, in0=g_lg, scalar1=g_st[:, 0:1],
                                              scalar2=None, op0=ALU.is_ge),
          reads=["g_lg", "g_st"], writes=["g_keep"])
        R("scalar", lambda e: e.activation(out=g_e, in_=g_d, func=AF.Exp),
          reads=["g_d"], writes=["g_e"])
        R("vector", lambda e: e.tensor_mul(g_e, g_e, g_keep),
          reads=["g_e", "g_keep"], writes=["g_e"])
        R("vector", lambda e: e.reduce_sum(g_st[:, 1:2], g_e, axis=AX.X),
          reads=["g_e"], writes=["g_st"])
        R("vector", lambda e: e.reciprocal(g_st[:, 2:3], g_st[:, 1:2]),
          reads=["g_st"], writes=["g_st"])
        R("vector", lambda e: e.tensor_scalar(out=g_gt, in0=g_e, scalar1=g_st[:, 2:3],
                                              scalar2=None, op0=ALU.mult),
          reads=["g_e", "g_st"], writes=["g_gt"])
        tr(pm[0:8, 8:9], g_gt, c_identf[0:1, 0:1], ["g_gt", "c_identf"], [PM])
        R("scalar", lambda e: e.copy(g_gc8, pm[0:8, 8:9]), reads=[PM], writes=["g_gc8"])
        R("vector", lambda e: e.tensor_copy(g_gcb[:, 0:1], g_gc8),
          reads=["g_gc8"], writes=["g_gcb"])
        R("vector", lambda e: e.tensor_sub(g_gcb[:, 2:3], g_gc8, g_gcb[:, 0:1]),
          reads=["g_gc8", "g_gcb"], writes=["g_gcb"])
        for tau in range(4):
            for hl in range(2):
                mm(pm[:, tau:tau + 1], c_esel[:, tau * 128:(tau + 1) * 128],
                   g_gcb[:, 2 * hl:2 * hl + 1], (tau == 0 and hl == 0),
                   (tau == 3 and hl == 1), ["c_esel", "g_gcb"], [PM])
        R("scalar", lambda e: e.copy(s_gc, pm[:, 0:4]), reads=[PM], writes=["s_gc"])
        for tau in range(4):
            R("vector", lambda e, u=tau: e.tensor_scalar(
                out=w_bwT[:, u * D:(u + 1) * D], in0=w_bwT[:, u * D:(u + 1) * D],
                scalar1=s_gc[:, u:u + 1], scalar2=None, op0=ALU.mult),
              reads=["w_bwT", "s_gc"], writes=["w_bwT"])
        for k in range(16):
            R("vector", lambda e, kk=k: e.tensor_scalar(
                out=w_wo[:, kk * D:(kk + 1) * D], in0=w_wo[:, kk * D:(kk + 1) * D],
                scalar1=s_gnw[:, kk % 4:kk % 4 + 1], scalar2=None, op0=ALU.mult),
              reads=["w_wo", "s_gnw"], writes=["w_wo"])

        R("vector", lambda e: e.tensor_copy(b_bthi, b_bt), reads=["b_bt"], writes=["b_bthi"])
        R("vector", lambda e: e.tensor_sub(b_btlo, b_bt, b_bthi),
          reads=["b_bt", "b_bthi"], writes=["b_btlo"])
        for tau in range(8):
            mm(pm[:, tau * 8:(tau + 1) * 8], b_bthi[:, tau * 128:(tau + 1) * 128],
               c_cmat, tau == 0, False, ["b_bthi", "c_cmat"], [PM])
            mm(pm[:, tau * 8:(tau + 1) * 8], b_btlo[:, tau * 128:(tau + 1) * 128],
               c_cmat, False, False, ["b_btlo", "c_cmat"], [PM])
            mm(pm[:, tau * 8:(tau + 1) * 8], c_ones[0:1, :], s_wmrow,
               False, tau == 7, ["c_ones", "s_wmrow"], [PM])
        R("scalar", lambda e: e.activation(out=b_w, in_=pm[:, 0:64], func=AF.Exp),
          reads=[PM], writes=["b_w"])
        R("vector", lambda e: e.memset(s_Sin, 0.0), writes=["s_Sin"])
        for r in range(NCORES):
            tb = b_Tr[0]
            tbn = "b_Tr_0"
            R("sync", lambda e, rr=r, d=tb: e.dma_start(
                out=d, in_=ap["T_all"][rr * C:(rr + 1) * C, :]),
              reads=["d:Tall"], writes=[tbn], queue="qs")
            for sg_ in range(8):
                R("vector", lambda e, g=sg_, rr=r, d=tb: e.scalar_tensor_tensor(
                    out=s_Sin[:, g * DV:(g + 1) * DV], in0=d[:, g * DV:(g + 1) * DV],
                    scalar=b_w[:, g * 8 + rr:g * 8 + rr + 1],
                    in1=s_Sin[:, g * DV:(g + 1) * DV],
                    op0=ALU.mult, op1=ALU.add),
                  reads=[tbn, "b_w", "s_Sin"], writes=["s_Sin"])
        R("scalar", lambda e: e.copy(s_Sinb, s_Sin), reads=["s_Sin"], writes=["s_Sinb"])

        def B0(i):
            for nm, key in [("xnT", "xnT_d"), ("o16", "o16_d"),
                            ("qgT", "qgT_d"), ("base", "base_d")]:
                d, dn = T(nm, i)
                R("sync", lambda e, cc=i, dd=d, k=key: e.dma_start(out=dd, in_=ap[k][cc]),
                  reads=["d:" + key], writes=[dn], queue="qs")

        def B1a(i):
            xnT, xnTn = T("xnT", i)
            qgT, qgTn = T("qgT", i)
            sg, sgn = T("sg", i)
            for j in range(8):
                for nh in range(2):
                    mm(pgA[:, nh * 512:(nh + 1) * 512], xnT[:, j * C:(j + 1) * C],
                       w_g[:, j * 2 * D + nh * 512: j * 2 * D + (nh + 1) * 512],
                       j == 0, j == 7, [xnTn, "w_g"], [PGA])
            for hi in range(2):
                for ji, j in enumerate((2 * hi, 2 * hi + 1)):
                    mm(pd[:, hi * DV:(hi + 1) * DV], qgT[:, j * C:(j + 1) * C],
                       s_Sinb[:, j * DV:(j + 1) * DV], ji == 0, ji == 1,
                       [qgTn, "s_Sinb"], [PD])
            R("scalar", lambda e: e.activation(out=sg[:, 0:D], in_=pgA, func=AF.Silu),
              reads=[PGA], writes=[sgn])

        def B1b(i):
            xnT, xnTn = T("xnT", i)
            o16, o16n = T("o16", i)
            qgT, qgTn = T("qgT", i)
            ob, obn = T("ob", i)
            sg, sgn = T("sg", i)
            mid, midn = T("mid", i)
            for j in range(8):
                for nh in range(2):
                    mm(pgA[:, nh * 512:(nh + 1) * 512], xnT[:, j * C:(j + 1) * C],
                       w_g[:, j * 2 * D + D + nh * 512: j * 2 * D + D + (nh + 1) * 512],
                       j == 0, j == 7, [xnTn, "w_g"], [PGA])
            for hi in range(2):
                R("vector", lambda e, h=hi: e.tensor_tensor(
                    ob[:, h * DV:(h + 1) * DV], o16[:, h * DV:(h + 1) * DV],
                    pd[:, h * DV:(h + 1) * DV], ALU.add),
                  reads=[o16n, PD], writes=[obn])
            for hi in range(2):
                h = 2 + hi
                for ji, j in enumerate((2 * h, 2 * h + 1)):
                    mm(pd[:, hi * DV:(hi + 1) * DV], qgT[:, j * C:(j + 1) * C],
                       s_Sinb[:, j * DV:(j + 1) * DV], ji == 0, ji == 1,
                       [qgTn, "s_Sinb"], [PD])
            R("scalar", lambda e: e.activation(out=sg[:, D:2 * D], in_=pgA, func=AF.Silu),
              reads=[PGA], writes=[sgn])
            for j in range(8):
                mm(pm[:, 0:512], xnT[:, j * C:(j + 1) * C],
                   w_alay[:, j * 512:(j + 1) * 512], j == 0, j == 7,
                   [xnTn, "w_alay"], [PM])
            for hi in range(2):
                h = 2 + hi
                R("vector", lambda e, h2=h, i2=hi: e.tensor_tensor(
                    ob[:, h2 * DV:(h2 + 1) * DV], o16[:, h2 * DV:(h2 + 1) * DV],
                    pd[:, i2 * DV:(i2 + 1) * DV], ALU.add),
                  reads=[o16n, PD], writes=[obn])
            R("vector", lambda e: e.tensor_copy(mid, pm[:, 0:512]), reads=[PM], writes=[midn])

        def B1c(i):
            ob, obn = T("ob", i)
            rs, rsn = T("rs", i)
            sd, sdn = T("sd", i)
            rst, rstn = T("rst", i)
            junk, junkn = T("junk", i)
            sg, sgn = T("sg", i)
            t1, t1n = ob, obn
            for h in range(4):
                R("scalar", lambda e, hh=h: e.activation(
                    out=junk, in_=ob[:, hh * DV:(hh + 1) * DV], func=AF.Square,
                    accum_out=rs[:, hh:hh + 1]),
                  reads=[obn], writes=[junkn, rsn])
            R("vector", lambda e: e.tensor_scalar(out=rs, in0=rs, scalar1=1.0 / DV,
                                                  scalar2=EPS, op0=ALU.mult, op1=ALU.add),
              reads=[rsn], writes=[rsn])
            R("vector", lambda e: e.tensor_scalar(
                out=rst[:].bitcast(I32), in0=rs[:].bitcast(I32), scalar1=1,
                scalar2=None, op0=ALU.logical_shift_right),
              reads=[rsn], writes=[rstn])
            R("vector", lambda e: e.tensor_scalar(
                out=rst[:].bitcast(I32), in0=rst[:].bitcast(I32), scalar1=-1,
                scalar2=0x5f3759df, op0=ALU.mult, op1=ALU.add),
              reads=[rstn], writes=[rstn])
            for _ in range(3):
                R("vector", lambda e: e.tensor_mul(sd, rst, rst), reads=[rstn], writes=[sdn])
                R("vector", lambda e: e.tensor_mul(sd, sd, rs), reads=[sdn, rsn], writes=[sdn])
                R("vector", lambda e: e.tensor_scalar(out=sd, in0=sd, scalar1=-0.5,
                                                      scalar2=1.5, op0=ALU.mult, op1=ALU.add),
                  reads=[sdn], writes=[sdn])
                R("vector", lambda e: e.tensor_mul(rst, rst, sd), reads=[rstn, sdn], writes=[rstn])
            R("vector", lambda e: e.tensor_mul(t1, ob, sg),
              reads=[obn, sgn], writes=[t1n])
            for h in range(4):
                R("vector", lambda e, hh=h: e.tensor_scalar(
                    out=t1[:, hh * DV:(hh + 1) * DV], in0=t1[:, hh * DV:(hh + 1) * DV],
                    scalar1=rst[:, hh:hh + 1], scalar2=None, op0=ALU.mult),
                  reads=[t1n, rstn], writes=[t1n])

        def B2mid(i):
            mid, midn = T("mid", i)
            midT, midTn = T("midT", i)
            for j in range(4):
                tr(ptbB[:, j * C:(j + 1) * C], mid[:, j * C:(j + 1) * C], c_identb,
                   [midn, "c_identb"], [PTBB])
            R("scalar", lambda e: e.copy(midT, ptbB[:, 0:512]), reads=[PTBB], writes=[midTn])

        def B2og(i, half):
            t1, t1n = T("ob", i)
            ogT, ogTn = T("ogT", i)
            for j in range(8):
                tr(ptbB[:, j * C:(j + 1) * C],
                   t1[:, half * D + j * C: half * D + (j + 1) * C],
                   c_identb, [t1n, "c_identb"], [PTBB])
            R("scalar", lambda e, h=half: e.copy(ogT[:, h * D:(h + 1) * D], ptbB),
              reads=[PTBB], writes=[ogTn])

        def B2wo(i):
            ogT, ogTn = T("ogT", i)
            midT, midTn = T("midT", i)
            base, basen = T("base", i)
            for nh in range(2):
                sl = slice(nh * 512, (nh + 1) * 512)
                for k in range(16):
                    mm(pwo[:, sl], ogT[:, k * C:(k + 1) * C],
                       w_wo[:, k * D + nh * 512: k * D + (nh + 1) * 512],
                       k == 0, False, [ogTn, "w_wo"], [PWO])
                for j in range(4):
                    mm(pwo[:, sl], midT[:, j * C:(j + 1) * C],
                       w_bwT[:, j * D + nh * 512: j * D + (nh + 1) * 512],
                       False, j == 3, [midTn, "w_bwT"], [PWO])
            R("vector", lambda e: e.tensor_add(base, base, pwo),
              reads=[basen, PWO], writes=[basen])
            R("sync", lambda e, cc=i: e.dma_start(
                out=ap["y"][cc * C:(cc + 1) * C, :], in_=base),
              reads=[basen], writes=["d:y"], queue="qs")

        B0(0)
        B0(1)
        for i in range(NCH + 1):
            if 0 <= i + 2 < NCH:
                B0(i + 2)
            if 0 <= i - 1 < NCH:
                B2mid(i - 1)
            if 0 <= i < NCH:
                B1a(i)
            if 0 <= i - 1 < NCH:
                B2og(i - 1, 0)
            if 0 <= i < NCH:
                B1b(i)
            if 0 <= i - 1 < NCH:
                B2og(i - 1, 1)
            if 0 <= i < NCH:
                B1c(i)
            if 0 <= i - 1 < NCH:
                B2wo(i - 1)

        p.barrier()
        block = es.enter_context(nc.Block())
        p.finalize(es, block)
    return nc


def _prep_shared(inputs):
    con = _tile_consts()
    w = {}
    w["wq"] = np.ascontiguousarray(inputs["L0_attn_Wq"], np.float32)
    w["wk"] = np.ascontiguousarray(inputs["L0_attn_Wk"], np.float32)
    w["wv"] = np.ascontiguousarray(inputs["L0_attn_Wv"], np.float32)
    w["wgk1"] = np.ascontiguousarray(inputs["L0_attn_Wgk1"], np.float32)
    w["wgk2"] = np.ascontiguousarray(inputs["L0_attn_Wgk2"], np.float32)
    w["bgk2"] = np.ascontiguousarray(inputs["L0_attn_bgk2"], np.float32).reshape(1, D)
    w["wg"] = np.ascontiguousarray(inputs["L0_attn_Wg"], np.float32)
    w["wo"] = np.ascontiguousarray(inputs["L0_attn_Wo"], np.float32)
    w["alay"] = np.ascontiguousarray(
        np.asarray(inputs["L0_ffn_A"], np.float32).reshape(512, D).T)
    w["bwT"] = np.ascontiguousarray(
        np.asarray(inputs["L0_ffn_Bw"], np.float32).transpose(0, 2, 1).reshape(512, D))
    w["wrT"] = np.ascontiguousarray(np.asarray(inputs["L0_ffn_Wr"], np.float32).T)
    w["gnw"] = np.ascontiguousarray(
        np.asarray(inputs["L0_attn_g_norm_weight"], np.float32).reshape(4, 128).T)
    x = np.asarray(inputs["L0_attn_x"], np.float32).reshape(NCORES * TOK, D)
    w["x0"] = np.ascontiguousarray(x[0:1, :])
    return con, w, x


def kernel(**inputs):
    con, w, x = _prep_shared(inputs)

    nc_a = build_phase_a()
    in_a = []
    for cidx in range(NCORES):
        m = dict(
            x=np.ascontiguousarray(x[cidx * TOK:(cidx + 1) * TOK]),
            wq=w["wq"], wk=w["wk"], wv=w["wv"], wgk1=w["wgk1"], wgk2=w["wgk2"],
            bgk2=w["bgk2"],
            ctrib=con["trib"], ctrir=con["trir"], cidentb=con["identb"],
            conesb=con["onesb"], cmask=con["mask"], cidentf=con["identf"],
            cln16=np.full((C, 1), -LN16, np.float32),
        )
        in_a.append(m)
    res_a = run_bass_kernel_spmd(nc_a, in_a, core_ids=list(range(NCORES)))
    ra = res_a.results

    T_all = np.concatenate([np.asarray(ra[cidx]["T_d"]) for cidx in range(NCORES)], axis=0)
    btot_all = np.zeros((8, D), np.float32)
    for cidx in range(NCORES):
        btot_all[cidx] = np.asarray(ra[cidx]["btot_d"])[0]

    nc_b = build_phase_b()
    in_b = []
    for cidx in range(NCORES):
        m = dict(
            o16_d=np.asarray(ra[cidx]["o16_d"]),
            qgT_d=np.asarray(ra[cidx]["qgT_d"]),
            base_d=np.asarray(ra[cidx]["base_d"]),
            xnT_d=np.asarray(ra[cidx]["xnT_d"]),
            T_all=T_all, btot_all=btot_all,
            x0=w["x0"], wg=w["wg"], wo=w["wo"], alay=w["alay"], bwT=w["bwT"],
            wrT=w["wrT"], gnw=w["gnw"],
            cmatx=_cmat(cidx),
            wmrow=np.where(np.arange(8) >= cidx, -1e4, 0.0).astype(np.float32)
                    .astype(BF).reshape(1, 8),
            conesb=con["onesb"],
            cidentb=con["identb"], cidentf=con["identf"], cesel=con["esel"],
        )
        in_b.append(m)
    res_b = run_bass_kernel_spmd(nc_b, in_b, core_ids=list(range(NCORES)))
    y = np.concatenate([np.asarray(res_b.results[cidx]["y"]) for cidx in range(NCORES)],
                       axis=0)
    return y.reshape(1, NCORES * TOK, D)


# version 10
# speedup vs baseline: 1.0276x; 1.0276x over previous
import numpy as np
import ml_dtypes
from contextlib import ExitStack

import concourse.bass as bass
import concourse.mybir as mybir
from concourse.bass_utils import run_bass_kernel_spmd

F32 = mybir.dt.float32
BF16 = mybir.dt.bfloat16
F16 = mybir.dt.float16
AF = mybir.ActivationFunctionType
ALU = mybir.AluOpType
AX = mybir.AxisListType
I32 = mybir.dt.int32
BF = ml_dtypes.bfloat16

NCORES = 8
TOK = 2048
D = 1024
C = 128
NCH = TOK // C
HEADS = 4
DK = 256
DV = 512
EPS = 1e-5
LN16 = float(np.log(16.0))

ENGS = ("tensor", "vector", "scalar", "gpsimd", "sync")


class Prog:

    NSLOT = 8
    NSLOT_MAP = {}

    def _nslot(self, q):
        return self.NSLOT_MAP.get(q, self.NSLOT)

    def __init__(self, nc):
        self.nc = nc
        self.ops = {e: [] for e in ENGS}
        self.nops = {e: 0 for e in ENGS}
        self.qcount = {}
        self.last_write = {}
        self.readers = {}
        self.last_psum = {}
        self.waited = {e: {} for e in ENGS}
        self.need_inc = {e: set() for e in ENGS}

    def _src_key(self, tok):
        return ("E", tok[1]) if tok[0] == "E" else ("Q",) + tuple(tok[1])

    def rec(self, eng, emit, reads=(), writes=(), queue=None, tag=""):
        deps = {}

        def add(tok):
            k = self._src_key(tok)
            v = tok[2]
            if k not in deps or deps[k] < v:
                deps[k] = v

        for r in reads:
            t = self.last_write.get(r)
            if t is not None:
                add(t)
            if r.startswith("ps_"):
                t2 = self.last_psum.get(r)
                if t2 is not None:
                    add(t2)
        for w in writes:
            t = self.last_write.get(w)
            if t is not None:
                add(t)
            for kk, vv in self.readers.get(w, {}).items():
                add(("E", kk[1], vv) if kk[0] == "E" else ("Q", (kk[1], kk[2]), vv))
            if w.startswith("ps_"):
                t2 = self.last_psum.get(w)
                if t2 is not None:
                    add(t2)

        if queue is not None:
            idx = None
            n = self.qcount.get(queue, 0) + 1
            self.qcount[queue] = n
            ns = self._nslot(queue)
            slot = (n - 1) % ns
            gen = (n - 1) // ns + 1
            my = ("Q", (queue, slot), gen)
            if gen > 1:
                add(("Q", (queue, slot), gen - 1))
        else:
            self.nops[eng] += 1
            idx = self.nops[eng]
            my = ("E", eng, idx)
        myk = self._src_key(my)

        waits = []
        for k, v in deps.items():
            if k[0] == "E" and k[1] == "tensor" and eng == "tensor":
                continue
            got = self.waited[eng].get(k, 0)
            if v <= got:
                continue
            waits.append((k, v))
            self.waited[eng][k] = v
            if k[0] == "E":
                self.need_inc[k[1]].add(v)

        for r in reads:
            self.readers.setdefault(r, {})[myk] = max(
                self.readers.get(r, {}).get(myk, 0), my[2]
            )
            if r.startswith("ps_"):
                self.last_psum[r] = my
        for w in writes:
            self.last_write[w] = my
            self.readers[w] = {}
            if w.startswith("ps_"):
                self.last_psum[w] = my

        self.ops[eng].append(
            dict(emit=emit, waits=waits, queue=queue, idx=idx, tag=tag)
        )
        return my

    def _queue_toks(self):
        toks = []
        for q, n in self.qcount.items():
            ns = self._nslot(q)
            for slot in range(min(n, ns)):
                gen = (n - 1 - slot) // ns + 1
                toks.append(("Q", (q, slot), gen))
        return toks

    def barrier(self):
        toks = []
        for e in ENGS:
            if self.nops[e]:
                toks.append(("E", e, self.nops[e]))
        toks += self._queue_toks()
        for e in ENGS:
            for t in toks:
                k = self._src_key(t)
                if k == ("E", e):
                    continue
                v = t[2]
                if self.waited[e].get(k, 0) < v:
                    self.waited[e][k] = v
                    if k[0] == "E":
                        self.need_inc[k[1]].add(v)
                    self.ops[e].append(dict(emit=None, waits=[(k, v)], queue=None,
                                            idx=None, tag="barrier"))

    def finalize(self, es, block):
        sems = {}
        for e in ENGS:
            sems[("E", e)] = es.enter_context(self.nc.semaphore(f"sem_{e}"))
        for q in self.qcount:
            for slot in range(self._nslot(q)):
                sems[("Q", q, slot)] = es.enter_context(
                    self.nc.semaphore(f"sem_{q}_{slot}"))
        inc_prefix = {}
        for e in ENGS:
            pref = {}
            cnt = 0
            marks = self.need_inc[e]
            for i in range(1, self.nops[e] + 1):
                if i in marks:
                    cnt += 1
                pref[i] = cnt
            inc_prefix[e] = pref

        def val_of(k, v):
            if k[0] == "Q":
                return 16 * v
            return inc_prefix[k[1]].get(v, 0)

        qseen = {}

        def body(e):
            def run(eng):
                for op in self.ops[e]:
                    for (k, v) in op["waits"]:
                        eng.wait_ge(sems[k], val_of(k, v))
                    if op["emit"] is None:
                        continue
                    inst = op["emit"](eng)
                    if op["queue"] is not None:
                        q = op["queue"]
                        n = qseen.get(q, 0)
                        qseen[q] = n + 1
                        inst.then_inc(sems[("Q", q, n % self.NSLOT)], 16)
                    elif op["idx"] in self.need_inc[e]:
                        inst.then_inc(sems[("E", e)], 1)
            return run

        for e in ENGS:
            if self.ops[e]:
                getattr(block, e)(body(e))


def _tile_consts():
    i = np.arange(C)
    trib = ((i[:, None] <= i[None, :]).astype(np.float32) / 16.0).astype(BF)
    trir = ((i[:, None] > i[None, :]).astype(np.float32) / 16.0).astype(BF)
    mask = np.tile((i[:, None] <= i[None, :]).astype(np.float32), (1, 4))
    identb = np.eye(C, dtype=np.float32).astype(BF)
    identf = np.eye(C, dtype=np.float32)
    onesb = np.ones((C, C), dtype=np.float32).astype(BF)
    esel = np.zeros((8, 512), dtype=np.float32)
    for p in range(512):
        esel[p // 64, p] = 1.0
    return dict(trib=trib, trir=trir, mask=mask, identb=identb, identf=identf,
                onesb=onesb, esel=esel.astype(BF))


def _cmat(core):
    m_ = np.zeros((8, 8), dtype=np.float32)
    for r in range(8):
        for m in range(8):
            if r < m < core:
                m_[m, r] = 1.0
    for r in range(8):
        if r >= core:
            m_[:, r] = 0.0
    return m_.astype(BF)


def _wmask(core):
    m = np.zeros((8,), np.float32)
    m[:core] = 1.0
    return m



def build_phase_a():
    nc = bass.Bass("TRN2", target_bir_lowering=False, debug=False)
    ap = {}
    ap["x"] = nc.dram_tensor("x", [TOK, D], F32, kind="ExternalInput").ap()
    for n, shp in [("wq", [D, D]), ("wk", [D, D]), ("wv", [D, 2 * D]),
                   ("wgk1", [D, 16]), ("wgk2", [16, D]), ("bgk2", [1, D])]:
        ap[n] = nc.dram_tensor(n, shp, F32, kind="ExternalInput").ap()
    for n, shp in [("ctrib", [C, C]), ("ctrir", [C, C]), ("cidentb", [C, C]),
                   ("conesb", [C, C])]:
        ap[n] = nc.dram_tensor(n, shp, BF16, kind="ExternalInput").ap()
    ap["cmask"] = nc.dram_tensor("cmask", [C, 4 * C], F32, kind="ExternalInput").ap()
    ap["cidentf"] = nc.dram_tensor("cidentf", [C, C], F32, kind="ExternalInput").ap()
    ap["cln16"] = nc.dram_tensor("cln16", [C, 1], F32, kind="ExternalInput").ap()
    ap["o16_d"] = nc.dram_tensor("o16_d", [NCH, C, 2 * D], F16, kind="ExternalOutput").ap()
    ap["qgT_d"] = nc.dram_tensor("qgT_d", [NCH, C, D], BF16, kind="ExternalOutput").ap()
    ap["base_d"] = nc.dram_tensor("base_d", [NCH, C, D], F32, kind="ExternalOutput").ap()
    ap["xnT_d"] = nc.dram_tensor("xnT_d", [NCH, C, D], BF16, kind="ExternalOutput").ap()
    ap["T_d"] = nc.dram_tensor("T_d", [C, 8 * DV], BF16, kind="ExternalOutput").ap()
    ap["btot_d"] = nc.dram_tensor("btot_d", [1, D], F32, kind="ExternalOutput").ap()
    for n, shp in [("wg", [D, 2 * D]), ("wo", [2 * D, D]),
                   ("alay", [D, 512]), ("bwT", [512, D])]:
        ap[n] = nc.dram_tensor(n, shp, F32, kind="ExternalInput").ap()
        ap[n + "b"] = nc.dram_tensor(n + "b", shp, BF16, kind="ExternalOutput").ap()

    es = ExitStack()
    with es:
        def sb(n, s, t):
            return es.enter_context(nc.sbuf_tensor(n, s, t))[:]

        def psm(n, s, t):
            return es.enter_context(nc.psum_tensor(n, s, t))[:]

        w_q = sb("w_q", [128, 8 * D], BF16)
        w_k = sb("w_k", [128, 8 * D], BF16)
        w_v = sb("w_v", [128, 8 * 2 * D], BF16)
        w_gk1 = sb("w_gk1", [128, 8 * 16], BF16)
        w_gk2 = sb("w_gk2", [16, D], BF16)
        s_bgk2 = sb("s_bgk2", [1, D], BF16)
        c_trib = sb("c_trib", [C, C], BF16)
        c_trir = sb("c_trir", [C, C], BF16)
        c_identb = sb("c_identb", [C, C], BF16)
        c_ones = sb("c_ones", [C, C], BF16)
        c_mask4 = sb("c_mask4", [C, 4 * C], F32)
        c_ln16 = sb("c_ln16", [C, 1], F32)
        s_Sb = sb("s_Sb", [128, 8 * DV], BF16)
        s_bpreT = sb("s_bpreT", [C, 8], F32)
        s_btrow = sb("s_btrow", [1, D], F32)
        c_identf = sb("c_identf", [C, C], F32)

        spec = {
            "x": ([C, D], F32, 2), "st": ([C, 8], F32, 1), "xn": ([C, D], F32, 1),
            "xnb": ([C, D], BF16, 1), "xnT": ([C, D], BF16, 3), "f1": ([C, D], F32, 1),
            "ghi": ([C, D], BF16, 2), "glo": ([C, D], BF16, 2),
            "z1e": ([C, 16], BF16, 1), "z1b": ([16, C], BF16, 1),
            "lam": ([C, D], BF16, 2), "lami": ([C, D], BF16, 2),
            "lrev": ([C, D], BF16, 2),
            "qh": ([C, D], BF16, 2),
            "bnd": ([C, 8], F32, 2), "expbp": ([C, 8], F32, 4),
            "kt": ([C, D], BF16, 2), "kh": ([C, D], BF16, 3),
            "v": ([C, 2 * D], BF16, 3),
            "qhT": ([C, D], BF16, 3), "ktT": ([C, D], BF16, 2), "qgT": ([C, D], BF16, 2),
            "pt": ([C, 4 * C], BF16, 2), "o16": ([C, 2 * D], F16, 2),
            "dc": ([C, 8], F32, 5),
        }
        tiles = {}
        for nm, (shp, dt2, nb) in spec.items():
            tiles[nm] = [sb(f"t_{nm}_{b}", shp, dt2) for b in range(nb)]

        def T(nm, i):
            nb = spec[nm][2]
            return tiles[nm][i % nb], f"t_{nm}_{i % nb}"

        ptb1 = psm("ptb1", [128, 1024], BF16)
        ptb4 = psm("ptb4", [128, 1024], BF16)
        pc = psm("pc", [128, 1024], F32)
        pp = psm("pp", [128, 1024], F32)
        psc = psm("psc", [128, 512], F32)
        psc2 = psm("psc2", [128, 512], F32)
        PTB1, PTB4, PC, PP, PSC, PSC2 = ("ps_ptb1", "ps_ptb4", "ps_pc", "ps_pp",
                                         "ps_psc", "ps_psc2")

        p = Prog(nc)
        R = p.rec

        def ld(engq, dst, src, rd, wr):
            R(engq[0], lambda e, d=dst, s=src: e.dma_start(out=d, in_=s),
              reads=rd, writes=wr, queue=engq[1])

        gq = ("gpsimd", "qg")
        sq = ("sync", "qs")
        for kt in range(8):
            ld(gq, w_q[:, kt * D:(kt + 1) * D],
               ap["wq"][kt * 128:(kt + 1) * 128, :], ["d:wq"], ["w_q"])
            ld(gq, w_k[:, kt * D:(kt + 1) * D],
               ap["wk"][kt * 128:(kt + 1) * 128, :], ["d:wk"], ["w_k"])
            ld(gq, w_v[:, kt * 2 * D:(kt + 1) * 2 * D],
               ap["wv"][kt * 128:(kt + 1) * 128, :], ["d:wv"], ["w_v"])
        ld(gq, w_gk1.rearrange("p (kt n) -> p kt n", kt=8),
           ap["wgk1"].rearrange("(kt p) n -> p kt n", p=128), ["d:wgk1"], ["w_gk1"])
        ld(gq, w_gk2, ap["wgk2"], ["d:wgk2"], ["w_gk2"])
        ld(gq, s_bgk2, ap["bgk2"], ["d:bgk2"], ["s_bgk2"])
        ld(gq, c_trib, ap["ctrib"], ["d:ctrib"], ["c_trib"])
        ld(gq, c_trir, ap["ctrir"], ["d:ctrir"], ["c_trir"])
        ld(gq, c_identb, ap["cidentb"], ["d:cidentb"], ["c_identb"])
        ld(gq, c_ones, ap["conesb"], ["d:conesb"], ["c_ones"])
        ld(sq, c_mask4, ap["cmask"], ["d:cmask"], ["c_mask4"])
        ld(sq, c_identf, ap["cidentf"], ["d:cidentf"], ["c_identf"])
        ld(sq, c_ln16, ap["cln16"], ["d:cln16"], ["c_ln16"])
        for n in ("wg", "wo", "alay", "bwT"):
            ld(gq, ap[n + "b"], ap[n], ["d:" + n], ["d:" + n + "b"])
        R("vector", lambda e: e.memset(s_Sb, 0.0), writes=["s_Sb"])
        R("vector", lambda e: e.memset(s_bpreT, 0.0), writes=["s_bpreT"])

        def mm(out, lhsT, rhs, start, stop, rd, wr):
            R("tensor", lambda e, o=out, l=lhsT, r=rhs, s1=start, s2=stop:
              e.matmul(o, l, r, start=s1, stop=s2), reads=rd, writes=wr)

        def tr(out, in_, ident, rd, wr):
            R("tensor", lambda e, o=out, i=in_, d=ident: e.transpose(o, i, d),
              reads=rd, writes=wr)

        def S0(i):
            d, dn = T("x", i)
            R("sync", lambda e, dd=d, sl=ap["x"][i * C:(i + 1) * C, :]:
              e.dma_start(out=dd, in_=sl), reads=["d:x"], writes=[dn], queue="qs")

        def S1a(i):
            x, xn_ = T("x", i)
            st, stn = T("st", i)
            xn, xnn = T("xn", i)
            xnb, xnbn = T("xnb", i)
            xnT, xnTn = T("xnT", i)
            f1, f1n = T("f1", i)
            R("vector", lambda e: e.reduce_sum(st[:, 0:1], x, axis=AX.X),
              reads=[xn_], writes=[stn])
            R("scalar", lambda e: e.activation(out=f1, in_=x, func=AF.Square,
                                               accum_out=st[:, 1:2]),
              reads=[xn_], writes=[f1n, stn])
            R("vector", lambda e: e.tensor_scalar(out=st[:, 2:3], in0=st[:, 0:1],
                                                  scalar1=1.0 / D, scalar2=None, op0=ALU.mult),
              reads=[stn], writes=[stn])
            R("vector", lambda e: e.tensor_scalar(out=st[:, 3:4], in0=st[:, 1:2],
                                                  scalar1=1.0 / D, scalar2=None, op0=ALU.mult),
              reads=[stn], writes=[stn])
            R("vector", lambda e: e.tensor_mul(st[:, 4:5], st[:, 2:3], st[:, 2:3]),
              reads=[stn], writes=[stn])
            R("vector", lambda e: e.tensor_sub(st[:, 4:5], st[:, 3:4], st[:, 4:5]),
              reads=[stn], writes=[stn])
            R("vector", lambda e: e.tensor_scalar(out=st[:, 4:5], in0=st[:, 4:5],
                                                  scalar1=1.0, scalar2=EPS, op0=ALU.mult,
                                                  op1=ALU.add),
              reads=[stn], writes=[stn])
            R("scalar", lambda e: e.activation(out=st[:, 5:6], in_=st[:, 4:5], func=AF.Ln),
              reads=[stn], writes=[stn])
            R("scalar", lambda e: e.activation(out=st[:, 6:7], in_=st[:, 5:6], func=AF.Exp,
                                               scale=-0.5),
              reads=[stn], writes=[stn])
            R("vector", lambda e: e.tensor_scalar(out=xn, in0=x, scalar1=st[:, 2:3],
                                                  scalar2=st[:, 6:7], op0=ALU.subtract,
                                                  op1=ALU.mult),
              reads=[xn_, stn], writes=[xnn])
            R("scalar", lambda e: e.copy(xnb, xn), reads=[xnn], writes=[xnbn])
            R("vector", lambda e: e.tensor_add(x, x, xn), reads=[xn_, xnn], writes=[xn_])
            R("sync", lambda e, cc=i: e.dma_start(out=ap["base_d"][cc], in_=x),
              reads=[xn_], writes=["d:base"], queue="qs")
            for j in range(8):
                tr(ptb1[:, j * C:(j + 1) * C], xnb[:, j * C:(j + 1) * C], c_identb,
                   [xnbn, "c_identb"], [PTB1])
            R("scalar", lambda e: e.copy(xnT, ptb1), reads=[PTB1], writes=[xnTn])
            R("scalar", lambda e, cc=i: e.dma_start(out=ap["xnT_d"][cc], in_=xnT),
              reads=[xnTn], writes=["d:xnT"], queue="qa")

        def S1b(i):
            xnT, xnTn = T("xnT", i)
            xn, xnn = T("xn", i)
            f1, f1n = T("f1", i)
            ghi, ghin = T("ghi", i)
            glo, glon = T("glo", i)
            z1e, z1en = T("z1e", i)
            z1b, z1bn = T("z1b", i)
            for j in range(8):
                mm(pc[:, 0:16], xnT[:, j * C:(j + 1) * C],
                   w_gk1[:, j * 16:(j + 1) * 16], j == 0, j == 7, [xnTn, "w_gk1"], [PC])
            R("vector", lambda e: e.tensor_copy(z1e, pc[:, 0:16]), reads=[PC], writes=[z1en])
            tr(ptb1[0:16, 0:C], z1e, c_identb, [z1en, "c_identb"], [PTB1])
            R("scalar", lambda e: e.copy(z1b, ptb1[0:16, 0:C]), reads=[PTB1], writes=[z1bn])
            for nh in range(2):
                sl = slice(nh * 512, (nh + 1) * 512)
                mm(pc[:, sl], z1b, w_gk2[:, sl], True, False, [z1bn, "w_gk2"], [PC])
                mm(pc[:, sl], c_ones[0:1, :], s_bgk2[:, sl], False, True,
                   ["c_ones", "s_bgk2"], [PC])
            R("scalar", lambda e: e.activation(out=f1, in_=pc, func=AF.Abs),
              reads=[PC], writes=[f1n])
            R("scalar", lambda e: e.activation(out=f1, in_=f1, func=AF.Exp, scale=-1.0),
              reads=[f1n], writes=[f1n])
            R("scalar", lambda e: e.activation(out=f1, in_=f1, func=AF.Ln, bias=1.0),
              reads=[f1n], writes=[f1n])
            R("vector", lambda e: e.tensor_scalar(out=xn, in0=pc, scalar1=0.0,
                                                  scalar2=None, op0=ALU.min),
              reads=[PC], writes=[xnn])
            R("vector", lambda e: e.tensor_sub(xn, xn, f1), reads=[xnn, f1n], writes=[xnn])
            R("scalar", lambda e: e.copy(ghi, xn), reads=[xnn], writes=[ghin])
            R("vector", lambda e: e.tensor_sub(glo, xn, ghi), reads=[xnn, ghin], writes=[glon])

        def S2a(i):
            ghi, ghin = T("ghi", i)
            glo, glon = T("glo", i)
            lam, lamn = T("lam", i)
            lami, lamin = T("lami", i)
            for nh in range(2):
                sl = slice(nh * 512, (nh + 1) * 512)
                mm(pc[:, sl], c_trib, ghi[:, sl], True, False, ["c_trib", ghin], [PC])
                mm(pc[:, sl], c_trib, glo[:, sl], False, True, ["c_trib", glon], [PC])
            R("scalar", lambda e: e.activation(out=lam, in_=pc, func=AF.Exp, bias=c_ln16),
              reads=[PC, "c_ln16"], writes=[lamn])
            R("scalar", lambda e: e.activation(out=lami, in_=pc, func=AF.Exp, scale=-1.0),
              reads=[PC], writes=[lamin])

        def S2b(i):
            ghi, ghin = T("ghi", i)
            glo, glon = T("glo", i)
            lrev, lrevn = T("lrev", i)
            for nh in range(2):
                sl = slice(nh * 512, (nh + 1) * 512)
                mm(pp[:, sl], c_trir, ghi[:, sl], True, False, ["c_trir", ghin], [PP])
                mm(pp[:, sl], c_trir, glo[:, sl], False, True, ["c_trir", glon], [PP])
            R("scalar", lambda e: e.activation(out=lrev, in_=pp, func=AF.Exp),
              reads=[PP], writes=[lrevn])

        def S2c(i):
            ghi, ghin = T("ghi", i)
            glo, glon = T("glo", i)
            dc, dcn = T("dc", i)
            bnd, bndn = T("bnd", i)
            expbp, expbpn = T("expbp", i)
            for j in range(8):
                mm(pc[:, j:j + 1], ghi[:, j * C:(j + 1) * C], c_ones[:, 0:1],
                   j == 0, False, [ghin, "c_ones"], [PC])
                mm(pc[:, j:j + 1], glo[:, j * C:(j + 1) * C], c_ones[:, 0:1],
                   False, j == 7, [glon, "c_ones"], [PC])
            R("scalar", lambda e: e.activation(out=dc, in_=pc[:, 0:8], func=AF.Exp),
              reads=[PC], writes=[dcn])
            R("vector", lambda e: e.tensor_copy(bnd, pc[:, 0:8]), reads=[PC], writes=[bndn])
            R("scalar", lambda e: e.activation(out=expbp, in_=s_bpreT, func=AF.Exp),
              reads=["s_bpreT"], writes=[expbpn])
            R("vector", lambda e: e.tensor_add(s_bpreT, s_bpreT, bnd),
              reads=["s_bpreT", bndn], writes=["s_bpreT"])

        def S3q(i):
            xnT, xnTn = T("xnT", i)
            lam, lamn = T("lam", i)
            qh, qhn = T("qh", i)
            for j in range(8):
                for nh in range(2):
                    sl = slice(nh * 512, (nh + 1) * 512)
                    mm(pp[:, sl], xnT[:, j * C:(j + 1) * C],
                       w_q[:, j * D + nh * 512: j * D + (nh + 1) * 512],
                       j == 0, j == 7, [xnTn, "w_q"], [PP])
            R("vector", lambda e: e.tensor_mul(qh, pp, lam), reads=[PP, lamn], writes=[qhn])

        def S3k(i):
            xnT, xnTn = T("xnT", i)
            lami, lamin = T("lami", i)
            lrev, lrevn = T("lrev", i)
            kt, ktn = T("kt", i)
            kh, khn = T("kh", i)
            for j in range(8):
                for nh in range(2):
                    sl = slice(nh * 512, (nh + 1) * 512)
                    mm(pp[:, sl], xnT[:, j * C:(j + 1) * C],
                       w_k[:, j * D + nh * 512: j * D + (nh + 1) * 512],
                       j == 0, j == 7, [xnTn, "w_k"], [PP])
            R("vector", lambda e: e.tensor_mul(kt, pp, lami), reads=[PP, lamin], writes=[ktn])
            R("vector", lambda e: e.tensor_mul(kh, pp, lrev), reads=[PP, lrevn], writes=[khn])

        def S3v(i, half):
            xnT, xnTn = T("xnT", i)
            v, vn = T("v", i)
            for j in range(8):
                for nh in range(2):
                    mm(pp[:, nh * 512:(nh + 1) * 512], xnT[:, j * C:(j + 1) * C],
                       w_v[:, j * 2 * D + half * D + nh * 512:
                           j * 2 * D + half * D + (nh + 1) * 512],
                       j == 0, j == 7, [xnTn, "w_v"], [PP])
            R("scalar", lambda e, h=half: e.copy(v[:, h * D:(h + 1) * D], pp),
              reads=[PP], writes=[vn])

        def S4(i):
            for src, dst in (("qh", "qhT"), ("kt", "ktT")):
                sT, sn = T(src, i)
                dT, dn = T(dst, i)
                for j in range(8):
                    tr(ptb4[:, j * C:(j + 1) * C], sT[:, j * C:(j + 1) * C], c_identb,
                       [sn, "c_identb"], [PTB4])
                R("scalar", lambda e, d=dT, ps=ptb4: e.copy(d, ps), reads=[PTB4], writes=[dn])
            qhT, qhTn = T("qhT", i)
            qgT, qgTn = T("qgT", i)
            expbp, expbpn = T("expbp", i)
            for j in range(8):
                R("vector", lambda e, jj=j: e.tensor_scalar(
                    out=qgT[:, jj * C:(jj + 1) * C], in0=qhT[:, jj * C:(jj + 1) * C],
                    scalar1=expbp[:, jj:jj + 1], scalar2=None, op0=ALU.mult),
                  reads=[qhTn, expbpn], writes=[qgTn])
            R("scalar", lambda e, cc=i: e.dma_start(out=ap["qgT_d"][cc], in_=qgT),
              reads=[qgTn], writes=["d:qgT"], queue="qa")

        def S5a(i):
            qhT, qhTn = T("qhT", i)
            ktT, ktTn = T("ktT", i)
            pt, ptn = T("pt", i)
            for h in range(4):
                for ji, j in enumerate((2 * h, 2 * h + 1)):
                    mm(psc2[:, h * C:(h + 1) * C], ktT[:, j * C:(j + 1) * C],
                       qhT[:, j * C:(j + 1) * C], (h == 0 and ji == 0),
                       (h == 3 and ji == 1), [ktTn, qhTn], [PSC2])
            R("vector", lambda e: e.tensor_mul(pt, psc2, c_mask4),
              reads=[PSC2, "c_mask4"], writes=[ptn])

        def S5b(i, h):
            qhT, qhTn = T("qhT", i)
            kh, khn = T("kh", i)
            v, vn = T("v", i)
            pt, ptn = T("pt", i)
            o16, o16n = T("o16", i)
            dc, dcn = T("dc", i)
            mm(psc[:, 0:DV], pt[:, h * C:(h + 1) * C], v[:, h * DV:(h + 1) * DV],
               True, False, [ptn, vn], [PSC])
            for ji, j in enumerate((2 * h, 2 * h + 1)):
                mm(psc[:, 0:DV], qhT[:, j * C:(j + 1) * C], s_Sb[:, j * DV:(j + 1) * DV],
                   False, ji == 1, [qhTn, "s_Sb"], [PSC])
            R("scalar", lambda e: e.copy(o16[:, h * DV:(h + 1) * DV], psc[:, 0:DV]),
              reads=[PSC], writes=[o16n])
            for ih in range(2):
                sg = 2 * h + ih
                mm(psc[:, 0:DV], kh[:, h * DK + ih * C: h * DK + (ih + 1) * C],
                   v[:, h * DV:(h + 1) * DV], True, True, [khn, vn], [PSC])
                R("vector", lambda e, g=sg: e.scalar_tensor_tensor(
                    out=s_Sb[:, g * DV:(g + 1) * DV], in0=s_Sb[:, g * DV:(g + 1) * DV],
                    scalar=dc[:, g:g + 1], in1=psc[:, 0:DV],
                    op0=ALU.mult, op1=ALU.add),
                  reads=["s_Sb", dcn, PSC], writes=["s_Sb"])
            if h == 3:
                R("sync", lambda e, cc=i: e.dma_start(out=ap["o16_d"][cc], in_=o16),
                  reads=[o16n], writes=["d:o16"], queue="qs")

        S0(0)
        for i in range(NCH + 6):
            if 0 <= i + 1 < NCH:
                S0(i + 1)
            if 0 <= i - 2 < NCH:
                S3q(i - 2)
            if 0 <= i - 5 < NCH:
                S5b(i - 5, 0)
            if 0 <= i - 3 < NCH:
                S3k(i - 3)
            if 0 <= i - 5 < NCH:
                S5b(i - 5, 1)
            if 0 <= i - 3 < NCH:
                S3v(i - 3, 0)
            if 0 <= i - 5 < NCH:
                S5b(i - 5, 2)
            if 0 <= i - 3 < NCH:
                S3v(i - 3, 1)
            if 0 <= i - 5 < NCH:
                S5b(i - 5, 3)
            if 0 <= i - 4 < NCH:
                S5a(i - 4)
            if 0 <= i - 1 < NCH:
                S2a(i - 1)
            if 0 <= i - 3 < NCH:
                S4(i - 3)
            if 0 <= i - 1 < NCH:
                S2b(i - 1)
            if 0 <= i - 1 < NCH:
                S2c(i - 1)
            if 0 <= i < NCH:
                S1a(i)
            if 0 <= i < NCH:
                S1b(i)

        R("sync", lambda e: e.dma_start(out=ap["T_d"], in_=s_Sb),
          reads=["s_Sb"], writes=["d:T"], queue="qs")
        for j in range(8):
            tr(pc[0:1, j * C:(j + 1) * C], s_bpreT[:, j:j + 1], c_identf,
               ["s_bpreT", "c_identf"], [PC])
        R("vector", lambda e: e.tensor_copy(s_btrow, pc[0:1, :]),
          reads=[PC], writes=["s_btrow"])
        R("sync", lambda e: e.dma_start(out=ap["btot_d"], in_=s_btrow),
          reads=["s_btrow"], writes=["d:btot"], queue="qs")
        p.barrier()

        block = es.enter_context(nc.Block())
        p.finalize(es, block)
    return nc


def build_phase_b():
    nc = bass.Bass("TRN2", target_bir_lowering=False, debug=False)
    ap = {}
    for n, shp, dt in [("o16_d", [NCH, C, 2 * D], F16), ("qgT_d", [NCH, C, D], BF16),
                       ("base_d", [NCH, C, D], F32), ("xnT_d", [NCH, C, D], BF16),
                       ("T_all", [NCORES * C, 8 * DV], BF16), ("btot_all", [8, D], F32),
                       ("x0", [1, D], F32),
                       ("wg", [D, 2 * D], BF16), ("wo", [2 * D, D], BF16),
                       ("alay", [D, 512], BF16), ("bwT", [512, D], BF16),
                       ("wrT", [D, 8], F32), ("gnw", [128, 4], F32),
                       ("cmatx", [8, 8], BF16), ("wmrow", [1, 8], BF16),
                       ("conesb", [C, C], BF16),
                       ("cidentb", [C, C], BF16), ("cidentf", [C, C], F32),
                       ("cesel", [8, 512], BF16)]:
        ap[n] = nc.dram_tensor(n, shp, dt, kind="ExternalInput").ap()
    ap["y"] = nc.dram_tensor("y", [TOK, D], F32, kind="ExternalOutput").ap()

    es = ExitStack()
    with es:
        def sb(n, s, t):
            return es.enter_context(nc.sbuf_tensor(n, s, t))[:]

        def psm(n, s, t):
            return es.enter_context(nc.psum_tensor(n, s, t))[:]

        w_g = sb("w_g", [128, 8 * 2 * D], BF16)
        w_wo = sb("w_wo", [128, 16 * D], BF16)
        w_alay = sb("w_alay", [128, 8 * 512], BF16)
        w_bwT = sb("w_bwT", [128, 4 * D], BF16)
        w_wrT = sb("w_wrT", [128, 8 * 8], F32)
        s_gnw = sb("s_gnw", [128, 4], F32)
        c_identb = sb("c_identb", [C, C], BF16)
        c_identf = sb("c_identf", [C, C], F32)
        c_esel = sb("c_esel", [8, 512], BF16)
        c_cmat = sb("c_cmat", [8, 8], BF16)
        c_ones = sb("c_ones", [C, C], BF16)
        s_wmrow = sb("s_wmrow", [1, 8], BF16)
        s_x0 = sb("s_x0", [1, D], F32)
        s_x0b = sb("s_x0b", [1, D], F32)
        g_st = sb("g_st", [1, 8], F32)
        g_junk = sb("g_junk", [1, D], F32)
        s_x0T = sb("s_x0T", [128, 8], F32)
        g_lg8 = sb("g_lg8", [8, 1], F32)
        g_lg = sb("g_lg", [1, 8], F32)
        g_d = sb("g_d", [1, 8], F32)
        g_ind = sb("g_ind", [1, 8], F32)
        g_l2 = sb("g_l2", [1, 8], F32)
        g_keep = sb("g_keep", [1, 8], F32)
        g_e = sb("g_e", [1, 8], F32)
        g_gt = sb("g_gt", [1, 8], F32)
        g_gc8 = sb("g_gc8", [8, 1], F32)
        g_gcb = sb("g_gcb", [8, 4], BF16)
        s_gc = sb("s_gc", [128, 4], F32)
        b_bt = sb("b_bt", [8, D], F32)
        b_bthi = sb("b_bthi", [8, D], BF16)
        b_btlo = sb("b_btlo", [8, D], BF16)
        b_w = sb("b_w", [128, 64], F32)
        b_Tr = [sb("b_Tr_0", [128, 8 * DV], BF16), sb("b_Tr_1", [128, 8 * DV], BF16)]
        s_Sin = sb("s_Sin", [128, 8 * DV], BF16)

        spec = {
            "xnT": ([C, D], BF16, 3), "o16": ([C, 2 * D], F16, 4),
            "qgT": ([C, D], BF16, 4), "base": ([C, D], F32, 5),
            "ob": ([C, 2 * D], BF16, 2), "rs": ([C, 4], F32, 2),
            "sd": ([C, 4], F32, 2), "rst": ([C, 4], F32, 2),
            "junk": ([C, DV], F32, 1), "sg": ([C, 2 * D], BF16, 2),
            "ogT": ([C, 2 * D], BF16, 1),
            "mid": ([C, 512], BF16, 3), "midT": ([C, 512], BF16, 1),
        }
        tiles = {}
        for nm, (shp, dt2, nb) in spec.items():
            tiles[nm] = [sb(f"b_{nm}_{b}", shp, dt2) for b in range(nb)]

        def T(nm, i):
            nb = spec[nm][2]
            return tiles[nm][i % nb], f"b_{nm}_{i % nb}"

        pgA = psm("pgA", [128, 1024], F32)
        pd = psm("pd", [128, 1024], F32)
        pm = psm("pm", [128, 512], F32)
        pwo = psm("pwo", [128, 1024], F32)
        ptbB = psm("ptbB", [128, 1024], BF16)
        PGA, PD, PM, PWO, PTBB = "ps_pgA", "ps_pd", "ps_pm", "ps_pwo", "ps_ptbB"

        p = Prog(nc)
        R = p.rec

        def mm(out, lhsT, rhs, start, stop, rd, wr):
            R("tensor", lambda e, o=out, l=lhsT, r=rhs, s1=start, s2=stop:
              e.matmul(o, l, r, start=s1, stop=s2), reads=rd, writes=wr)

        def tr(out, in_, ident, rd, wr):
            R("tensor", lambda e, o=out, i=in_, d=ident: e.transpose(o, i, d),
              reads=rd, writes=wr)

        def ldg(dst, src, rd, wr):
            R("gpsimd", lambda e, d=dst, s=src: e.dma_start(out=d, in_=s),
              reads=rd, writes=wr, queue="qg")

        def lda(dst, src, rd, wr):
            R("scalar", lambda e, d=dst, s=src: e.dma_start(out=d, in_=s),
              reads=rd, writes=wr, queue="qa")

        ldg(c_identb, ap["cidentb"], ["d:cidentb"], ["c_identb"])
        ldg(c_esel, ap["cesel"], ["d:cesel"], ["c_esel"])
        ldg(c_cmat, ap["cmatx"], ["d:cmatx"], ["c_cmat"])
        ldg(c_ones, ap["conesb"], ["d:conesb"], ["c_ones"])
        ldg(s_wmrow, ap["wmrow"], ["d:wmrow"], ["s_wmrow"])
        ldg(w_alay.rearrange("p (kt n) -> p kt n", kt=8),
            ap["alay"].rearrange("(kt p) n -> p kt n", p=128), ["d:alay"], ["w_alay"])
        ldg(w_bwT.rearrange("p (kt n) -> p kt n", kt=4),
            ap["bwT"].rearrange("(kt p) n -> p kt n", p=128), ["d:bwT"], ["w_bwT"])
        for half in range(2):
            lda(w_g[:, half * 8 * D:(half + 1) * 8 * D]
                .rearrange("p (kt n) -> p kt n", kt=4),
                ap["wg"][half * 512:(half + 1) * 512, :]
                .rearrange("(kt p) n -> p kt n", p=128),
                ["d:wg"], [f"w_gh{half}"])
        R("sync", lambda e: e.dma_start(
            out=w_wrT.rearrange("p (kt n) -> p kt n", kt=8),
            in_=ap["wrT"].rearrange("(kt p) n -> p kt n", p=128)),
          reads=["d:wrT"], writes=["w_wrT"], queue="qs")
        R("sync", lambda e: e.dma_start(out=s_gnw, in_=ap["gnw"]),
          reads=["d:gnw"], writes=["s_gnw"], queue="qs")
        R("sync", lambda e: e.dma_start(out=c_identf, in_=ap["cidentf"]),
          reads=["d:cidentf"], writes=["c_identf"], queue="qs")
        R("sync", lambda e: e.dma_start(out=s_x0, in_=ap["x0"]),
          reads=["d:x0"], writes=["s_x0"], queue="qs")
        R("sync", lambda e: e.dma_start(out=b_bt, in_=ap["btot_all"]),
          reads=["d:btot"], writes=["b_bt"], queue="qs")

        def emit_gates_a():
            R("vector", lambda e: e.reduce_sum(g_st[:, 0:1], s_x0, axis=AX.X),
              reads=["s_x0"], writes=["g_st"])
            R("scalar", lambda e: e.activation(out=g_junk, in_=s_x0, func=AF.Square,
                                               accum_out=g_st[:, 1:2]),
              reads=["s_x0"], writes=["g_junk", "g_st"])
            R("vector", lambda e: e.tensor_scalar(out=g_st[:, 2:3], in0=g_st[:, 0:1],
                                                  scalar1=1.0 / D, scalar2=None, op0=ALU.mult),
              reads=["g_st"], writes=["g_st"])
            R("vector", lambda e: e.tensor_scalar(out=g_st[:, 3:4], in0=g_st[:, 1:2],
                                                  scalar1=1.0 / D, scalar2=None, op0=ALU.mult),
              reads=["g_st"], writes=["g_st"])
            R("vector", lambda e: e.tensor_mul(g_st[:, 4:5], g_st[:, 2:3], g_st[:, 2:3]),
              reads=["g_st"], writes=["g_st"])
            R("vector", lambda e: e.tensor_sub(g_st[:, 4:5], g_st[:, 3:4], g_st[:, 4:5]),
              reads=["g_st"], writes=["g_st"])
            R("vector", lambda e: e.tensor_scalar(out=g_st[:, 4:5], in0=g_st[:, 4:5],
                                                  scalar1=1.0, scalar2=EPS, op0=ALU.mult,
                                                  op1=ALU.add),
              reads=["g_st"], writes=["g_st"])
            R("scalar", lambda e: e.activation(out=g_st[:, 5:6], in_=g_st[:, 4:5], func=AF.Ln),
              reads=["g_st"], writes=["g_st"])
            R("scalar", lambda e: e.activation(out=g_st[:, 6:7], in_=g_st[:, 5:6], func=AF.Exp,
                                               scale=-0.5),
              reads=["g_st"], writes=["g_st"])
            R("vector", lambda e: e.tensor_scalar(out=s_x0b, in0=s_x0, scalar1=g_st[:, 2:3],
                                                  scalar2=g_st[:, 6:7], op0=ALU.subtract,
                                                  op1=ALU.mult),
              reads=["s_x0", "g_st"], writes=["s_x0b"])
            for j in range(8):
                tr(pm[:, j:j + 1], s_x0b[:, j * C:(j + 1) * C], c_identf[0:1, 0:1],
                   ["s_x0b", "c_identf"], [PM])
            R("scalar", lambda e: e.copy(s_x0T, pm[:, 0:8]), reads=[PM], writes=["s_x0T"])
            lda(w_wo.rearrange("p (kt n) -> p kt n", kt=16),
                ap["wo"].rearrange("(kt p) n -> p kt n", p=128),
                ["d:wo"], ["w_wo"])

        def emit_gates_b():
            for j in range(8):
                mm(pd[0:8, 0:1], w_wrT[:, j * 8:(j + 1) * 8], s_x0T[:, j:j + 1],
                   j == 0, j == 7, ["w_wrT", "s_x0T"], [PD])
            R("scalar", lambda e: e.copy(g_lg8, pd[0:8, 0:1]), reads=[PD], writes=["g_lg8"])
            tr(pm[0:1, 0:8], g_lg8, c_identf[0:8, 0:8], ["g_lg8", "c_identf"], [PM])
            R("scalar", lambda e: e.copy(g_lg, pm[0:1, 0:8]), reads=[PM], writes=["g_lg"])

        def emit_gates_c():
            R("vector", lambda e: e.reduce_max(g_st[:, 7:8], g_lg, axis=AX.X),
              reads=["g_lg"], writes=["g_st"])
            R("vector", lambda e: e.tensor_scalar(out=g_d, in0=g_lg, scalar1=g_st[:, 7:8],
                                                  scalar2=None, op0=ALU.subtract),
              reads=["g_lg", "g_st"], writes=["g_d"])
            R("vector", lambda e: e.tensor_scalar(out=g_ind, in0=g_d, scalar1=0.0,
                                                  scalar2=None, op0=ALU.is_ge),
              reads=["g_d"], writes=["g_ind"])
            R("vector", lambda e: e.scalar_tensor_tensor(out=g_l2, in0=g_ind, scalar=-1e30,
                                                         in1=g_lg, op0=ALU.mult, op1=ALU.add),
              reads=["g_ind", "g_lg"], writes=["g_l2"])
            R("vector", lambda e: e.reduce_max(g_st[:, 0:1], g_l2, axis=AX.X),
              reads=["g_l2"], writes=["g_st"])
            R("vector", lambda e: e.tensor_scalar(out=g_keep, in0=g_lg, scalar1=g_st[:, 0:1],
                                                  scalar2=None, op0=ALU.is_ge),
              reads=["g_lg", "g_st"], writes=["g_keep"])
            R("scalar", lambda e: e.activation(out=g_e, in_=g_d, func=AF.Exp),
              reads=["g_d"], writes=["g_e"])
            R("vector", lambda e: e.tensor_mul(g_e, g_e, g_keep),
              reads=["g_e", "g_keep"], writes=["g_e"])
            R("vector", lambda e: e.reduce_sum(g_st[:, 1:2], g_e, axis=AX.X),
              reads=["g_e"], writes=["g_st"])
            R("vector", lambda e: e.reciprocal(g_st[:, 2:3], g_st[:, 1:2]),
              reads=["g_st"], writes=["g_st"])
            R("vector", lambda e: e.tensor_scalar(out=g_gt, in0=g_e, scalar1=g_st[:, 2:3],
                                                  scalar2=None, op0=ALU.mult),
              reads=["g_e", "g_st"], writes=["g_gt"])
            tr(pwo[0:8, 8:9], g_gt, c_identf[0:1, 0:1], ["g_gt", "c_identf"], [PWO])
            R("scalar", lambda e: e.copy(g_gc8, pwo[0:8, 8:9]), reads=[PWO], writes=["g_gc8"])
            R("vector", lambda e: e.tensor_copy(g_gcb[:, 0:1], g_gc8),
              reads=["g_gc8"], writes=["g_gcb"])
            R("vector", lambda e: e.tensor_sub(g_gcb[:, 2:3], g_gc8, g_gcb[:, 0:1]),
              reads=["g_gc8", "g_gcb"], writes=["g_gcb"])
            for tau in range(4):
                for hl in range(2):
                    mm(pwo[:, tau:tau + 1], c_esel[:, tau * 128:(tau + 1) * 128],
                       g_gcb[:, 2 * hl:2 * hl + 1], (tau == 0 and hl == 0),
                       (tau == 3 and hl == 1), ["c_esel", "g_gcb"], [PWO])
            R("scalar", lambda e: e.copy(s_gc, pwo[:, 0:4]), reads=[PWO], writes=["s_gc"])
            for tau in range(4):
                R("vector", lambda e, u=tau: e.tensor_scalar(
                    out=w_bwT[:, u * D:(u + 1) * D], in0=w_bwT[:, u * D:(u + 1) * D],
                    scalar1=s_gc[:, u:u + 1], scalar2=None, op0=ALU.mult),
                  reads=["w_bwT", "s_gc"], writes=["w_bwT"])
            for k in range(16):
                R("vector", lambda e, kk=k: e.tensor_scalar(
                    out=w_wo[:, kk * D:(kk + 1) * D], in0=w_wo[:, kk * D:(kk + 1) * D],
                    scalar1=s_gnw[:, kk % 4:kk % 4 + 1], scalar2=None, op0=ALU.mult),
                  reads=["w_wo", "s_gnw"], writes=[f"w_wo{k}"])


        def emit_combine():
            R("vector", lambda e: e.tensor_copy(b_bthi, b_bt), reads=["b_bt"], writes=["b_bthi"])
            R("vector", lambda e: e.tensor_sub(b_btlo, b_bt, b_bthi),
              reads=["b_bt", "b_bthi"], writes=["b_btlo"])
            for tau in range(8):
                mm(pm[:, tau * 8:(tau + 1) * 8], b_bthi[:, tau * 128:(tau + 1) * 128],
                   c_cmat, tau == 0, False, ["b_bthi", "c_cmat"], [PM])
                mm(pm[:, tau * 8:(tau + 1) * 8], b_btlo[:, tau * 128:(tau + 1) * 128],
                   c_cmat, False, False, ["b_btlo", "c_cmat"], [PM])
                mm(pm[:, tau * 8:(tau + 1) * 8], c_ones[0:1, :], s_wmrow,
                   False, tau == 7, ["c_ones", "s_wmrow"], [PM])
            R("scalar", lambda e: e.activation(out=b_w, in_=pm[:, 0:64], func=AF.Exp),
              reads=[PM], writes=["b_w"])
            R("vector", lambda e: e.memset(s_Sin, 0.0), writes=["s_Sin"])
            for r in range(NCORES):
                tb = b_Tr[r % 2]
                tbn = f"b_Tr_{r % 2}"
                R("gpsimd" if r < 4 else "sync",
                  lambda e, rr=r, d=tb: e.dma_start(
                      out=d, in_=ap["T_all"][rr * C:(rr + 1) * C, :]),
                  reads=["d:Tall"], writes=[tbn],
                  queue="qg" if r < 4 else "qs")
                for sg_ in range(8):
                    R("vector", lambda e, g=sg_, rr=r, d=tb: e.scalar_tensor_tensor(
                        out=s_Sin[:, g * DV:(g + 1) * DV], in0=d[:, g * DV:(g + 1) * DV],
                        scalar=b_w[:, g * 8 + rr:g * 8 + rr + 1],
                        in1=s_Sin[:, g * DV:(g + 1) * DV],
                        op0=ALU.mult, op1=ALU.add),
                      reads=[tbn, "b_w", "s_Sin"], writes=["s_Sin"])



        def B0(i):
            for nm, key in [("xnT", "xnT_d"), ("o16", "o16_d"),
                            ("qgT", "qgT_d"), ("base", "base_d")]:
                d, dn = T(nm, i)
                R("sync", lambda e, cc=i, dd=d, k=key: e.dma_start(out=dd, in_=ap[k][cc]),
                  reads=["d:" + key], writes=[dn], queue="qs")

        def B1a(i):
            xnT, xnTn = T("xnT", i)
            sg, sgn = T("sg", i)
            for j in range(8):
                for nh in range(2):
                    mm(pgA[:, nh * 512:(nh + 1) * 512], xnT[:, j * C:(j + 1) * C],
                       w_g[:, j * 2 * D + nh * 512: j * 2 * D + (nh + 1) * 512],
                       j == 0, j == 7, [xnTn, f"w_gh{j // 4}"], [PGA])
            R("scalar", lambda e: e.activation(out=sg[:, 0:D], in_=pgA, func=AF.Silu),
              reads=[PGA], writes=[sgn])

        def B1b(i):
            xnT, xnTn = T("xnT", i)
            sg, sgn = T("sg", i)
            mid, midn = T("mid", i)
            for j in range(8):
                for nh in range(2):
                    mm(pgA[:, nh * 512:(nh + 1) * 512], xnT[:, j * C:(j + 1) * C],
                       w_g[:, j * 2 * D + D + nh * 512: j * 2 * D + D + (nh + 1) * 512],
                       j == 0, j == 7, [xnTn, f"w_gh{j // 4}"], [PGA])
            R("scalar", lambda e: e.activation(out=sg[:, D:2 * D], in_=pgA, func=AF.Silu),
              reads=[PGA], writes=[sgn])
            for j in range(8):
                mm(pm[:, 0:512], xnT[:, j * C:(j + 1) * C],
                   w_alay[:, j * 512:(j + 1) * 512], j == 0, j == 7,
                   [xnTn, "w_alay"], [PM])
            R("vector", lambda e: e.tensor_copy(mid, pm[:, 0:512]), reads=[PM], writes=[midn])

        def B1d(i):
            o16, o16n = T("o16", i)
            qgT, qgTn = T("qgT", i)
            ob, obn = T("ob", i)
            for rnd in range(2):
                for hi in range(2):
                    h = 2 * rnd + hi
                    for ji, j in enumerate((2 * h, 2 * h + 1)):
                        mm(pd[:, hi * DV:(hi + 1) * DV], qgT[:, j * C:(j + 1) * C],
                           s_Sin[:, j * DV:(j + 1) * DV], ji == 0, ji == 1,
                           [qgTn, "s_Sin"], [PD])
                for hi in range(2):
                    h = 2 * rnd + hi
                    R("vector", lambda e, h2=h, i2=hi: e.tensor_tensor(
                        ob[:, h2 * DV:(h2 + 1) * DV], o16[:, h2 * DV:(h2 + 1) * DV],
                        pd[:, i2 * DV:(i2 + 1) * DV], ALU.add),
                      reads=[o16n, PD], writes=[obn])

        def B1c(i):
            ob, obn = T("ob", i)
            rs, rsn = T("rs", i)
            sd, sdn = T("sd", i)
            rst, rstn = T("rst", i)
            junk, junkn = T("junk", i)
            sg, sgn = T("sg", i)
            t1, t1n = ob, obn
            for h in range(4):
                R("scalar", lambda e, hh=h: e.activation(
                    out=junk, in_=ob[:, hh * DV:(hh + 1) * DV], func=AF.Square,
                    accum_out=rs[:, hh:hh + 1]),
                  reads=[obn], writes=[junkn, rsn])
            R("vector", lambda e: e.tensor_scalar(out=rs, in0=rs, scalar1=1.0 / DV,
                                                  scalar2=EPS, op0=ALU.mult, op1=ALU.add),
              reads=[rsn], writes=[rsn])
            R("vector", lambda e: e.tensor_scalar(
                out=rst[:].bitcast(I32), in0=rs[:].bitcast(I32), scalar1=1,
                scalar2=None, op0=ALU.logical_shift_right),
              reads=[rsn], writes=[rstn])
            R("vector", lambda e: e.tensor_scalar(
                out=rst[:].bitcast(I32), in0=rst[:].bitcast(I32), scalar1=-1,
                scalar2=0x5f3759df, op0=ALU.mult, op1=ALU.add),
              reads=[rstn], writes=[rstn])
            for _ in range(3):
                R("vector", lambda e: e.tensor_mul(sd, rst, rst), reads=[rstn], writes=[sdn])
                R("vector", lambda e: e.tensor_mul(sd, sd, rs), reads=[sdn, rsn], writes=[sdn])
                R("vector", lambda e: e.tensor_scalar(out=sd, in0=sd, scalar1=-0.5,
                                                      scalar2=1.5, op0=ALU.mult, op1=ALU.add),
                  reads=[sdn], writes=[sdn])
                R("vector", lambda e: e.tensor_mul(rst, rst, sd), reads=[rstn, sdn], writes=[rstn])
            R("vector", lambda e: e.tensor_mul(t1, ob, sg),
              reads=[obn, sgn], writes=[t1n])
            for h in range(4):
                R("vector", lambda e, hh=h: e.tensor_scalar(
                    out=t1[:, hh * DV:(hh + 1) * DV], in0=t1[:, hh * DV:(hh + 1) * DV],
                    scalar1=rst[:, hh:hh + 1], scalar2=None, op0=ALU.mult),
                  reads=[t1n, rstn], writes=[t1n])

        def B2mid(i):
            mid, midn = T("mid", i)
            midT, midTn = T("midT", i)
            for j in range(4):
                tr(ptbB[:, j * C:(j + 1) * C], mid[:, j * C:(j + 1) * C], c_identb,
                   [midn, "c_identb"], [PTBB])
            R("scalar", lambda e: e.copy(midT, ptbB[:, 0:512]), reads=[PTBB], writes=[midTn])

        def B2og(i, half):
            t1, t1n = T("ob", i)
            ogT, ogTn = T("ogT", i)
            for j in range(8):
                tr(ptbB[:, j * C:(j + 1) * C],
                   t1[:, half * D + j * C: half * D + (j + 1) * C],
                   c_identb, [t1n, "c_identb"], [PTBB])
            R("scalar", lambda e, h=half: e.copy(ogT[:, h * D:(h + 1) * D], ptbB),
              reads=[PTBB], writes=[ogTn])

        def B2wo(i):
            ogT, ogTn = T("ogT", i)
            midT, midTn = T("midT", i)
            base, basen = T("base", i)
            for nh in range(2):
                sl = slice(nh * 512, (nh + 1) * 512)
                for k in range(16):
                    mm(pwo[:, sl], ogT[:, k * C:(k + 1) * C],
                       w_wo[:, k * D + nh * 512: k * D + (nh + 1) * 512],
                       k == 0, False, [ogTn, f"w_wo{k}"], [PWO])
                for j in range(4):
                    mm(pwo[:, sl], midT[:, j * C:(j + 1) * C],
                       w_bwT[:, j * D + nh * 512: j * D + (nh + 1) * 512],
                       False, j == 3, [midTn, "w_bwT"], [PWO])
            R("vector", lambda e: e.tensor_add(base, base, pwo),
              reads=[basen, PWO], writes=[basen])
            R("sync", lambda e, cc=i: e.dma_start(
                out=ap["y"][cc * C:(cc + 1) * C, :], in_=base),
              reads=[basen], writes=["d:y"], queue="qs")

        B0(0)
        B0(1)
        emit_gates_a()
        emit_combine()
        for i in range(NCH + 3):
            if 0 <= i + 2 < NCH:
                B0(i + 2)
            if 0 <= i - 2 < NCH:
                B2mid(i - 2)
            if 0 <= i < NCH:
                B1a(i)
            if i == 0:
                emit_gates_b()
            if 0 <= i - 2 < NCH:
                B2og(i - 2, 0)
            if 0 <= i < NCH:
                B1b(i)
            if i == 0:
                emit_gates_c()
            if 0 <= i - 2 < NCH:
                B2og(i - 2, 1)
            if 0 <= i - 1 < NCH:
                B1d(i - 1)
            if 0 <= i - 2 < NCH:
                B2wo(i - 2)
            if 0 <= i - 1 < NCH:
                B1c(i - 1)

        p.barrier()
        block = es.enter_context(nc.Block())
        p.finalize(es, block)
    return nc


def _prep_shared(inputs):
    con = _tile_consts()
    w = {}
    w["wq"] = np.ascontiguousarray(inputs["L0_attn_Wq"], np.float32)
    w["wk"] = np.ascontiguousarray(inputs["L0_attn_Wk"], np.float32)
    w["wv"] = np.ascontiguousarray(inputs["L0_attn_Wv"], np.float32)
    w["wgk1"] = np.ascontiguousarray(inputs["L0_attn_Wgk1"], np.float32)
    w["wgk2"] = np.ascontiguousarray(inputs["L0_attn_Wgk2"], np.float32)
    w["bgk2"] = np.ascontiguousarray(inputs["L0_attn_bgk2"], np.float32).reshape(1, D)
    w["wg"] = np.ascontiguousarray(inputs["L0_attn_Wg"], np.float32)
    w["wo"] = np.ascontiguousarray(inputs["L0_attn_Wo"], np.float32)
    w["alay"] = np.ascontiguousarray(
        np.asarray(inputs["L0_ffn_A"], np.float32).reshape(512, D).T)
    w["bwT"] = np.ascontiguousarray(
        np.asarray(inputs["L0_ffn_Bw"], np.float32).transpose(0, 2, 1).reshape(512, D))
    w["wrT"] = np.ascontiguousarray(np.asarray(inputs["L0_ffn_Wr"], np.float32).T)
    w["gnw"] = np.ascontiguousarray(
        np.asarray(inputs["L0_attn_g_norm_weight"], np.float32).reshape(4, 128).T)
    x = np.asarray(inputs["L0_attn_x"], np.float32).reshape(NCORES * TOK, D)
    w["x0"] = np.ascontiguousarray(x[0:1, :])
    return con, w, x


def kernel(**inputs):
    con, w, x = _prep_shared(inputs)

    nc_a = build_phase_a()
    in_a = []
    for cidx in range(NCORES):
        m = dict(
            x=np.ascontiguousarray(x[cidx * TOK:(cidx + 1) * TOK]),
            wq=w["wq"], wk=w["wk"], wv=w["wv"], wgk1=w["wgk1"], wgk2=w["wgk2"],
            bgk2=w["bgk2"],
            ctrib=con["trib"], ctrir=con["trir"], cidentb=con["identb"],
            conesb=con["onesb"], cmask=con["mask"], cidentf=con["identf"],
            cln16=np.full((C, 1), -LN16, np.float32),
            wg=w["wg"], wo=w["wo"], alay=w["alay"], bwT=w["bwT"],
        )
        in_a.append(m)
    res_a = run_bass_kernel_spmd(nc_a, in_a, core_ids=list(range(NCORES)))
    ra = res_a.results

    T_all = np.concatenate([np.asarray(ra[cidx]["T_d"]) for cidx in range(NCORES)], axis=0)
    btot_all = np.zeros((8, D), np.float32)
    for cidx in range(NCORES):
        btot_all[cidx] = np.asarray(ra[cidx]["btot_d"])[0]

    nc_b = build_phase_b()
    in_b = []
    for cidx in range(NCORES):
        m = dict(
            o16_d=np.asarray(ra[cidx]["o16_d"]),
            qgT_d=np.asarray(ra[cidx]["qgT_d"]),
            base_d=np.asarray(ra[cidx]["base_d"]),
            xnT_d=np.asarray(ra[cidx]["xnT_d"]),
            T_all=T_all, btot_all=btot_all,
            x0=w["x0"], wg=np.asarray(ra[cidx]["wgb"]), wo=np.asarray(ra[cidx]["wob"]),
            alay=np.asarray(ra[cidx]["alayb"]), bwT=np.asarray(ra[cidx]["bwTb"]),
            wrT=w["wrT"], gnw=w["gnw"],
            cmatx=_cmat(cidx),
            wmrow=np.where(np.arange(8) >= cidx, -1e4, 0.0).astype(np.float32)
                    .astype(BF).reshape(1, 8),
            conesb=con["onesb"],
            cidentb=con["identb"], cidentf=con["identf"], cesel=con["esel"],
        )
        in_b.append(m)
    res_b = run_bass_kernel_spmd(nc_b, in_b, core_ids=list(range(NCORES)))
    y = np.concatenate([np.asarray(res_b.results[cidx]["y"]) for cidx in range(NCORES)],
                       axis=0)
    return y.reshape(1, NCORES * TOK, D)


# version 11
# speedup vs baseline: 1.0542x; 1.0259x over previous
import numpy as np
import ml_dtypes
from contextlib import ExitStack

import concourse.bass as bass
import concourse.mybir as mybir
from concourse.bass_utils import run_bass_kernel_spmd

F32 = mybir.dt.float32
BF16 = mybir.dt.bfloat16
F16 = mybir.dt.float16
AF = mybir.ActivationFunctionType
ALU = mybir.AluOpType
AX = mybir.AxisListType
I32 = mybir.dt.int32
BF = ml_dtypes.bfloat16

NCORES = 8
TOK = 2048
D = 1024
C = 128
NCH = TOK // C
HEADS = 4
DK = 256
DV = 512
EPS = 1e-5
LN16 = float(np.log(16.0))

ENGS = ("tensor", "vector", "scalar", "gpsimd", "sync")


class Prog:

    NSLOT = 8
    NSLOT_MAP = {}

    def _nslot(self, q):
        return self.NSLOT_MAP.get(q, self.NSLOT)

    def __init__(self, nc):
        self.nc = nc
        self.ops = {e: [] for e in ENGS}
        self.nops = {e: 0 for e in ENGS}
        self.qcount = {}
        self.last_write = {}
        self.readers = {}
        self.last_psum = {}
        self.waited = {e: {} for e in ENGS}
        self.need_inc = {e: set() for e in ENGS}

    def _src_key(self, tok):
        return ("E", tok[1]) if tok[0] == "E" else ("Q",) + tuple(tok[1])

    def rec(self, eng, emit, reads=(), writes=(), queue=None, tag=""):
        deps = {}

        def add(tok):
            k = self._src_key(tok)
            v = tok[2]
            if k not in deps or deps[k] < v:
                deps[k] = v

        for r in reads:
            t = self.last_write.get(r)
            if t is not None:
                add(t)
            if r.startswith("ps_"):
                t2 = self.last_psum.get(r)
                if t2 is not None:
                    add(t2)
        for w in writes:
            t = self.last_write.get(w)
            if t is not None:
                add(t)
            for kk, vv in self.readers.get(w, {}).items():
                add(("E", kk[1], vv) if kk[0] == "E" else ("Q", (kk[1], kk[2]), vv))
            if w.startswith("ps_"):
                t2 = self.last_psum.get(w)
                if t2 is not None:
                    add(t2)

        if queue is not None:
            idx = None
            n = self.qcount.get(queue, 0) + 1
            self.qcount[queue] = n
            ns = self._nslot(queue)
            slot = (n - 1) % ns
            gen = (n - 1) // ns + 1
            my = ("Q", (queue, slot), gen)
            if gen > 1:
                add(("Q", (queue, slot), gen - 1))
        else:
            self.nops[eng] += 1
            idx = self.nops[eng]
            my = ("E", eng, idx)
        myk = self._src_key(my)

        waits = []
        for k, v in deps.items():
            if k[0] == "E" and k[1] == "tensor" and eng == "tensor":
                continue
            got = self.waited[eng].get(k, 0)
            if v <= got:
                continue
            waits.append((k, v))
            self.waited[eng][k] = v
            if k[0] == "E":
                self.need_inc[k[1]].add(v)

        for r in reads:
            self.readers.setdefault(r, {})[myk] = max(
                self.readers.get(r, {}).get(myk, 0), my[2]
            )
            if r.startswith("ps_"):
                self.last_psum[r] = my
        for w in writes:
            self.last_write[w] = my
            self.readers[w] = {}
            if w.startswith("ps_"):
                self.last_psum[w] = my

        self.ops[eng].append(
            dict(emit=emit, waits=waits, queue=queue, idx=idx, tag=tag)
        )
        return my

    def _queue_toks(self):
        toks = []
        for q, n in self.qcount.items():
            ns = self._nslot(q)
            for slot in range(min(n, ns)):
                gen = (n - 1 - slot) // ns + 1
                toks.append(("Q", (q, slot), gen))
        return toks

    def barrier(self):
        toks = []
        for e in ENGS:
            if self.nops[e]:
                toks.append(("E", e, self.nops[e]))
        toks += self._queue_toks()
        for e in ENGS:
            for t in toks:
                k = self._src_key(t)
                if k == ("E", e):
                    continue
                v = t[2]
                if self.waited[e].get(k, 0) < v:
                    self.waited[e][k] = v
                    if k[0] == "E":
                        self.need_inc[k[1]].add(v)
                    self.ops[e].append(dict(emit=None, waits=[(k, v)], queue=None,
                                            idx=None, tag="barrier"))

    def finalize(self, es, block):
        sems = {}
        for e in ENGS:
            sems[("E", e)] = es.enter_context(self.nc.semaphore(f"sem_{e}"))
        for q in self.qcount:
            for slot in range(self._nslot(q)):
                sems[("Q", q, slot)] = es.enter_context(
                    self.nc.semaphore(f"sem_{q}_{slot}"))
        inc_prefix = {}
        for e in ENGS:
            pref = {}
            cnt = 0
            marks = self.need_inc[e]
            for i in range(1, self.nops[e] + 1):
                if i in marks:
                    cnt += 1
                pref[i] = cnt
            inc_prefix[e] = pref

        def val_of(k, v):
            if k[0] == "Q":
                return 16 * v
            return inc_prefix[k[1]].get(v, 0)

        qseen = {}

        def body(e):
            def run(eng):
                for op in self.ops[e]:
                    for (k, v) in op["waits"]:
                        eng.wait_ge(sems[k], val_of(k, v))
                    if op["emit"] is None:
                        continue
                    inst = op["emit"](eng)
                    if op["queue"] is not None:
                        q = op["queue"]
                        n = qseen.get(q, 0)
                        qseen[q] = n + 1
                        inst.then_inc(sems[("Q", q, n % self.NSLOT)], 16)
                    elif op["idx"] in self.need_inc[e]:
                        inst.then_inc(sems[("E", e)], 1)
            return run

        for e in ENGS:
            if self.ops[e]:
                getattr(block, e)(body(e))


def _tile_consts():
    i = np.arange(C)
    trib = ((i[:, None] <= i[None, :]).astype(np.float32) / 16.0).astype(BF)
    trir = ((i[:, None] > i[None, :]).astype(np.float32) / 16.0).astype(BF)
    mask = np.tile((i[:, None] <= i[None, :]).astype(np.float32), (1, 4))
    identb = np.eye(C, dtype=np.float32).astype(BF)
    identf = np.eye(C, dtype=np.float32)
    onesb = np.ones((C, C), dtype=np.float32).astype(BF)
    esel = np.zeros((8, 512), dtype=np.float32)
    for p in range(512):
        esel[p // 64, p] = 1.0
    return dict(trib=trib, trir=trir, mask=mask, identb=identb, identf=identf,
                onesb=onesb, esel=esel.astype(BF))


def _cmat(core):
    m_ = np.zeros((8, 8), dtype=np.float32)
    for r in range(8):
        for m in range(8):
            if r < m < core:
                m_[m, r] = 1.0
    for r in range(8):
        if r >= core:
            m_[:, r] = 0.0
    return m_.astype(BF)


def _wmask(core):
    m = np.zeros((8,), np.float32)
    m[:core] = 1.0
    return m



def build_phase_a():
    nc = bass.Bass("TRN2", target_bir_lowering=False, debug=False)
    ap = {}
    ap["x"] = nc.dram_tensor("x", [TOK, D], F32, kind="ExternalInput").ap()
    for n, shp in [("wq", [D, D]), ("wk", [D, D]), ("wv", [D, 2 * D]),
                   ("wgk1", [D, 16]), ("wgk2", [16, D]), ("bgk2", [1, D])]:
        ap[n] = nc.dram_tensor(n, shp, F32, kind="ExternalInput").ap()
    for n, shp in [("ctrib", [C, C]), ("ctrir", [C, C]), ("cidentb", [C, C]),
                   ("conesb", [C, C])]:
        ap[n] = nc.dram_tensor(n, shp, BF16, kind="ExternalInput").ap()
    ap["cmask"] = nc.dram_tensor("cmask", [C, 4 * C], F32, kind="ExternalInput").ap()
    ap["cidentf"] = nc.dram_tensor("cidentf", [C, C], F32, kind="ExternalInput").ap()
    ap["cln16"] = nc.dram_tensor("cln16", [C, 1], F32, kind="ExternalInput").ap()
    ap["o16_d"] = nc.dram_tensor("o16_d", [NCH, C, 2 * D], F16, kind="ExternalOutput").ap()
    ap["qgT_d"] = nc.dram_tensor("qgT_d", [NCH, C, D], BF16, kind="ExternalOutput").ap()
    ap["base_d"] = nc.dram_tensor("base_d", [NCH, C, D], F32, kind="ExternalOutput").ap()
    ap["xnT_d"] = nc.dram_tensor("xnT_d", [NCH, C, D], BF16, kind="ExternalOutput").ap()
    ap["T_d"] = nc.dram_tensor("T_d", [C, 8 * DV], BF16, kind="ExternalOutput").ap()
    ap["btot_d"] = nc.dram_tensor("btot_d", [1, D], F32, kind="ExternalOutput").ap()
    for n, shp in [("wg", [D, 2 * D]), ("wo", [2 * D, D]),
                   ("alay", [D, 512]), ("bwT", [512, D])]:
        ap[n] = nc.dram_tensor(n, shp, F32, kind="ExternalInput").ap()
        ap[n + "b"] = nc.dram_tensor(n + "b", shp, BF16, kind="ExternalOutput").ap()

    es = ExitStack()
    with es:
        def sb(n, s, t):
            return es.enter_context(nc.sbuf_tensor(n, s, t))[:]

        def psm(n, s, t):
            return es.enter_context(nc.psum_tensor(n, s, t))[:]

        w_q = sb("w_q", [128, 8 * D], BF16)
        w_k = sb("w_k", [128, 8 * D], BF16)
        w_v = sb("w_v", [128, 8 * 2 * D], BF16)
        w_gk1 = sb("w_gk1", [128, 8 * 16], BF16)
        w_gk2 = sb("w_gk2", [16, D], BF16)
        s_bgk2 = sb("s_bgk2", [1, D], BF16)
        c_trib = sb("c_trib", [C, C], BF16)
        c_trir = sb("c_trir", [C, C], BF16)
        c_identb = sb("c_identb", [C, C], BF16)
        c_ones = sb("c_ones", [C, C], BF16)
        c_mask4 = sb("c_mask4", [C, 4 * C], F32)
        c_ln16 = sb("c_ln16", [C, 1], F32)
        s_Sb = sb("s_Sb", [128, 8 * DV], BF16)
        s_bpreT = sb("s_bpreT", [C, 8], F32)
        s_btrow = sb("s_btrow", [1, D], F32)
        c_identf = sb("c_identf", [C, C], F32)

        spec = {
            "x": ([C, D], F32, 2), "st": ([C, 8], F32, 1), "xn": ([C, D], F32, 1),
            "xnb": ([C, D], BF16, 2), "xnT": ([C, D], BF16, 3), "f1": ([C, D], F32, 1),
            "ghi": ([C, D], BF16, 2), "glo": ([C, D], BF16, 2),
            "z1e": ([C, 16], BF16, 1), "z1b": ([16, C], BF16, 1),
            "lam": ([C, D], BF16, 2), "lami": ([C, D], BF16, 2),
            "lrev": ([C, D], BF16, 2),
            "qh": ([C, D], BF16, 2),
            "bnd": ([C, 8], F32, 2), "expbp": ([C, 8], F32, 4),
            "kt": ([C, D], BF16, 2), "kh": ([C, D], BF16, 3),
            "v": ([C, 2 * D], BF16, 3),
            "qhT": ([C, D], BF16, 3), "ktT": ([C, D], BF16, 2), "qgT": ([C, D], BF16, 2),
            "pt": ([C, 4 * C], BF16, 2), "o16": ([C, 2 * D], F16, 2),
            "dc": ([C, 8], F32, 5),
        }
        tiles = {}
        for nm, (shp, dt2, nb) in spec.items():
            tiles[nm] = [sb(f"t_{nm}_{b}", shp, dt2) for b in range(nb)]

        def T(nm, i):
            nb = spec[nm][2]
            return tiles[nm][i % nb], f"t_{nm}_{i % nb}"

        ptb1 = psm("ptb1", [128, 1024], BF16)
        ptb4 = psm("ptb4", [128, 1024], BF16)
        pc = psm("pc", [128, 1024], F32)
        pp = psm("pp", [128, 1024], F32)
        psc = psm("psc", [128, 512], F32)
        psc2 = psm("psc2", [128, 512], F32)
        PTB1, PTB4, PC, PP, PSC, PSC2 = ("ps_ptb1", "ps_ptb4", "ps_pc", "ps_pp",
                                         "ps_psc", "ps_psc2")

        p = Prog(nc)
        R = p.rec

        def ld(engq, dst, src, rd, wr):
            R(engq[0], lambda e, d=dst, s=src: e.dma_start(out=d, in_=s),
              reads=rd, writes=wr, queue=engq[1])

        gq = ("gpsimd", "qg")
        sq = ("sync", "qs")
        for kt in range(8):
            ld(gq, w_q[:, kt * D:(kt + 1) * D],
               ap["wq"][kt * 128:(kt + 1) * 128, :], ["d:wq"], ["w_q"])
            ld(gq, w_k[:, kt * D:(kt + 1) * D],
               ap["wk"][kt * 128:(kt + 1) * 128, :], ["d:wk"], ["w_k"])
            ld(gq, w_v[:, kt * 2 * D:(kt + 1) * 2 * D],
               ap["wv"][kt * 128:(kt + 1) * 128, :], ["d:wv"], ["w_v"])
        ld(gq, w_gk1.rearrange("p (kt n) -> p kt n", kt=8),
           ap["wgk1"].rearrange("(kt p) n -> p kt n", p=128), ["d:wgk1"], ["w_gk1"])
        ld(gq, w_gk2, ap["wgk2"], ["d:wgk2"], ["w_gk2"])
        ld(gq, s_bgk2, ap["bgk2"], ["d:bgk2"], ["s_bgk2"])
        ld(gq, c_trib, ap["ctrib"], ["d:ctrib"], ["c_trib"])
        ld(gq, c_trir, ap["ctrir"], ["d:ctrir"], ["c_trir"])
        ld(gq, c_identb, ap["cidentb"], ["d:cidentb"], ["c_identb"])
        ld(gq, c_ones, ap["conesb"], ["d:conesb"], ["c_ones"])
        ld(sq, c_mask4, ap["cmask"], ["d:cmask"], ["c_mask4"])
        ld(sq, c_identf, ap["cidentf"], ["d:cidentf"], ["c_identf"])
        ld(sq, c_ln16, ap["cln16"], ["d:cln16"], ["c_ln16"])
        for n in ("wg", "wo", "alay", "bwT"):
            ld(gq, ap[n + "b"], ap[n], ["d:" + n], ["d:" + n + "b"])
        R("vector", lambda e: e.memset(s_Sb, 0.0), writes=["s_Sb"])
        R("vector", lambda e: e.memset(s_bpreT, 0.0), writes=["s_bpreT"])

        def mm(out, lhsT, rhs, start, stop, rd, wr):
            R("tensor", lambda e, o=out, l=lhsT, r=rhs, s1=start, s2=stop:
              e.matmul(o, l, r, start=s1, stop=s2), reads=rd, writes=wr)

        def tr(out, in_, ident, rd, wr):
            R("tensor", lambda e, o=out, i=in_, d=ident: e.transpose(o, i, d),
              reads=rd, writes=wr)

        def S0(i):
            d, dn = T("x", i)
            R("sync", lambda e, dd=d, sl=ap["x"][i * C:(i + 1) * C, :]:
              e.dma_start(out=dd, in_=sl), reads=["d:x"], writes=[dn], queue="qs")

        def S1a_e(i):
            x, xn_ = T("x", i)
            st, stn = T("st", i)
            xn, xnn = T("xn", i)
            xnb, xnbn = T("xnb", i)
            xnT, xnTn = T("xnT", i)
            f1, f1n = T("f1", i)
            R("vector", lambda e: e.reduce_sum(st[:, 0:1], x, axis=AX.X),
              reads=[xn_], writes=[stn])
            R("scalar", lambda e: e.activation(out=f1, in_=x, func=AF.Square,
                                               accum_out=st[:, 1:2]),
              reads=[xn_], writes=[f1n, stn])
            R("vector", lambda e: e.tensor_scalar(out=st[:, 2:3], in0=st[:, 0:1],
                                                  scalar1=1.0 / D, scalar2=None, op0=ALU.mult),
              reads=[stn], writes=[stn])
            R("vector", lambda e: e.tensor_scalar(out=st[:, 3:4], in0=st[:, 1:2],
                                                  scalar1=1.0 / D, scalar2=None, op0=ALU.mult),
              reads=[stn], writes=[stn])
            R("vector", lambda e: e.tensor_mul(st[:, 4:5], st[:, 2:3], st[:, 2:3]),
              reads=[stn], writes=[stn])
            R("vector", lambda e: e.tensor_sub(st[:, 4:5], st[:, 3:4], st[:, 4:5]),
              reads=[stn], writes=[stn])
            R("vector", lambda e: e.tensor_scalar(out=st[:, 4:5], in0=st[:, 4:5],
                                                  scalar1=1.0, scalar2=EPS, op0=ALU.mult,
                                                  op1=ALU.add),
              reads=[stn], writes=[stn])
            R("scalar", lambda e: e.activation(out=st[:, 5:6], in_=st[:, 4:5], func=AF.Ln),
              reads=[stn], writes=[stn])
            R("scalar", lambda e: e.activation(out=st[:, 6:7], in_=st[:, 5:6], func=AF.Exp,
                                               scale=-0.5),
              reads=[stn], writes=[stn])
            R("vector", lambda e: e.tensor_scalar(out=xn, in0=x, scalar1=st[:, 2:3],
                                                  scalar2=st[:, 6:7], op0=ALU.subtract,
                                                  op1=ALU.mult),
              reads=[xn_, stn], writes=[xnn])
            R("scalar", lambda e: e.copy(xnb, xn), reads=[xnn], writes=[xnbn])
            R("gpsimd", lambda e: e.tensor_add(x, x, xn), reads=[xn_, xnn], writes=[xn_])
            R("sync", lambda e, cc=i: e.dma_start(out=ap["base_d"][cc], in_=x),
              reads=[xn_], writes=["d:base"], queue="qs")

        def S1a_t(i):
            xnb, xnbn = T("xnb", i)
            xnT, xnTn = T("xnT", i)
            for j in range(8):
                tr(ptb1[:, j * C:(j + 1) * C], xnb[:, j * C:(j + 1) * C], c_identb,
                   [xnbn, "c_identb"], [PTB1])
            R("scalar", lambda e: e.copy(xnT, ptb1), reads=[PTB1], writes=[xnTn])
            R("scalar", lambda e, cc=i: e.dma_start(out=ap["xnT_d"][cc], in_=xnT),
              reads=[xnTn], writes=["d:xnT"], queue="qa")

        def S1b(i):
            xnT, xnTn = T("xnT", i)
            xn, xnn = T("xn", i)
            f1, f1n = T("f1", i)
            ghi, ghin = T("ghi", i)
            glo, glon = T("glo", i)
            z1e, z1en = T("z1e", i)
            z1b, z1bn = T("z1b", i)
            for j in range(8):
                mm(pc[:, 0:16], xnT[:, j * C:(j + 1) * C],
                   w_gk1[:, j * 16:(j + 1) * 16], j == 0, j == 7, [xnTn, "w_gk1"], [PC])
            R("vector", lambda e: e.tensor_copy(z1e, pc[:, 0:16]), reads=[PC], writes=[z1en])
            tr(ptb1[0:16, 0:C], z1e, c_identb, [z1en, "c_identb"], [PTB1])
            R("scalar", lambda e: e.copy(z1b, ptb1[0:16, 0:C]), reads=[PTB1], writes=[z1bn])
            for nh in range(2):
                sl = slice(nh * 512, (nh + 1) * 512)
                mm(pc[:, sl], z1b, w_gk2[:, sl], True, False, [z1bn, "w_gk2"], [PC])
                mm(pc[:, sl], c_ones[0:1, :], s_bgk2[:, sl], False, True,
                   ["c_ones", "s_bgk2"], [PC])
            R("scalar", lambda e: e.activation(out=f1, in_=pc, func=AF.Abs),
              reads=[PC], writes=[f1n])
            R("scalar", lambda e: e.activation(out=f1, in_=f1, func=AF.Exp, scale=-1.0),
              reads=[f1n], writes=[f1n])
            R("scalar", lambda e: e.activation(out=f1, in_=f1, func=AF.Ln, bias=1.0),
              reads=[f1n], writes=[f1n])
            R("vector", lambda e: e.tensor_scalar(out=xn, in0=pc, scalar1=0.0,
                                                  scalar2=None, op0=ALU.min),
              reads=[PC], writes=[xnn])
            R("vector", lambda e: e.tensor_sub(xn, xn, f1), reads=[xnn, f1n], writes=[xnn])
            R("scalar", lambda e: e.copy(ghi, xn), reads=[xnn], writes=[ghin])
            R("vector", lambda e: e.tensor_sub(glo, xn, ghi), reads=[xnn, ghin], writes=[glon])

        def S2a(i):
            ghi, ghin = T("ghi", i)
            glo, glon = T("glo", i)
            lam, lamn = T("lam", i)
            lami, lamin = T("lami", i)
            for nh in range(2):
                sl = slice(nh * 512, (nh + 1) * 512)
                mm(pc[:, sl], c_trib, ghi[:, sl], True, False, ["c_trib", ghin], [PC])
                mm(pc[:, sl], c_trib, glo[:, sl], False, True, ["c_trib", glon], [PC])
            R("scalar", lambda e: e.activation(out=lam, in_=pc, func=AF.Exp, bias=c_ln16),
              reads=[PC, "c_ln16"], writes=[lamn])
            R("scalar", lambda e: e.activation(out=lami, in_=pc, func=AF.Exp, scale=-1.0),
              reads=[PC], writes=[lamin])

        def S2b(i):
            ghi, ghin = T("ghi", i)
            glo, glon = T("glo", i)
            lrev, lrevn = T("lrev", i)
            for nh in range(2):
                sl = slice(nh * 512, (nh + 1) * 512)
                mm(pp[:, sl], c_trir, ghi[:, sl], True, False, ["c_trir", ghin], [PP])
                mm(pp[:, sl], c_trir, glo[:, sl], False, True, ["c_trir", glon], [PP])
            R("scalar", lambda e: e.activation(out=lrev, in_=pp, func=AF.Exp),
              reads=[PP], writes=[lrevn])

        def S2c(i):
            ghi, ghin = T("ghi", i)
            glo, glon = T("glo", i)
            dc, dcn = T("dc", i)
            bnd, bndn = T("bnd", i)
            expbp, expbpn = T("expbp", i)
            for j in range(8):
                mm(pc[:, j:j + 1], ghi[:, j * C:(j + 1) * C], c_ones[:, 0:1],
                   j == 0, False, [ghin, "c_ones"], [PC])
                mm(pc[:, j:j + 1], glo[:, j * C:(j + 1) * C], c_ones[:, 0:1],
                   False, j == 7, [glon, "c_ones"], [PC])
            R("scalar", lambda e: e.activation(out=dc, in_=pc[:, 0:8], func=AF.Exp),
              reads=[PC], writes=[dcn])
            R("vector", lambda e: e.tensor_copy(bnd, pc[:, 0:8]), reads=[PC], writes=[bndn])
            R("scalar", lambda e: e.activation(out=expbp, in_=s_bpreT, func=AF.Exp),
              reads=["s_bpreT"], writes=[expbpn])
            R("vector", lambda e: e.tensor_add(s_bpreT, s_bpreT, bnd),
              reads=["s_bpreT", bndn], writes=["s_bpreT"])

        def S3q(i):
            xnT, xnTn = T("xnT", i)
            lam, lamn = T("lam", i)
            qh, qhn = T("qh", i)
            for j in range(8):
                for nh in range(2):
                    sl = slice(nh * 512, (nh + 1) * 512)
                    mm(pp[:, sl], xnT[:, j * C:(j + 1) * C],
                       w_q[:, j * D + nh * 512: j * D + (nh + 1) * 512],
                       j == 0, j == 7, [xnTn, "w_q"], [PP])
            R("vector", lambda e: e.tensor_mul(qh, pp, lam), reads=[PP, lamn], writes=[qhn])

        def S3k(i):
            xnT, xnTn = T("xnT", i)
            lami, lamin = T("lami", i)
            lrev, lrevn = T("lrev", i)
            kt, ktn = T("kt", i)
            kh, khn = T("kh", i)
            for j in range(8):
                for nh in range(2):
                    sl = slice(nh * 512, (nh + 1) * 512)
                    mm(pp[:, sl], xnT[:, j * C:(j + 1) * C],
                       w_k[:, j * D + nh * 512: j * D + (nh + 1) * 512],
                       j == 0, j == 7, [xnTn, "w_k"], [PP])
            R("vector", lambda e: e.tensor_mul(kt, pp, lami), reads=[PP, lamin], writes=[ktn])
            R("vector", lambda e: e.tensor_mul(kh, pp, lrev), reads=[PP, lrevn], writes=[khn])

        def S3v(i, half):
            xnT, xnTn = T("xnT", i)
            v, vn = T("v", i)
            for j in range(8):
                for nh in range(2):
                    mm(pp[:, nh * 512:(nh + 1) * 512], xnT[:, j * C:(j + 1) * C],
                       w_v[:, j * 2 * D + half * D + nh * 512:
                           j * 2 * D + half * D + (nh + 1) * 512],
                       j == 0, j == 7, [xnTn, "w_v"], [PP])
            R("scalar", lambda e, h=half: e.copy(v[:, h * D:(h + 1) * D], pp),
              reads=[PP], writes=[vn])

        def S4(i):
            for src, dst in (("qh", "qhT"), ("kt", "ktT")):
                sT, sn = T(src, i)
                dT, dn = T(dst, i)
                for j in range(8):
                    tr(ptb4[:, j * C:(j + 1) * C], sT[:, j * C:(j + 1) * C], c_identb,
                       [sn, "c_identb"], [PTB4])
                R("scalar", lambda e, d=dT, ps=ptb4: e.copy(d, ps), reads=[PTB4], writes=[dn])
            qhT, qhTn = T("qhT", i)
            qgT, qgTn = T("qgT", i)
            expbp, expbpn = T("expbp", i)
            for j in range(8):
                R("vector", lambda e, jj=j: e.tensor_scalar(
                    out=qgT[:, jj * C:(jj + 1) * C], in0=qhT[:, jj * C:(jj + 1) * C],
                    scalar1=expbp[:, jj:jj + 1], scalar2=None, op0=ALU.mult),
                  reads=[qhTn, expbpn], writes=[qgTn])
            R("scalar", lambda e, cc=i: e.dma_start(out=ap["qgT_d"][cc], in_=qgT),
              reads=[qgTn], writes=["d:qgT"], queue="qa")

        def S5a(i):
            qhT, qhTn = T("qhT", i)
            ktT, ktTn = T("ktT", i)
            pt, ptn = T("pt", i)
            for h in range(4):
                for ji, j in enumerate((2 * h, 2 * h + 1)):
                    mm(psc2[:, h * C:(h + 1) * C], ktT[:, j * C:(j + 1) * C],
                       qhT[:, j * C:(j + 1) * C], (h == 0 and ji == 0),
                       (h == 3 and ji == 1), [ktTn, qhTn], [PSC2])
            R("vector", lambda e: e.tensor_mul(pt, psc2, c_mask4),
              reads=[PSC2, "c_mask4"], writes=[ptn])

        def S5b(i, h):
            qhT, qhTn = T("qhT", i)
            kh, khn = T("kh", i)
            v, vn = T("v", i)
            pt, ptn = T("pt", i)
            o16, o16n = T("o16", i)
            dc, dcn = T("dc", i)
            mm(psc[:, 0:DV], pt[:, h * C:(h + 1) * C], v[:, h * DV:(h + 1) * DV],
               True, False, [ptn, vn], [PSC])
            for ji, j in enumerate((2 * h, 2 * h + 1)):
                mm(psc[:, 0:DV], qhT[:, j * C:(j + 1) * C], s_Sb[:, j * DV:(j + 1) * DV],
                   False, ji == 1, [qhTn, "s_Sb"], [PSC])
            R("scalar", lambda e: e.copy(o16[:, h * DV:(h + 1) * DV], psc[:, 0:DV]),
              reads=[PSC], writes=[o16n])
            for ih in range(2):
                sg = 2 * h + ih
                mm(psc[:, 0:DV], kh[:, h * DK + ih * C: h * DK + (ih + 1) * C],
                   v[:, h * DV:(h + 1) * DV], True, True, [khn, vn], [PSC])
                R("vector", lambda e, g=sg: e.scalar_tensor_tensor(
                    out=s_Sb[:, g * DV:(g + 1) * DV], in0=s_Sb[:, g * DV:(g + 1) * DV],
                    scalar=dc[:, g:g + 1], in1=psc[:, 0:DV],
                    op0=ALU.mult, op1=ALU.add),
                  reads=["s_Sb", dcn, PSC], writes=["s_Sb"])
            if h == 3:
                R("sync", lambda e, cc=i: e.dma_start(out=ap["o16_d"][cc], in_=o16),
                  reads=[o16n], writes=["d:o16"], queue="qs")

        S0(0)
        S1a_e(0)
        for i in range(NCH + 6):
            if 0 <= i + 1 < NCH:
                S0(i + 1)
            if 0 <= i - 2 < NCH:
                S3q(i - 2)
            if 0 <= i - 5 < NCH:
                S5b(i - 5, 0)
            if 0 <= i - 3 < NCH:
                S3k(i - 3)
            if 0 <= i - 5 < NCH:
                S5b(i - 5, 1)
            if 0 <= i - 3 < NCH:
                S3v(i - 3, 0)
            if 0 <= i - 5 < NCH:
                S5b(i - 5, 2)
            if 0 <= i - 3 < NCH:
                S3v(i - 3, 1)
            if 0 <= i - 5 < NCH:
                S5b(i - 5, 3)
            if 0 <= i + 1 < NCH:
                S1a_e(i + 1)
            if 0 <= i - 4 < NCH:
                S5a(i - 4)
            if 0 <= i - 1 < NCH:
                S2a(i - 1)
            if 0 <= i - 3 < NCH:
                S4(i - 3)
            if 0 <= i - 1 < NCH:
                S2b(i - 1)
            if 0 <= i - 1 < NCH:
                S2c(i - 1)
            if 0 <= i < NCH:
                S1a_t(i)
            if 0 <= i < NCH:
                S1b(i)

        R("sync", lambda e: e.dma_start(out=ap["T_d"], in_=s_Sb),
          reads=["s_Sb"], writes=["d:T"], queue="qs")
        for j in range(8):
            tr(pc[0:1, j * C:(j + 1) * C], s_bpreT[:, j:j + 1], c_identf,
               ["s_bpreT", "c_identf"], [PC])
        R("vector", lambda e: e.tensor_copy(s_btrow, pc[0:1, :]),
          reads=[PC], writes=["s_btrow"])
        R("sync", lambda e: e.dma_start(out=ap["btot_d"], in_=s_btrow),
          reads=["s_btrow"], writes=["d:btot"], queue="qs")
        p.barrier()

        block = es.enter_context(nc.Block())
        p.finalize(es, block)
    return nc


def build_phase_b():
    nc = bass.Bass("TRN2", target_bir_lowering=False, debug=False)
    ap = {}
    for n, shp, dt in [("o16_d", [NCH, C, 2 * D], F16), ("qgT_d", [NCH, C, D], BF16),
                       ("base_d", [NCH, C, D], F32), ("xnT_d", [NCH, C, D], BF16),
                       ("T_all", [NCORES * C, 8 * DV], BF16), ("btot_all", [8, D], F32),
                       ("x0", [1, D], F32),
                       ("wg", [D, 2 * D], BF16), ("wo", [2 * D, D], BF16),
                       ("alay", [D, 512], BF16), ("bwT", [512, D], BF16),
                       ("wrT", [D, 8], F32), ("gnw", [128, 4], F32),
                       ("cmatx", [8, 8], BF16), ("wmrow", [1, 8], BF16),
                       ("conesb", [C, C], BF16),
                       ("cidentb", [C, C], BF16), ("cidentf", [C, C], F32),
                       ("cesel", [8, 512], BF16)]:
        ap[n] = nc.dram_tensor(n, shp, dt, kind="ExternalInput").ap()
    ap["y"] = nc.dram_tensor("y", [TOK, D], F32, kind="ExternalOutput").ap()

    es = ExitStack()
    with es:
        def sb(n, s, t):
            return es.enter_context(nc.sbuf_tensor(n, s, t))[:]

        def psm(n, s, t):
            return es.enter_context(nc.psum_tensor(n, s, t))[:]

        w_g = sb("w_g", [128, 8 * 2 * D], BF16)
        w_wo = sb("w_wo", [128, 16 * D], BF16)
        w_alay = sb("w_alay", [128, 8 * 512], BF16)
        w_bwT = sb("w_bwT", [128, 4 * D], BF16)
        w_wrT = sb("w_wrT", [128, 8 * 8], F32)
        s_gnw = sb("s_gnw", [128, 4], F32)
        c_identb = sb("c_identb", [C, C], BF16)
        c_identf = sb("c_identf", [C, C], F32)
        c_esel = sb("c_esel", [8, 512], BF16)
        c_cmat = sb("c_cmat", [8, 8], BF16)
        c_ones = sb("c_ones", [C, C], BF16)
        s_wmrow = sb("s_wmrow", [1, 8], BF16)
        s_x0 = sb("s_x0", [1, D], F32)
        s_x0b = sb("s_x0b", [1, D], F32)
        g_st = sb("g_st", [1, 8], F32)
        g_junk = sb("g_junk", [1, D], F32)
        s_x0T = sb("s_x0T", [128, 8], F32)
        g_lg8 = sb("g_lg8", [8, 1], F32)
        g_lg = sb("g_lg", [1, 8], F32)
        g_d = sb("g_d", [1, 8], F32)
        g_ind = sb("g_ind", [1, 8], F32)
        g_l2 = sb("g_l2", [1, 8], F32)
        g_keep = sb("g_keep", [1, 8], F32)
        g_e = sb("g_e", [1, 8], F32)
        g_gt = sb("g_gt", [1, 8], F32)
        g_gc8 = sb("g_gc8", [8, 1], F32)
        g_gcb = sb("g_gcb", [8, 4], BF16)
        s_gc = sb("s_gc", [128, 4], F32)
        b_bt = sb("b_bt", [8, D], F32)
        b_bthi = sb("b_bthi", [8, D], BF16)
        b_btlo = sb("b_btlo", [8, D], BF16)
        b_w = sb("b_w", [128, 64], F32)
        b_Tr = [sb("b_Tr_0", [128, 8 * DV], BF16), sb("b_Tr_1", [128, 8 * DV], BF16)]
        s_Sin = sb("s_Sin", [128, 8 * DV], BF16)

        spec = {
            "xnT": ([C, D], BF16, 3), "o16": ([C, 2 * D], F16, 4),
            "qgT": ([C, D], BF16, 4), "base": ([C, D], F32, 5),
            "ob": ([C, 2 * D], BF16, 2), "rs": ([C, 4], F32, 2),
            "sd": ([C, 4], F32, 2), "rst": ([C, 4], F32, 2),
            "junk": ([C, DV], F32, 1), "sg": ([C, 2 * D], BF16, 2),
            "ogT": ([C, 2 * D], BF16, 1),
            "mid": ([C, 512], BF16, 3), "midT": ([C, 512], BF16, 1),
        }
        tiles = {}
        for nm, (shp, dt2, nb) in spec.items():
            tiles[nm] = [sb(f"b_{nm}_{b}", shp, dt2) for b in range(nb)]

        def T(nm, i):
            nb = spec[nm][2]
            return tiles[nm][i % nb], f"b_{nm}_{i % nb}"

        pgA = psm("pgA", [128, 1024], F32)
        pd = psm("pd", [128, 1024], F32)
        pm = psm("pm", [128, 512], F32)
        pwo = psm("pwo", [128, 1024], F32)
        ptbB = psm("ptbB", [128, 1024], BF16)
        PGA, PD, PM, PWO, PTBB = "ps_pgA", "ps_pd", "ps_pm", "ps_pwo", "ps_ptbB"

        p = Prog(nc)
        R = p.rec

        def mm(out, lhsT, rhs, start, stop, rd, wr):
            R("tensor", lambda e, o=out, l=lhsT, r=rhs, s1=start, s2=stop:
              e.matmul(o, l, r, start=s1, stop=s2), reads=rd, writes=wr)

        def tr(out, in_, ident, rd, wr):
            R("tensor", lambda e, o=out, i=in_, d=ident: e.transpose(o, i, d),
              reads=rd, writes=wr)

        def ldg(dst, src, rd, wr):
            R("gpsimd", lambda e, d=dst, s=src: e.dma_start(out=d, in_=s),
              reads=rd, writes=wr, queue="qg")

        def lda(dst, src, rd, wr):
            R("scalar", lambda e, d=dst, s=src: e.dma_start(out=d, in_=s),
              reads=rd, writes=wr, queue="qa")

        ldg(c_identb, ap["cidentb"], ["d:cidentb"], ["c_identb"])
        ldg(c_esel, ap["cesel"], ["d:cesel"], ["c_esel"])
        ldg(c_cmat, ap["cmatx"], ["d:cmatx"], ["c_cmat"])
        ldg(c_ones, ap["conesb"], ["d:conesb"], ["c_ones"])
        ldg(s_wmrow, ap["wmrow"], ["d:wmrow"], ["s_wmrow"])
        ldg(w_alay.rearrange("p (kt n) -> p kt n", kt=8),
            ap["alay"].rearrange("(kt p) n -> p kt n", p=128), ["d:alay"], ["w_alay"])
        ldg(w_bwT.rearrange("p (kt n) -> p kt n", kt=4),
            ap["bwT"].rearrange("(kt p) n -> p kt n", p=128), ["d:bwT"], ["w_bwT"])
        for half in range(2):
            lda(w_g[:, half * 8 * D:(half + 1) * 8 * D]
                .rearrange("p (kt n) -> p kt n", kt=4),
                ap["wg"][half * 512:(half + 1) * 512, :]
                .rearrange("(kt p) n -> p kt n", p=128),
                ["d:wg"], [f"w_gh{half}"])
        R("sync", lambda e: e.dma_start(
            out=w_wrT.rearrange("p (kt n) -> p kt n", kt=8),
            in_=ap["wrT"].rearrange("(kt p) n -> p kt n", p=128)),
          reads=["d:wrT"], writes=["w_wrT"], queue="qs")
        R("sync", lambda e: e.dma_start(out=s_gnw, in_=ap["gnw"]),
          reads=["d:gnw"], writes=["s_gnw"], queue="qs")
        R("sync", lambda e: e.dma_start(out=c_identf, in_=ap["cidentf"]),
          reads=["d:cidentf"], writes=["c_identf"], queue="qs")
        R("sync", lambda e: e.dma_start(out=s_x0, in_=ap["x0"]),
          reads=["d:x0"], writes=["s_x0"], queue="qs")
        R("sync", lambda e: e.dma_start(out=b_bt, in_=ap["btot_all"]),
          reads=["d:btot"], writes=["b_bt"], queue="qs")

        def emit_gates_a():
            R("vector", lambda e: e.reduce_sum(g_st[:, 0:1], s_x0, axis=AX.X),
              reads=["s_x0"], writes=["g_st"])
            R("scalar", lambda e: e.activation(out=g_junk, in_=s_x0, func=AF.Square,
                                               accum_out=g_st[:, 1:2]),
              reads=["s_x0"], writes=["g_junk", "g_st"])
            R("vector", lambda e: e.tensor_scalar(out=g_st[:, 2:3], in0=g_st[:, 0:1],
                                                  scalar1=1.0 / D, scalar2=None, op0=ALU.mult),
              reads=["g_st"], writes=["g_st"])
            R("vector", lambda e: e.tensor_scalar(out=g_st[:, 3:4], in0=g_st[:, 1:2],
                                                  scalar1=1.0 / D, scalar2=None, op0=ALU.mult),
              reads=["g_st"], writes=["g_st"])
            R("vector", lambda e: e.tensor_mul(g_st[:, 4:5], g_st[:, 2:3], g_st[:, 2:3]),
              reads=["g_st"], writes=["g_st"])
            R("vector", lambda e: e.tensor_sub(g_st[:, 4:5], g_st[:, 3:4], g_st[:, 4:5]),
              reads=["g_st"], writes=["g_st"])
            R("vector", lambda e: e.tensor_scalar(out=g_st[:, 4:5], in0=g_st[:, 4:5],
                                                  scalar1=1.0, scalar2=EPS, op0=ALU.mult,
                                                  op1=ALU.add),
              reads=["g_st"], writes=["g_st"])
            R("scalar", lambda e: e.activation(out=g_st[:, 5:6], in_=g_st[:, 4:5], func=AF.Ln),
              reads=["g_st"], writes=["g_st"])
            R("scalar", lambda e: e.activation(out=g_st[:, 6:7], in_=g_st[:, 5:6], func=AF.Exp,
                                               scale=-0.5),
              reads=["g_st"], writes=["g_st"])
            R("vector", lambda e: e.tensor_scalar(out=s_x0b, in0=s_x0, scalar1=g_st[:, 2:3],
                                                  scalar2=g_st[:, 6:7], op0=ALU.subtract,
                                                  op1=ALU.mult),
              reads=["s_x0", "g_st"], writes=["s_x0b"])
            for j in range(8):
                tr(pm[:, j:j + 1], s_x0b[:, j * C:(j + 1) * C], c_identf[0:1, 0:1],
                   ["s_x0b", "c_identf"], [PM])
            R("scalar", lambda e: e.copy(s_x0T, pm[:, 0:8]), reads=[PM], writes=["s_x0T"])
            lda(w_wo.rearrange("p (kt n) -> p kt n", kt=16),
                ap["wo"].rearrange("(kt p) n -> p kt n", p=128),
                ["d:wo"], ["w_wo"])

        def emit_gates_b():
            for j in range(8):
                mm(pd[0:8, 0:1], w_wrT[:, j * 8:(j + 1) * 8], s_x0T[:, j:j + 1],
                   j == 0, j == 7, ["w_wrT", "s_x0T"], [PD])
            R("scalar", lambda e: e.copy(g_lg8, pd[0:8, 0:1]), reads=[PD], writes=["g_lg8"])
            tr(pm[0:1, 0:8], g_lg8, c_identf[0:8, 0:8], ["g_lg8", "c_identf"], [PM])
            R("scalar", lambda e: e.copy(g_lg, pm[0:1, 0:8]), reads=[PM], writes=["g_lg"])

        def emit_gates_c():
            R("vector", lambda e: e.reduce_max(g_st[:, 7:8], g_lg, axis=AX.X),
              reads=["g_lg"], writes=["g_st"])
            R("vector", lambda e: e.tensor_scalar(out=g_d, in0=g_lg, scalar1=g_st[:, 7:8],
                                                  scalar2=None, op0=ALU.subtract),
              reads=["g_lg", "g_st"], writes=["g_d"])
            R("vector", lambda e: e.tensor_scalar(out=g_ind, in0=g_d, scalar1=0.0,
                                                  scalar2=None, op0=ALU.is_ge),
              reads=["g_d"], writes=["g_ind"])
            R("vector", lambda e: e.scalar_tensor_tensor(out=g_l2, in0=g_ind, scalar=-1e30,
                                                         in1=g_lg, op0=ALU.mult, op1=ALU.add),
              reads=["g_ind", "g_lg"], writes=["g_l2"])
            R("vector", lambda e: e.reduce_max(g_st[:, 0:1], g_l2, axis=AX.X),
              reads=["g_l2"], writes=["g_st"])
            R("vector", lambda e: e.tensor_scalar(out=g_keep, in0=g_lg, scalar1=g_st[:, 0:1],
                                                  scalar2=None, op0=ALU.is_ge),
              reads=["g_lg", "g_st"], writes=["g_keep"])
            R("scalar", lambda e: e.activation(out=g_e, in_=g_d, func=AF.Exp),
              reads=["g_d"], writes=["g_e"])
            R("vector", lambda e: e.tensor_mul(g_e, g_e, g_keep),
              reads=["g_e", "g_keep"], writes=["g_e"])
            R("vector", lambda e: e.reduce_sum(g_st[:, 1:2], g_e, axis=AX.X),
              reads=["g_e"], writes=["g_st"])
            R("vector", lambda e: e.reciprocal(g_st[:, 2:3], g_st[:, 1:2]),
              reads=["g_st"], writes=["g_st"])
            R("vector", lambda e: e.tensor_scalar(out=g_gt, in0=g_e, scalar1=g_st[:, 2:3],
                                                  scalar2=None, op0=ALU.mult),
              reads=["g_e", "g_st"], writes=["g_gt"])
            tr(pwo[0:8, 8:9], g_gt, c_identf[0:1, 0:1], ["g_gt", "c_identf"], [PWO])
            R("scalar", lambda e: e.copy(g_gc8, pwo[0:8, 8:9]), reads=[PWO], writes=["g_gc8"])
            R("vector", lambda e: e.tensor_copy(g_gcb[:, 0:1], g_gc8),
              reads=["g_gc8"], writes=["g_gcb"])
            R("vector", lambda e: e.tensor_sub(g_gcb[:, 2:3], g_gc8, g_gcb[:, 0:1]),
              reads=["g_gc8", "g_gcb"], writes=["g_gcb"])
            for tau in range(4):
                for hl in range(2):
                    mm(pwo[:, tau:tau + 1], c_esel[:, tau * 128:(tau + 1) * 128],
                       g_gcb[:, 2 * hl:2 * hl + 1], (tau == 0 and hl == 0),
                       (tau == 3 and hl == 1), ["c_esel", "g_gcb"], [PWO])
            R("scalar", lambda e: e.copy(s_gc, pwo[:, 0:4]), reads=[PWO], writes=["s_gc"])
            for tau in range(4):
                R("vector", lambda e, u=tau: e.tensor_scalar(
                    out=w_bwT[:, u * D:(u + 1) * D], in0=w_bwT[:, u * D:(u + 1) * D],
                    scalar1=s_gc[:, u:u + 1], scalar2=None, op0=ALU.mult),
                  reads=["w_bwT", "s_gc"], writes=["w_bwT"])
            for k in range(16):
                R("vector", lambda e, kk=k: e.tensor_scalar(
                    out=w_wo[:, kk * D:(kk + 1) * D], in0=w_wo[:, kk * D:(kk + 1) * D],
                    scalar1=s_gnw[:, kk % 4:kk % 4 + 1], scalar2=None, op0=ALU.mult),
                  reads=["w_wo", "s_gnw"], writes=[f"w_wo{k}"])


        def emit_combine():
            R("vector", lambda e: e.tensor_copy(b_bthi, b_bt), reads=["b_bt"], writes=["b_bthi"])
            R("vector", lambda e: e.tensor_sub(b_btlo, b_bt, b_bthi),
              reads=["b_bt", "b_bthi"], writes=["b_btlo"])
            for tau in range(8):
                mm(pm[:, tau * 8:(tau + 1) * 8], b_bthi[:, tau * 128:(tau + 1) * 128],
                   c_cmat, tau == 0, False, ["b_bthi", "c_cmat"], [PM])
                mm(pm[:, tau * 8:(tau + 1) * 8], b_btlo[:, tau * 128:(tau + 1) * 128],
                   c_cmat, False, False, ["b_btlo", "c_cmat"], [PM])
                mm(pm[:, tau * 8:(tau + 1) * 8], c_ones[0:1, :], s_wmrow,
                   False, tau == 7, ["c_ones", "s_wmrow"], [PM])
            R("scalar", lambda e: e.activation(out=b_w, in_=pm[:, 0:64], func=AF.Exp),
              reads=[PM], writes=["b_w"])
            R("vector", lambda e: e.memset(s_Sin, 0.0), writes=["s_Sin"])
            for r in range(NCORES):
                tb = b_Tr[r % 2]
                tbn = f"b_Tr_{r % 2}"
                R("gpsimd" if r < 4 else "sync",
                  lambda e, rr=r, d=tb: e.dma_start(
                      out=d, in_=ap["T_all"][rr * C:(rr + 1) * C, :]),
                  reads=["d:Tall"], writes=[tbn],
                  queue="qg" if r < 4 else "qs")
                for sg_ in range(8):
                    R("vector", lambda e, g=sg_, rr=r, d=tb: e.scalar_tensor_tensor(
                        out=s_Sin[:, g * DV:(g + 1) * DV], in0=d[:, g * DV:(g + 1) * DV],
                        scalar=b_w[:, g * 8 + rr:g * 8 + rr + 1],
                        in1=s_Sin[:, g * DV:(g + 1) * DV],
                        op0=ALU.mult, op1=ALU.add),
                      reads=[tbn, "b_w", "s_Sin"], writes=["s_Sin"])



        def B0(i):
            for nm, key in [("xnT", "xnT_d"), ("o16", "o16_d"),
                            ("qgT", "qgT_d"), ("base", "base_d")]:
                d, dn = T(nm, i)
                R("sync", lambda e, cc=i, dd=d, k=key: e.dma_start(out=dd, in_=ap[k][cc]),
                  reads=["d:" + key], writes=[dn], queue="qs")

        def B1a(i):
            xnT, xnTn = T("xnT", i)
            sg, sgn = T("sg", i)
            for j in range(8):
                for nh in range(2):
                    mm(pgA[:, nh * 512:(nh + 1) * 512], xnT[:, j * C:(j + 1) * C],
                       w_g[:, j * 2 * D + nh * 512: j * 2 * D + (nh + 1) * 512],
                       j == 0, j == 7, [xnTn, f"w_gh{j // 4}"], [PGA])
            R("scalar", lambda e: e.activation(out=sg[:, 0:D], in_=pgA, func=AF.Silu),
              reads=[PGA], writes=[sgn])

        def B1b(i):
            xnT, xnTn = T("xnT", i)
            sg, sgn = T("sg", i)
            mid, midn = T("mid", i)
            for j in range(8):
                for nh in range(2):
                    mm(pgA[:, nh * 512:(nh + 1) * 512], xnT[:, j * C:(j + 1) * C],
                       w_g[:, j * 2 * D + D + nh * 512: j * 2 * D + D + (nh + 1) * 512],
                       j == 0, j == 7, [xnTn, f"w_gh{j // 4}"], [PGA])
            R("scalar", lambda e: e.activation(out=sg[:, D:2 * D], in_=pgA, func=AF.Silu),
              reads=[PGA], writes=[sgn])
            for j in range(8):
                mm(pm[:, 0:512], xnT[:, j * C:(j + 1) * C],
                   w_alay[:, j * 512:(j + 1) * 512], j == 0, j == 7,
                   [xnTn, "w_alay"], [PM])
            R("vector", lambda e: e.tensor_copy(mid, pm[:, 0:512]), reads=[PM], writes=[midn])

        def B1d(i):
            o16, o16n = T("o16", i)
            qgT, qgTn = T("qgT", i)
            ob, obn = T("ob", i)
            for rnd in range(2):
                for hi in range(2):
                    h = 2 * rnd + hi
                    for ji, j in enumerate((2 * h, 2 * h + 1)):
                        mm(pd[:, hi * DV:(hi + 1) * DV], qgT[:, j * C:(j + 1) * C],
                           s_Sin[:, j * DV:(j + 1) * DV], ji == 0, ji == 1,
                           [qgTn, "s_Sin"], [PD])
                for hi in range(2):
                    h = 2 * rnd + hi
                    R("vector", lambda e, h2=h, i2=hi: e.tensor_tensor(
                        ob[:, h2 * DV:(h2 + 1) * DV], o16[:, h2 * DV:(h2 + 1) * DV],
                        pd[:, i2 * DV:(i2 + 1) * DV], ALU.add),
                      reads=[o16n, PD], writes=[obn])

        def B1c(i):
            ob, obn = T("ob", i)
            rs, rsn = T("rs", i)
            sd, sdn = T("sd", i)
            rst, rstn = T("rst", i)
            junk, junkn = T("junk", i)
            sg, sgn = T("sg", i)
            t1, t1n = ob, obn
            for h in range(4):
                R("scalar", lambda e, hh=h: e.activation(
                    out=junk, in_=ob[:, hh * DV:(hh + 1) * DV], func=AF.Square,
                    accum_out=rs[:, hh:hh + 1]),
                  reads=[obn], writes=[junkn, rsn])
            R("vector", lambda e: e.tensor_scalar(out=rs, in0=rs, scalar1=1.0 / DV,
                                                  scalar2=EPS, op0=ALU.mult, op1=ALU.add),
              reads=[rsn], writes=[rsn])
            R("scalar", lambda e: e.activation(out=sd, in_=rs, func=AF.Ln),
              reads=[rsn], writes=[sdn])
            R("scalar", lambda e: e.activation(out=rst, in_=sd, func=AF.Exp, scale=-0.5),
              reads=[sdn], writes=[rstn])
            R("vector", lambda e: e.tensor_mul(t1, ob, sg),
              reads=[obn, sgn], writes=[t1n])
            for h in range(4):
                R("vector", lambda e, hh=h: e.tensor_scalar(
                    out=t1[:, hh * DV:(hh + 1) * DV], in0=t1[:, hh * DV:(hh + 1) * DV],
                    scalar1=rst[:, hh:hh + 1], scalar2=None, op0=ALU.mult),
                  reads=[t1n, rstn], writes=[t1n])

        def B2mid(i):
            mid, midn = T("mid", i)
            midT, midTn = T("midT", i)
            for j in range(4):
                tr(ptbB[:, j * C:(j + 1) * C], mid[:, j * C:(j + 1) * C], c_identb,
                   [midn, "c_identb"], [PTBB])
            R("scalar", lambda e: e.copy(midT, ptbB[:, 0:512]), reads=[PTBB], writes=[midTn])

        def B2og(i, half):
            t1, t1n = T("ob", i)
            ogT, ogTn = T("ogT", i)
            for j in range(8):
                tr(ptbB[:, j * C:(j + 1) * C],
                   t1[:, half * D + j * C: half * D + (j + 1) * C],
                   c_identb, [t1n, "c_identb"], [PTBB])
            R("scalar", lambda e, h=half: e.copy(ogT[:, h * D:(h + 1) * D], ptbB),
              reads=[PTBB], writes=[ogTn])

        def B2wo(i):
            ogT, ogTn = T("ogT", i)
            midT, midTn = T("midT", i)
            base, basen = T("base", i)
            for nh in range(2):
                sl = slice(nh * 512, (nh + 1) * 512)
                for k in range(16):
                    mm(pwo[:, sl], ogT[:, k * C:(k + 1) * C],
                       w_wo[:, k * D + nh * 512: k * D + (nh + 1) * 512],
                       k == 0, False, [ogTn, f"w_wo{k}"], [PWO])
                for j in range(4):
                    mm(pwo[:, sl], midT[:, j * C:(j + 1) * C],
                       w_bwT[:, j * D + nh * 512: j * D + (nh + 1) * 512],
                       False, j == 3, [midTn, "w_bwT"], [PWO])
            R("vector", lambda e: e.tensor_add(base, base, pwo),
              reads=[basen, PWO], writes=[basen])
            R("sync", lambda e, cc=i: e.dma_start(
                out=ap["y"][cc * C:(cc + 1) * C, :], in_=base),
              reads=[basen], writes=["d:y"], queue="qs")

        B0(0)
        B0(1)
        emit_gates_a()
        emit_combine()
        for i in range(NCH + 3):
            if 0 <= i + 2 < NCH:
                B0(i + 2)
            if 0 <= i - 2 < NCH:
                B2mid(i - 2)
            if 0 <= i < NCH:
                B1a(i)
            if i == 0:
                emit_gates_b()
            if 0 <= i - 2 < NCH:
                B2og(i - 2, 0)
            if 0 <= i < NCH:
                B1b(i)
            if i == 0:
                emit_gates_c()
            if 0 <= i - 2 < NCH:
                B2og(i - 2, 1)
            if 0 <= i - 1 < NCH:
                B1d(i - 1)
            if 0 <= i - 2 < NCH:
                B2wo(i - 2)
            if 0 <= i - 1 < NCH:
                B1c(i - 1)

        p.barrier()
        block = es.enter_context(nc.Block())
        p.finalize(es, block)
    return nc


def _prep_shared(inputs):
    con = _tile_consts()
    w = {}
    w["wq"] = np.ascontiguousarray(inputs["L0_attn_Wq"], np.float32)
    w["wk"] = np.ascontiguousarray(inputs["L0_attn_Wk"], np.float32)
    w["wv"] = np.ascontiguousarray(inputs["L0_attn_Wv"], np.float32)
    w["wgk1"] = np.ascontiguousarray(inputs["L0_attn_Wgk1"], np.float32)
    w["wgk2"] = np.ascontiguousarray(inputs["L0_attn_Wgk2"], np.float32)
    w["bgk2"] = np.ascontiguousarray(inputs["L0_attn_bgk2"], np.float32).reshape(1, D)
    w["wg"] = np.ascontiguousarray(inputs["L0_attn_Wg"], np.float32)
    w["wo"] = np.ascontiguousarray(inputs["L0_attn_Wo"], np.float32)
    w["alay"] = np.ascontiguousarray(
        np.asarray(inputs["L0_ffn_A"], np.float32).reshape(512, D).T)
    w["bwT"] = np.ascontiguousarray(
        np.asarray(inputs["L0_ffn_Bw"], np.float32).transpose(0, 2, 1).reshape(512, D))
    w["wrT"] = np.ascontiguousarray(np.asarray(inputs["L0_ffn_Wr"], np.float32).T)
    w["gnw"] = np.ascontiguousarray(
        np.asarray(inputs["L0_attn_g_norm_weight"], np.float32).reshape(4, 128).T)
    x = np.asarray(inputs["L0_attn_x"], np.float32).reshape(NCORES * TOK, D)
    w["x0"] = np.ascontiguousarray(x[0:1, :])
    return con, w, x


def kernel(**inputs):
    con, w, x = _prep_shared(inputs)

    nc_a = build_phase_a()
    in_a = []
    for cidx in range(NCORES):
        m = dict(
            x=np.ascontiguousarray(x[cidx * TOK:(cidx + 1) * TOK]),
            wq=w["wq"], wk=w["wk"], wv=w["wv"], wgk1=w["wgk1"], wgk2=w["wgk2"],
            bgk2=w["bgk2"],
            ctrib=con["trib"], ctrir=con["trir"], cidentb=con["identb"],
            conesb=con["onesb"], cmask=con["mask"], cidentf=con["identf"],
            cln16=np.full((C, 1), -LN16, np.float32),
            wg=w["wg"], wo=w["wo"], alay=w["alay"], bwT=w["bwT"],
        )
        in_a.append(m)
    res_a = run_bass_kernel_spmd(nc_a, in_a, core_ids=list(range(NCORES)))
    ra = res_a.results

    T_all = np.concatenate([np.asarray(ra[cidx]["T_d"]) for cidx in range(NCORES)], axis=0)
    btot_all = np.zeros((8, D), np.float32)
    for cidx in range(NCORES):
        btot_all[cidx] = np.asarray(ra[cidx]["btot_d"])[0]

    nc_b = build_phase_b()
    in_b = []
    for cidx in range(NCORES):
        m = dict(
            o16_d=np.asarray(ra[cidx]["o16_d"]),
            qgT_d=np.asarray(ra[cidx]["qgT_d"]),
            base_d=np.asarray(ra[cidx]["base_d"]),
            xnT_d=np.asarray(ra[cidx]["xnT_d"]),
            T_all=T_all, btot_all=btot_all,
            x0=w["x0"], wg=np.asarray(ra[cidx]["wgb"]), wo=np.asarray(ra[cidx]["wob"]),
            alay=np.asarray(ra[cidx]["alayb"]), bwT=np.asarray(ra[cidx]["bwTb"]),
            wrT=w["wrT"], gnw=w["gnw"],
            cmatx=_cmat(cidx),
            wmrow=np.where(np.arange(8) >= cidx, -1e4, 0.0).astype(np.float32)
                    .astype(BF).reshape(1, 8),
            conesb=con["onesb"],
            cidentb=con["identb"], cidentf=con["identf"], cesel=con["esel"],
        )
        in_b.append(m)
    res_b = run_bass_kernel_spmd(nc_b, in_b, core_ids=list(range(NCORES)))
    y = np.concatenate([np.asarray(res_b.results[cidx]["y"]) for cidx in range(NCORES)],
                       axis=0)
    return y.reshape(1, NCORES * TOK, D)


# version 12
# speedup vs baseline: 1.0696x; 1.0147x over previous
import numpy as np
import ml_dtypes
from contextlib import ExitStack

import concourse.bass as bass
import concourse.mybir as mybir
from concourse.bass_utils import run_bass_kernel_spmd

F32 = mybir.dt.float32
BF16 = mybir.dt.bfloat16
F16 = mybir.dt.float16
AF = mybir.ActivationFunctionType
ALU = mybir.AluOpType
AX = mybir.AxisListType
I32 = mybir.dt.int32
BF = ml_dtypes.bfloat16

NCORES = 8
TOK = 2048
D = 1024
C = 128
NCH = TOK // C
HEADS = 4
DK = 256
DV = 512
EPS = 1e-5
LN16 = float(np.log(16.0))

ENGS = ("tensor", "vector", "scalar", "gpsimd", "sync")


class Prog:

    NSLOT = 8
    NSLOT_MAP = {}

    def _nslot(self, q):
        return self.NSLOT_MAP.get(q, self.NSLOT)

    def __init__(self, nc):
        self.nc = nc
        self.ops = {e: [] for e in ENGS}
        self.nops = {e: 0 for e in ENGS}
        self.qcount = {}
        self.last_write = {}
        self.readers = {}
        self.last_psum = {}
        self.waited = {e: {} for e in ENGS}
        self.need_inc = {e: set() for e in ENGS}

    def _src_key(self, tok):
        return ("E", tok[1]) if tok[0] == "E" else ("Q",) + tuple(tok[1])

    def rec(self, eng, emit, reads=(), writes=(), queue=None, tag=""):
        deps = {}

        def add(tok):
            k = self._src_key(tok)
            v = tok[2]
            if k not in deps or deps[k] < v:
                deps[k] = v

        for r in reads:
            t = self.last_write.get(r)
            if t is not None:
                add(t)
            if r.startswith("ps_"):
                t2 = self.last_psum.get(r)
                if t2 is not None:
                    add(t2)
        for w in writes:
            t = self.last_write.get(w)
            if t is not None:
                add(t)
            for kk, vv in self.readers.get(w, {}).items():
                add(("E", kk[1], vv) if kk[0] == "E" else ("Q", (kk[1], kk[2]), vv))
            if w.startswith("ps_"):
                t2 = self.last_psum.get(w)
                if t2 is not None:
                    add(t2)

        if queue is not None:
            idx = None
            n = self.qcount.get(queue, 0) + 1
            self.qcount[queue] = n
            ns = self._nslot(queue)
            slot = (n - 1) % ns
            gen = (n - 1) // ns + 1
            my = ("Q", (queue, slot), gen)
            if gen > 1:
                add(("Q", (queue, slot), gen - 1))
        else:
            self.nops[eng] += 1
            idx = self.nops[eng]
            my = ("E", eng, idx)
        myk = self._src_key(my)

        waits = []
        for k, v in deps.items():
            if k[0] == "E" and k[1] == "tensor" and eng == "tensor":
                continue
            got = self.waited[eng].get(k, 0)
            if v <= got:
                continue
            waits.append((k, v))
            self.waited[eng][k] = v
            if k[0] == "E":
                self.need_inc[k[1]].add(v)

        for r in reads:
            self.readers.setdefault(r, {})[myk] = max(
                self.readers.get(r, {}).get(myk, 0), my[2]
            )
            if r.startswith("ps_"):
                self.last_psum[r] = my
        for w in writes:
            self.last_write[w] = my
            self.readers[w] = {}
            if w.startswith("ps_"):
                self.last_psum[w] = my

        self.ops[eng].append(
            dict(emit=emit, waits=waits, queue=queue, idx=idx, tag=tag)
        )
        return my

    def _queue_toks(self):
        toks = []
        for q, n in self.qcount.items():
            ns = self._nslot(q)
            for slot in range(min(n, ns)):
                gen = (n - 1 - slot) // ns + 1
                toks.append(("Q", (q, slot), gen))
        return toks

    def barrier(self):
        toks = []
        for e in ENGS:
            if self.nops[e]:
                toks.append(("E", e, self.nops[e]))
        toks += self._queue_toks()
        for e in ENGS:
            for t in toks:
                k = self._src_key(t)
                if k == ("E", e):
                    continue
                v = t[2]
                if self.waited[e].get(k, 0) < v:
                    self.waited[e][k] = v
                    if k[0] == "E":
                        self.need_inc[k[1]].add(v)
                    self.ops[e].append(dict(emit=None, waits=[(k, v)], queue=None,
                                            idx=None, tag="barrier"))

    def finalize(self, es, block):
        sems = {}
        for e in ENGS:
            sems[("E", e)] = es.enter_context(self.nc.semaphore(f"sem_{e}"))
        for q in self.qcount:
            for slot in range(self._nslot(q)):
                sems[("Q", q, slot)] = es.enter_context(
                    self.nc.semaphore(f"sem_{q}_{slot}"))
        inc_prefix = {}
        for e in ENGS:
            pref = {}
            cnt = 0
            marks = self.need_inc[e]
            for i in range(1, self.nops[e] + 1):
                if i in marks:
                    cnt += 1
                pref[i] = cnt
            inc_prefix[e] = pref

        def val_of(k, v):
            if k[0] == "Q":
                return 16 * v
            return inc_prefix[k[1]].get(v, 0)

        qseen = {}

        def body(e):
            def run(eng):
                for op in self.ops[e]:
                    for (k, v) in op["waits"]:
                        eng.wait_ge(sems[k], val_of(k, v))
                    if op["emit"] is None:
                        continue
                    inst = op["emit"](eng)
                    if op["queue"] is not None:
                        q = op["queue"]
                        n = qseen.get(q, 0)
                        qseen[q] = n + 1
                        inst.then_inc(sems[("Q", q, n % self.NSLOT)], 16)
                    elif op["idx"] in self.need_inc[e]:
                        inst.then_inc(sems[("E", e)], 1)
            return run

        for e in ENGS:
            if self.ops[e]:
                getattr(block, e)(body(e))


def _tile_consts():
    i = np.arange(C)
    trib = ((i[:, None] <= i[None, :]).astype(np.float32) / 16.0).astype(BF)
    trir = ((i[:, None] > i[None, :]).astype(np.float32) / 16.0).astype(BF)
    mask = np.tile((i[:, None] <= i[None, :]).astype(np.float32), (1, 4))
    identb = np.eye(C, dtype=np.float32).astype(BF)
    identf = np.eye(C, dtype=np.float32)
    onesb = np.ones((C, C), dtype=np.float32).astype(BF)
    esel = np.zeros((8, 512), dtype=np.float32)
    for p in range(512):
        esel[p // 64, p] = 1.0
    return dict(trib=trib, trir=trir, mask=mask, identb=identb, identf=identf,
                onesb=onesb, esel=esel.astype(BF))


def _cmat(core):
    m_ = np.zeros((8, 8), dtype=np.float32)
    for r in range(8):
        for m in range(8):
            if r < m < core:
                m_[m, r] = 1.0
    for r in range(8):
        if r >= core:
            m_[:, r] = 0.0
    return m_.astype(BF)


def _wmask(core):
    m = np.zeros((8,), np.float32)
    m[:core] = 1.0
    return m



def build_phase_a():
    nc = bass.Bass("TRN2", target_bir_lowering=False, debug=False)
    ap = {}
    ap["x"] = nc.dram_tensor("x", [TOK, D], F32, kind="ExternalInput").ap()
    for n, shp in [("wq", [D, D]), ("wk", [D, D]), ("wv", [D, 2 * D]),
                   ("wgk1", [D, 16]), ("wgk2", [16, D]), ("bgk2", [1, D])]:
        ap[n] = nc.dram_tensor(n, shp, F32, kind="ExternalInput").ap()
    for n, shp in [("ctrib", [C, C]), ("ctrir", [C, C]), ("cidentb", [C, C]),
                   ("conesb", [C, C])]:
        ap[n] = nc.dram_tensor(n, shp, BF16, kind="ExternalInput").ap()
    ap["cmask"] = nc.dram_tensor("cmask", [C, 4 * C], F32, kind="ExternalInput").ap()
    ap["cidentf"] = nc.dram_tensor("cidentf", [C, C], F32, kind="ExternalInput").ap()
    ap["cln16"] = nc.dram_tensor("cln16", [C, 1], F32, kind="ExternalInput").ap()
    ap["o16_d"] = nc.dram_tensor("o16_d", [NCH, C, 2 * D], F16, kind="ExternalOutput").ap()
    ap["qgT_d"] = nc.dram_tensor("qgT_d", [NCH, C, D], BF16, kind="ExternalOutput").ap()
    ap["base_d"] = nc.dram_tensor("base_d", [NCH, C, D], F32, kind="ExternalOutput").ap()
    ap["xnT_d"] = nc.dram_tensor("xnT_d", [NCH, C, D], BF16, kind="ExternalOutput").ap()
    ap["T_d"] = nc.dram_tensor("T_d", [C, 8 * DV], BF16, kind="ExternalOutput").ap()
    ap["btot_d"] = nc.dram_tensor("btot_d", [1, D], F32, kind="ExternalOutput").ap()
    for n, shp in [("wg", [D, 2 * D]), ("wo", [2 * D, D]),
                   ("alay", [D, 512]), ("bwT", [512, D])]:
        ap[n] = nc.dram_tensor(n, shp, F32, kind="ExternalInput").ap()
        ap[n + "b"] = nc.dram_tensor(n + "b", shp, BF16, kind="ExternalOutput").ap()

    es = ExitStack()
    with es:
        def sb(n, s, t):
            return es.enter_context(nc.sbuf_tensor(n, s, t))[:]

        def psm(n, s, t):
            return es.enter_context(nc.psum_tensor(n, s, t))[:]

        w_q = sb("w_q", [128, 8 * D], BF16)
        w_k = sb("w_k", [128, 8 * D], BF16)
        w_v = sb("w_v", [128, 8 * 2 * D], BF16)
        w_gk1 = sb("w_gk1", [128, 8 * 16], BF16)
        w_gk2 = sb("w_gk2", [16, D], BF16)
        s_bgk2 = sb("s_bgk2", [1, D], BF16)
        c_trib = sb("c_trib", [C, C], BF16)
        c_trir = sb("c_trir", [C, C], BF16)
        c_identb = sb("c_identb", [C, C], BF16)
        c_ones = sb("c_ones", [C, C], BF16)
        c_mask4 = sb("c_mask4", [C, 4 * C], F32)
        c_ln16 = sb("c_ln16", [C, 1], F32)
        s_Sb = sb("s_Sb", [128, 8 * DV], BF16)
        s_bpreT = sb("s_bpreT", [C, 8], F32)
        s_btrow = sb("s_btrow", [1, D], F32)
        c_identf = sb("c_identf", [C, C], F32)

        spec = {
            "x": ([C, D], F32, 2), "st": ([C, 8], F32, 1), "xn": ([C, D], F32, 1),
            "xnb": ([C, D], BF16, 2), "xnT": ([C, D], BF16, 3), "f1": ([C, D], F32, 1),
            "ghi": ([C, D], BF16, 2), "glo": ([C, D], BF16, 2),
            "z1e": ([C, 16], BF16, 1), "z1b": ([16, C], BF16, 1),
            "lam": ([C, D], BF16, 2), "lami": ([C, D], BF16, 2),
            "lrev": ([C, D], BF16, 2),
            "qh": ([C, D], BF16, 2),
            "bnd": ([C, 8], F32, 2), "expbp": ([C, 8], F32, 4),
            "kt": ([C, D], BF16, 2), "kh": ([C, D], BF16, 3),
            "v": ([C, 2 * D], BF16, 3),
            "qhT": ([C, D], BF16, 3), "ktT": ([C, D], BF16, 2), "qgT": ([C, D], BF16, 2),
            "pt": ([C, 4 * C], BF16, 2), "o16": ([C, 2 * D], F16, 2),
            "dc": ([C, 8], F32, 5),
        }
        tiles = {}
        for nm, (shp, dt2, nb) in spec.items():
            tiles[nm] = [sb(f"t_{nm}_{b}", shp, dt2) for b in range(nb)]

        def T(nm, i):
            nb = spec[nm][2]
            return tiles[nm][i % nb], f"t_{nm}_{i % nb}"

        ptb1 = psm("ptb1", [128, 1024], BF16)
        ptb4 = psm("ptb4", [128, 1024], BF16)
        pc = psm("pc", [128, 1024], F32)
        pp = psm("pp", [128, 1024], F32)
        psc = psm("psc", [128, 512], F32)
        psc2 = psm("psc2", [128, 512], F32)
        PTB1, PTB4, PC, PP, PSC, PSC2 = ("ps_ptb1", "ps_ptb4", "ps_pc", "ps_pp",
                                         "ps_psc", "ps_psc2")

        p = Prog(nc)
        R = p.rec

        def ld(engq, dst, src, rd, wr):
            R(engq[0], lambda e, d=dst, s=src: e.dma_start(out=d, in_=s),
              reads=rd, writes=wr, queue=engq[1])

        gq = ("gpsimd", "qg")
        sq = ("sync", "qs")
        for kt in range(8):
            ld(gq, w_q[:, kt * D:(kt + 1) * D],
               ap["wq"][kt * 128:(kt + 1) * 128, :], ["d:wq"], ["w_q"])
            ld(gq, w_k[:, kt * D:(kt + 1) * D],
               ap["wk"][kt * 128:(kt + 1) * 128, :], ["d:wk"], ["w_k"])
            ld(gq, w_v[:, kt * 2 * D:(kt + 1) * 2 * D],
               ap["wv"][kt * 128:(kt + 1) * 128, :], ["d:wv"], ["w_v"])
        ld(gq, w_gk1.rearrange("p (kt n) -> p kt n", kt=8),
           ap["wgk1"].rearrange("(kt p) n -> p kt n", p=128), ["d:wgk1"], ["w_gk1"])
        ld(gq, w_gk2, ap["wgk2"], ["d:wgk2"], ["w_gk2"])
        ld(gq, s_bgk2, ap["bgk2"], ["d:bgk2"], ["s_bgk2"])
        ld(gq, c_trib, ap["ctrib"], ["d:ctrib"], ["c_trib"])
        ld(gq, c_trir, ap["ctrir"], ["d:ctrir"], ["c_trir"])
        ld(gq, c_identb, ap["cidentb"], ["d:cidentb"], ["c_identb"])
        ld(gq, c_ones, ap["conesb"], ["d:conesb"], ["c_ones"])
        ld(sq, c_mask4, ap["cmask"], ["d:cmask"], ["c_mask4"])
        ld(sq, c_identf, ap["cidentf"], ["d:cidentf"], ["c_identf"])
        ld(sq, c_ln16, ap["cln16"], ["d:cln16"], ["c_ln16"])
        for n in ("wg", "wo", "alay", "bwT"):
            ld(gq, ap[n + "b"], ap[n], ["d:" + n], ["d:" + n + "b"])
        R("vector", lambda e: e.memset(s_Sb, 0.0), writes=["s_Sb"])
        R("vector", lambda e: e.memset(s_bpreT, 0.0), writes=["s_bpreT"])

        def mm(out, lhsT, rhs, start, stop, rd, wr):
            R("tensor", lambda e, o=out, l=lhsT, r=rhs, s1=start, s2=stop:
              e.matmul(o, l, r, start=s1, stop=s2), reads=rd, writes=wr)

        def tr(out, in_, ident, rd, wr):
            R("tensor", lambda e, o=out, i=in_, d=ident: e.transpose(o, i, d),
              reads=rd, writes=wr)

        def S0(i):
            d, dn = T("x", i)
            R("sync", lambda e, dd=d, sl=ap["x"][i * C:(i + 1) * C, :]:
              e.dma_start(out=dd, in_=sl), reads=["d:x"], writes=[dn], queue="qs")

        def S1a_e(i):
            x, xn_ = T("x", i)
            st, stn = T("st", i)
            xn, xnn = T("xn", i)
            xnb, xnbn = T("xnb", i)
            xnT, xnTn = T("xnT", i)
            f1, f1n = T("f1", i)
            R("vector", lambda e: e.reduce_sum(st[:, 0:1], x, axis=AX.X),
              reads=[xn_], writes=[stn])
            R("scalar", lambda e: e.activation(out=f1, in_=x, func=AF.Square,
                                               accum_out=st[:, 1:2]),
              reads=[xn_], writes=[f1n, stn])
            R("vector", lambda e: e.tensor_scalar(out=st[:, 2:3], in0=st[:, 0:1],
                                                  scalar1=1.0 / D, scalar2=None, op0=ALU.mult),
              reads=[stn], writes=[stn])
            R("vector", lambda e: e.tensor_scalar(out=st[:, 3:4], in0=st[:, 1:2],
                                                  scalar1=1.0 / D, scalar2=None, op0=ALU.mult),
              reads=[stn], writes=[stn])
            R("vector", lambda e: e.tensor_mul(st[:, 4:5], st[:, 2:3], st[:, 2:3]),
              reads=[stn], writes=[stn])
            R("vector", lambda e: e.tensor_sub(st[:, 4:5], st[:, 3:4], st[:, 4:5]),
              reads=[stn], writes=[stn])
            R("vector", lambda e: e.tensor_scalar(out=st[:, 4:5], in0=st[:, 4:5],
                                                  scalar1=1.0, scalar2=EPS, op0=ALU.mult,
                                                  op1=ALU.add),
              reads=[stn], writes=[stn])
            R("scalar", lambda e: e.activation(out=st[:, 5:6], in_=st[:, 4:5], func=AF.Ln),
              reads=[stn], writes=[stn])
            R("scalar", lambda e: e.activation(out=st[:, 6:7], in_=st[:, 5:6], func=AF.Exp,
                                               scale=-0.5),
              reads=[stn], writes=[stn])
            R("vector", lambda e: e.tensor_scalar(out=xn, in0=x, scalar1=st[:, 2:3],
                                                  scalar2=st[:, 6:7], op0=ALU.subtract,
                                                  op1=ALU.mult),
              reads=[xn_, stn], writes=[xnn])
            R("scalar", lambda e: e.copy(xnb, xn), reads=[xnn], writes=[xnbn])
            R("gpsimd", lambda e: e.tensor_add(x, x, xn), reads=[xn_, xnn], writes=[xn_])
            R("sync", lambda e, cc=i: e.dma_start(out=ap["base_d"][cc], in_=x),
              reads=[xn_], writes=["d:base"], queue="qs")

        def S1a_t(i):
            xnb, xnbn = T("xnb", i)
            xnT, xnTn = T("xnT", i)
            for j in range(8):
                tr(ptb1[:, j * C:(j + 1) * C], xnb[:, j * C:(j + 1) * C], c_identb,
                   [xnbn, "c_identb"], [PTB1])
            R("scalar", lambda e: e.copy(xnT, ptb1), reads=[PTB1], writes=[xnTn])
            R("scalar", lambda e, cc=i: e.dma_start(out=ap["xnT_d"][cc], in_=xnT),
              reads=[xnTn], writes=["d:xnT"], queue="qa")

        def S1b(i):
            xnT, xnTn = T("xnT", i)
            xn, xnn = T("xn", i)
            f1, f1n = T("f1", i)
            ghi, ghin = T("ghi", i)
            glo, glon = T("glo", i)
            z1e, z1en = T("z1e", i)
            z1b, z1bn = T("z1b", i)
            for j in range(8):
                mm(pc[:, 0:16], xnT[:, j * C:(j + 1) * C],
                   w_gk1[:, j * 16:(j + 1) * 16], j == 0, j == 7, [xnTn, "w_gk1"], [PC])
            R("vector", lambda e: e.tensor_copy(z1e, pc[:, 0:16]), reads=[PC], writes=[z1en])
            tr(ptb1[0:16, 0:C], z1e, c_identb, [z1en, "c_identb"], [PTB1])
            R("scalar", lambda e: e.copy(z1b, ptb1[0:16, 0:C]), reads=[PTB1], writes=[z1bn])
            for nh in range(2):
                sl = slice(nh * 512, (nh + 1) * 512)
                mm(pc[:, sl], z1b, w_gk2[:, sl], True, False, [z1bn, "w_gk2"], [PC])
                mm(pc[:, sl], c_ones[0:1, :], s_bgk2[:, sl], False, True,
                   ["c_ones", "s_bgk2"], [PC])
            R("scalar", lambda e: e.activation(out=f1, in_=pc, func=AF.Abs),
              reads=[PC], writes=[f1n])
            R("scalar", lambda e: e.activation(out=f1, in_=f1, func=AF.Exp, scale=-1.0),
              reads=[f1n], writes=[f1n])
            R("scalar", lambda e: e.activation(out=f1, in_=f1, func=AF.Ln, bias=1.0),
              reads=[f1n], writes=[f1n])
            R("vector", lambda e: e.tensor_scalar(out=xn, in0=pc, scalar1=0.0,
                                                  scalar2=None, op0=ALU.min),
              reads=[PC], writes=[xnn])
            R("vector", lambda e: e.tensor_sub(xn, xn, f1), reads=[xnn, f1n], writes=[xnn])
            R("scalar", lambda e: e.copy(ghi, xn), reads=[xnn], writes=[ghin])
            R("vector", lambda e: e.tensor_sub(glo, xn, ghi), reads=[xnn, ghin], writes=[glon])

        def S2a(i):
            ghi, ghin = T("ghi", i)
            glo, glon = T("glo", i)
            lam, lamn = T("lam", i)
            lami, lamin = T("lami", i)
            for nh in range(2):
                sl = slice(nh * 512, (nh + 1) * 512)
                mm(pc[:, sl], c_trib, ghi[:, sl], True, False, ["c_trib", ghin], [PC])
                mm(pc[:, sl], c_trib, glo[:, sl], False, True, ["c_trib", glon], [PC])
            R("scalar", lambda e: e.activation(out=lam, in_=pc, func=AF.Exp, bias=c_ln16),
              reads=[PC, "c_ln16"], writes=[lamn])
            R("scalar", lambda e: e.activation(out=lami, in_=pc, func=AF.Exp, scale=-1.0),
              reads=[PC], writes=[lamin])

        def S2b(i):
            ghi, ghin = T("ghi", i)
            glo, glon = T("glo", i)
            lrev, lrevn = T("lrev", i)
            for nh in range(2):
                sl = slice(nh * 512, (nh + 1) * 512)
                mm(pp[:, sl], c_trir, ghi[:, sl], True, False, ["c_trir", ghin], [PP])
                mm(pp[:, sl], c_trir, glo[:, sl], False, True, ["c_trir", glon], [PP])
            R("scalar", lambda e: e.activation(out=lrev, in_=pp, func=AF.Exp),
              reads=[PP], writes=[lrevn])

        def S2c(i):
            ghi, ghin = T("ghi", i)
            glo, glon = T("glo", i)
            dc, dcn = T("dc", i)
            bnd, bndn = T("bnd", i)
            expbp, expbpn = T("expbp", i)
            for j in range(8):
                mm(pc[:, j:j + 1], ghi[:, j * C:(j + 1) * C], c_ones[:, 0:1],
                   j == 0, False, [ghin, "c_ones"], [PC])
                mm(pc[:, j:j + 1], glo[:, j * C:(j + 1) * C], c_ones[:, 0:1],
                   False, j == 7, [glon, "c_ones"], [PC])
            R("scalar", lambda e: e.activation(out=dc, in_=pc[:, 0:8], func=AF.Exp),
              reads=[PC], writes=[dcn])
            R("vector", lambda e: e.tensor_copy(bnd, pc[:, 0:8]), reads=[PC], writes=[bndn])
            R("scalar", lambda e: e.activation(out=expbp, in_=s_bpreT, func=AF.Exp),
              reads=["s_bpreT"], writes=[expbpn])
            R("vector", lambda e: e.tensor_add(s_bpreT, s_bpreT, bnd),
              reads=["s_bpreT", bndn], writes=["s_bpreT"])

        def S3q(i):
            xnT, xnTn = T("xnT", i)
            lam, lamn = T("lam", i)
            qh, qhn = T("qh", i)
            for j in range(8):
                for nh in range(2):
                    sl = slice(nh * 512, (nh + 1) * 512)
                    mm(pp[:, sl], xnT[:, j * C:(j + 1) * C],
                       w_q[:, j * D + nh * 512: j * D + (nh + 1) * 512],
                       j == 0, j == 7, [xnTn, "w_q"], [PP])
            R("vector", lambda e: e.tensor_mul(qh, pp, lam), reads=[PP, lamn], writes=[qhn])

        def S3k(i):
            xnT, xnTn = T("xnT", i)
            lami, lamin = T("lami", i)
            lrev, lrevn = T("lrev", i)
            kt, ktn = T("kt", i)
            kh, khn = T("kh", i)
            for j in range(8):
                for nh in range(2):
                    sl = slice(nh * 512, (nh + 1) * 512)
                    mm(pp[:, sl], xnT[:, j * C:(j + 1) * C],
                       w_k[:, j * D + nh * 512: j * D + (nh + 1) * 512],
                       j == 0, j == 7, [xnTn, "w_k"], [PP])
            R("vector", lambda e: e.tensor_mul(kt, pp, lami), reads=[PP, lamin], writes=[ktn])
            R("vector", lambda e: e.tensor_mul(kh, pp, lrev), reads=[PP, lrevn], writes=[khn])

        def S3v(i, half):
            xnT, xnTn = T("xnT", i)
            v, vn = T("v", i)
            for j in range(8):
                for nh in range(2):
                    mm(pp[:, nh * 512:(nh + 1) * 512], xnT[:, j * C:(j + 1) * C],
                       w_v[:, j * 2 * D + half * D + nh * 512:
                           j * 2 * D + half * D + (nh + 1) * 512],
                       j == 0, j == 7, [xnTn, "w_v"], [PP])
            R("scalar", lambda e, h=half: e.copy(v[:, h * D:(h + 1) * D], pp),
              reads=[PP], writes=[vn])

        def S4(i):
            for src, dst in (("qh", "qhT"), ("kt", "ktT")):
                sT, sn = T(src, i)
                dT, dn = T(dst, i)
                for j in range(8):
                    tr(ptb4[:, j * C:(j + 1) * C], sT[:, j * C:(j + 1) * C], c_identb,
                       [sn, "c_identb"], [PTB4])
                R("scalar", lambda e, d=dT, ps=ptb4: e.copy(d, ps), reads=[PTB4], writes=[dn])
            qhT, qhTn = T("qhT", i)
            qgT, qgTn = T("qgT", i)
            expbp, expbpn = T("expbp", i)
            for j in range(8):
                R("vector", lambda e, jj=j: e.tensor_scalar(
                    out=qgT[:, jj * C:(jj + 1) * C], in0=qhT[:, jj * C:(jj + 1) * C],
                    scalar1=expbp[:, jj:jj + 1], scalar2=None, op0=ALU.mult),
                  reads=[qhTn, expbpn], writes=[qgTn])
            R("scalar", lambda e, cc=i: e.dma_start(out=ap["qgT_d"][cc], in_=qgT),
              reads=[qgTn], writes=["d:qgT"], queue="qa")

        def S5a(i):
            qhT, qhTn = T("qhT", i)
            ktT, ktTn = T("ktT", i)
            pt, ptn = T("pt", i)
            for h in range(4):
                for ji, j in enumerate((2 * h, 2 * h + 1)):
                    mm(psc2[:, h * C:(h + 1) * C], ktT[:, j * C:(j + 1) * C],
                       qhT[:, j * C:(j + 1) * C], (h == 0 and ji == 0),
                       (h == 3 and ji == 1), [ktTn, qhTn], [PSC2])
            R("vector", lambda e: e.tensor_mul(pt, psc2, c_mask4),
              reads=[PSC2, "c_mask4"], writes=[ptn])

        def S5b(i, h):
            qhT, qhTn = T("qhT", i)
            kh, khn = T("kh", i)
            v, vn = T("v", i)
            pt, ptn = T("pt", i)
            o16, o16n = T("o16", i)
            dc, dcn = T("dc", i)
            mm(psc[:, 0:DV], pt[:, h * C:(h + 1) * C], v[:, h * DV:(h + 1) * DV],
               True, False, [ptn, vn], [PSC])
            for ji, j in enumerate((2 * h, 2 * h + 1)):
                mm(psc[:, 0:DV], qhT[:, j * C:(j + 1) * C], s_Sb[:, j * DV:(j + 1) * DV],
                   False, ji == 1, [qhTn, "s_Sb"], [PSC])
            R("scalar", lambda e: e.copy(o16[:, h * DV:(h + 1) * DV], psc[:, 0:DV]),
              reads=[PSC], writes=[o16n])
            for ih in range(2):
                sg = 2 * h + ih
                mm(psc[:, 0:DV], kh[:, h * DK + ih * C: h * DK + (ih + 1) * C],
                   v[:, h * DV:(h + 1) * DV], True, True, [khn, vn], [PSC])
                R("vector", lambda e, g=sg: e.scalar_tensor_tensor(
                    out=s_Sb[:, g * DV:(g + 1) * DV], in0=s_Sb[:, g * DV:(g + 1) * DV],
                    scalar=dc[:, g:g + 1], in1=psc[:, 0:DV],
                    op0=ALU.mult, op1=ALU.add),
                  reads=["s_Sb", dcn, PSC], writes=["s_Sb"])
            if h == 3:
                R("sync", lambda e, cc=i: e.dma_start(out=ap["o16_d"][cc], in_=o16),
                  reads=[o16n], writes=["d:o16"], queue="qs")

        S0(0)
        S1a_e(0)
        for i in range(NCH + 6):
            if 0 <= i + 1 < NCH:
                S0(i + 1)
            if 0 <= i - 2 < NCH:
                S3q(i - 2)
            if 0 <= i - 5 < NCH:
                S5b(i - 5, 0)
            if 0 <= i - 3 < NCH:
                S3k(i - 3)
            if 0 <= i - 5 < NCH:
                S5b(i - 5, 1)
            if 0 <= i - 3 < NCH:
                S3v(i - 3, 0)
            if 0 <= i - 5 < NCH:
                S5b(i - 5, 2)
            if 0 <= i - 3 < NCH:
                S3v(i - 3, 1)
            if 0 <= i - 5 < NCH:
                S5b(i - 5, 3)
            if 0 <= i + 1 < NCH:
                S1a_e(i + 1)
            if 0 <= i - 4 < NCH:
                S5a(i - 4)
            if 0 <= i - 1 < NCH:
                S2a(i - 1)
            if 0 <= i - 3 < NCH:
                S4(i - 3)
            if 0 <= i - 1 < NCH:
                S2b(i - 1)
            if 0 <= i - 1 < NCH:
                S2c(i - 1)
            if 0 <= i < NCH:
                S1a_t(i)
            if 0 <= i < NCH:
                S1b(i)

        R("sync", lambda e: e.dma_start(out=ap["T_d"], in_=s_Sb),
          reads=["s_Sb"], writes=["d:T"], queue="qs")
        for j in range(8):
            tr(pc[0:1, j * C:(j + 1) * C], s_bpreT[:, j:j + 1], c_identf,
               ["s_bpreT", "c_identf"], [PC])
        R("vector", lambda e: e.tensor_copy(s_btrow, pc[0:1, :]),
          reads=[PC], writes=["s_btrow"])
        R("sync", lambda e: e.dma_start(out=ap["btot_d"], in_=s_btrow),
          reads=["s_btrow"], writes=["d:btot"], queue="qs")
        p.barrier()

        block = es.enter_context(nc.Block())
        p.finalize(es, block)
    return nc


def build_phase_b():
    nc = bass.Bass("TRN2", target_bir_lowering=False, debug=False)
    ap = {}
    for n, shp, dt in [("o16_d", [NCH, C, 2 * D], F16), ("qgT_d", [NCH, C, D], BF16),
                       ("base_d", [NCH, C, D], F32), ("xnT_d", [NCH, C, D], BF16),
                       ("T_all", [NCORES * C, 8 * DV], BF16), ("btot_all", [8, D], F32),
                       ("x0", [1, D], F32),
                       ("wg", [D, 2 * D], BF16), ("wo", [2 * D, D], BF16),
                       ("alay", [D, 512], BF16), ("bwT", [512, D], BF16),
                       ("wrT", [D, 8], F32), ("gnw", [128, 4], F32),
                       ("cmatx", [8, 8], BF16), ("wmrow", [1, 8], BF16),
                       ("conesb", [C, C], BF16),
                       ("cidentb", [C, C], BF16), ("cidentf", [C, C], F32),
                       ("cesel", [8, 512], BF16)]:
        ap[n] = nc.dram_tensor(n, shp, dt, kind="ExternalInput").ap()
    ap["y"] = nc.dram_tensor("y", [TOK, D], F32, kind="ExternalOutput").ap()

    es = ExitStack()
    with es:
        def sb(n, s, t):
            return es.enter_context(nc.sbuf_tensor(n, s, t))[:]

        def psm(n, s, t):
            return es.enter_context(nc.psum_tensor(n, s, t))[:]

        w_g = sb("w_g", [128, 8 * 2 * D], BF16)
        w_wo = sb("w_wo", [128, 16 * D], BF16)
        w_alay = sb("w_alay", [128, 8 * 512], BF16)
        w_bwT = sb("w_bwT", [128, 4 * D], BF16)
        w_wrT = sb("w_wrT", [128, 8 * 8], F32)
        s_gnw = sb("s_gnw", [128, 4], F32)
        c_identb = sb("c_identb", [C, C], BF16)
        c_identf = sb("c_identf", [C, C], F32)
        c_esel = sb("c_esel", [8, 512], BF16)
        c_cmat = sb("c_cmat", [8, 8], BF16)
        c_ones = sb("c_ones", [C, C], BF16)
        s_wmrow = sb("s_wmrow", [1, 8], BF16)
        s_x0 = sb("s_x0", [1, D], F32)
        s_x0b = sb("s_x0b", [1, D], F32)
        g_st = sb("g_st", [1, 8], F32)
        g_junk = sb("g_junk", [1, D], F32)
        s_x0T = sb("s_x0T", [128, 8], F32)
        g_lg8 = sb("g_lg8", [8, 1], F32)
        g_lg = sb("g_lg", [1, 8], F32)
        g_d = sb("g_d", [1, 8], F32)
        g_ind = sb("g_ind", [1, 8], F32)
        g_l2 = sb("g_l2", [1, 8], F32)
        g_keep = sb("g_keep", [1, 8], F32)
        g_e = sb("g_e", [1, 8], F32)
        g_gt = sb("g_gt", [1, 8], F32)
        g_gc8 = sb("g_gc8", [8, 1], F32)
        g_gcb = sb("g_gcb", [8, 4], BF16)
        s_gc = sb("s_gc", [128, 4], F32)
        b_bt = sb("b_bt", [8, D], F32)
        b_bthi = sb("b_bthi", [8, D], BF16)
        b_btlo = sb("b_btlo", [8, D], BF16)
        b_w = sb("b_w", [128, 64], F32)
        b_Tr = [sb("b_Tr_0", [128, 8 * DV], BF16), sb("b_Tr_1", [128, 8 * DV], BF16)]
        b_dg = [sb(f"b_dg{t}", [128, 128], BF16) for t in range(4)]
        s_Sin = sb("s_Sin", [128, 8 * DV], BF16)

        spec = {
            "xnT": ([C, D], BF16, 3), "o16": ([C, 2 * D], F16, 4),
            "qgT": ([C, D], BF16, 4), "base": ([C, D], F32, 5),
            "ob": ([C, 2 * D], BF16, 2), "rs": ([C, 4], F32, 2),
            "sd": ([C, 4], F32, 2), "rst": ([C, 4], F32, 2),
            "junk": ([C, DV], F32, 1), "sg": ([C, 2 * D], BF16, 2),
            "ogT": ([C, 2 * D], BF16, 1),
            "mid": ([C, 512], BF16, 3), "midT": ([C, 512], BF16, 1),
        }
        tiles = {}
        for nm, (shp, dt2, nb) in spec.items():
            tiles[nm] = [sb(f"b_{nm}_{b}", shp, dt2) for b in range(nb)]

        def T(nm, i):
            nb = spec[nm][2]
            return tiles[nm][i % nb], f"b_{nm}_{i % nb}"

        pgA = psm("pgA", [128, 1024], F32)
        pd = psm("pd", [128, 1024], F32)
        pm = psm("pm", [128, 512], F32)
        pwo = psm("pwo", [128, 1024], F32)
        ptbB = psm("ptbB", [128, 1024], BF16)
        PGA, PD, PM, PWO, PTBB = "ps_pgA", "ps_pd", "ps_pm", "ps_pwo", "ps_ptbB"

        p = Prog(nc)
        R = p.rec

        def mm(out, lhsT, rhs, start, stop, rd, wr):
            R("tensor", lambda e, o=out, l=lhsT, r=rhs, s1=start, s2=stop:
              e.matmul(o, l, r, start=s1, stop=s2), reads=rd, writes=wr)

        def tr(out, in_, ident, rd, wr):
            R("tensor", lambda e, o=out, i=in_, d=ident: e.transpose(o, i, d),
              reads=rd, writes=wr)

        def ldg(dst, src, rd, wr):
            R("gpsimd", lambda e, d=dst, s=src: e.dma_start(out=d, in_=s),
              reads=rd, writes=wr, queue="qg")

        def lda(dst, src, rd, wr):
            R("scalar", lambda e, d=dst, s=src: e.dma_start(out=d, in_=s),
              reads=rd, writes=wr, queue="qa")

        ldg(c_identb, ap["cidentb"], ["d:cidentb"], ["c_identb"])
        ldg(c_esel, ap["cesel"], ["d:cesel"], ["c_esel"])
        ldg(c_cmat, ap["cmatx"], ["d:cmatx"], ["c_cmat"])
        ldg(c_ones, ap["conesb"], ["d:conesb"], ["c_ones"])
        ldg(s_wmrow, ap["wmrow"], ["d:wmrow"], ["s_wmrow"])
        ldg(w_alay.rearrange("p (kt n) -> p kt n", kt=8),
            ap["alay"].rearrange("(kt p) n -> p kt n", p=128), ["d:alay"], ["w_alay"])
        ldg(w_bwT.rearrange("p (kt n) -> p kt n", kt=4),
            ap["bwT"].rearrange("(kt p) n -> p kt n", p=128), ["d:bwT"], ["w_bwT"])
        for half in range(2):
            lda(w_g[:, half * 8 * D:(half + 1) * 8 * D]
                .rearrange("p (kt n) -> p kt n", kt=4),
                ap["wg"][half * 512:(half + 1) * 512, :]
                .rearrange("(kt p) n -> p kt n", p=128),
                ["d:wg"], [f"w_gh{half}"])
        R("sync", lambda e: e.dma_start(
            out=w_wrT.rearrange("p (kt n) -> p kt n", kt=8),
            in_=ap["wrT"].rearrange("(kt p) n -> p kt n", p=128)),
          reads=["d:wrT"], writes=["w_wrT"], queue="qs")
        R("sync", lambda e: e.dma_start(out=s_gnw, in_=ap["gnw"]),
          reads=["d:gnw"], writes=["s_gnw"], queue="qs")
        R("sync", lambda e: e.dma_start(out=c_identf, in_=ap["cidentf"]),
          reads=["d:cidentf"], writes=["c_identf"], queue="qs")
        R("sync", lambda e: e.dma_start(out=s_x0, in_=ap["x0"]),
          reads=["d:x0"], writes=["s_x0"], queue="qs")
        R("sync", lambda e: e.dma_start(out=b_bt, in_=ap["btot_all"]),
          reads=["d:btot"], writes=["b_bt"], queue="qs")

        def emit_gates_a():
            R("vector", lambda e: e.reduce_sum(g_st[:, 0:1], s_x0, axis=AX.X),
              reads=["s_x0"], writes=["g_st"])
            R("scalar", lambda e: e.activation(out=g_junk, in_=s_x0, func=AF.Square,
                                               accum_out=g_st[:, 1:2]),
              reads=["s_x0"], writes=["g_junk", "g_st"])
            R("vector", lambda e: e.tensor_scalar(out=g_st[:, 2:3], in0=g_st[:, 0:1],
                                                  scalar1=1.0 / D, scalar2=None, op0=ALU.mult),
              reads=["g_st"], writes=["g_st"])
            R("vector", lambda e: e.tensor_scalar(out=g_st[:, 3:4], in0=g_st[:, 1:2],
                                                  scalar1=1.0 / D, scalar2=None, op0=ALU.mult),
              reads=["g_st"], writes=["g_st"])
            R("vector", lambda e: e.tensor_mul(g_st[:, 4:5], g_st[:, 2:3], g_st[:, 2:3]),
              reads=["g_st"], writes=["g_st"])
            R("vector", lambda e: e.tensor_sub(g_st[:, 4:5], g_st[:, 3:4], g_st[:, 4:5]),
              reads=["g_st"], writes=["g_st"])
            R("vector", lambda e: e.tensor_scalar(out=g_st[:, 4:5], in0=g_st[:, 4:5],
                                                  scalar1=1.0, scalar2=EPS, op0=ALU.mult,
                                                  op1=ALU.add),
              reads=["g_st"], writes=["g_st"])
            R("scalar", lambda e: e.activation(out=g_st[:, 5:6], in_=g_st[:, 4:5], func=AF.Ln),
              reads=["g_st"], writes=["g_st"])
            R("scalar", lambda e: e.activation(out=g_st[:, 6:7], in_=g_st[:, 5:6], func=AF.Exp,
                                               scale=-0.5),
              reads=["g_st"], writes=["g_st"])
            R("vector", lambda e: e.tensor_scalar(out=s_x0b, in0=s_x0, scalar1=g_st[:, 2:3],
                                                  scalar2=g_st[:, 6:7], op0=ALU.subtract,
                                                  op1=ALU.mult),
              reads=["s_x0", "g_st"], writes=["s_x0b"])
            for j in range(8):
                tr(pm[:, j:j + 1], s_x0b[:, j * C:(j + 1) * C], c_identf[0:1, 0:1],
                   ["s_x0b", "c_identf"], [PM])
            R("scalar", lambda e: e.copy(s_x0T, pm[:, 0:8]), reads=[PM], writes=["s_x0T"])
            lda(w_wo.rearrange("p (kt n) -> p kt n", kt=16),
                ap["wo"].rearrange("(kt p) n -> p kt n", p=128),
                ["d:wo"], ["w_wo"])

        def emit_gates_b():
            for j in range(8):
                mm(pm[0:8, 0:1], w_wrT[:, j * 8:(j + 1) * 8], s_x0T[:, j:j + 1],
                   j == 0, j == 7, ["w_wrT", "s_x0T"], [PM])
            R("scalar", lambda e: e.copy(g_lg8, pm[0:8, 0:1]), reads=[PM], writes=["g_lg8"])
            tr(pm[0:1, 0:8], g_lg8, c_identf[0:8, 0:8], ["g_lg8", "c_identf"], [PM])
            R("scalar", lambda e: e.copy(g_lg, pm[0:1, 0:8]), reads=[PM], writes=["g_lg"])

        def emit_gates_c():
            R("vector", lambda e: e.reduce_max(g_st[:, 7:8], g_lg, axis=AX.X),
              reads=["g_lg"], writes=["g_st"])
            R("vector", lambda e: e.tensor_scalar(out=g_d, in0=g_lg, scalar1=g_st[:, 7:8],
                                                  scalar2=None, op0=ALU.subtract),
              reads=["g_lg", "g_st"], writes=["g_d"])
            R("vector", lambda e: e.tensor_scalar(out=g_ind, in0=g_d, scalar1=0.0,
                                                  scalar2=None, op0=ALU.is_ge),
              reads=["g_d"], writes=["g_ind"])
            R("vector", lambda e: e.scalar_tensor_tensor(out=g_l2, in0=g_ind, scalar=-1e30,
                                                         in1=g_lg, op0=ALU.mult, op1=ALU.add),
              reads=["g_ind", "g_lg"], writes=["g_l2"])
            R("vector", lambda e: e.reduce_max(g_st[:, 0:1], g_l2, axis=AX.X),
              reads=["g_l2"], writes=["g_st"])
            R("vector", lambda e: e.tensor_scalar(out=g_keep, in0=g_lg, scalar1=g_st[:, 0:1],
                                                  scalar2=None, op0=ALU.is_ge),
              reads=["g_lg", "g_st"], writes=["g_keep"])
            R("scalar", lambda e: e.activation(out=g_e, in_=g_d, func=AF.Exp),
              reads=["g_d"], writes=["g_e"])
            R("vector", lambda e: e.tensor_mul(g_e, g_e, g_keep),
              reads=["g_e", "g_keep"], writes=["g_e"])
            R("vector", lambda e: e.reduce_sum(g_st[:, 1:2], g_e, axis=AX.X),
              reads=["g_e"], writes=["g_st"])
            R("vector", lambda e: e.reciprocal(g_st[:, 2:3], g_st[:, 1:2]),
              reads=["g_st"], writes=["g_st"])
            R("vector", lambda e: e.tensor_scalar(out=g_gt, in0=g_e, scalar1=g_st[:, 2:3],
                                                  scalar2=None, op0=ALU.mult),
              reads=["g_e", "g_st"], writes=["g_gt"])
            tr(pm[0:8, 8:9], g_gt, c_identf[0:1, 0:1], ["g_gt", "c_identf"], [PM])
            R("scalar", lambda e: e.copy(g_gc8, pm[0:8, 8:9]), reads=[PM], writes=["g_gc8"])
            R("vector", lambda e: e.tensor_copy(g_gcb[:, 0:1], g_gc8),
              reads=["g_gc8"], writes=["g_gcb"])
            R("vector", lambda e: e.tensor_sub(g_gcb[:, 2:3], g_gc8, g_gcb[:, 0:1]),
              reads=["g_gc8", "g_gcb"], writes=["g_gcb"])
            for tau in range(4):
                for hl in range(2):
                    mm(pm[:, tau:tau + 1], c_esel[:, tau * 128:(tau + 1) * 128],
                       g_gcb[:, 2 * hl:2 * hl + 1], (tau == 0 and hl == 0),
                       (tau == 3 and hl == 1), ["c_esel", "g_gcb"], [PM])
            R("scalar", lambda e: e.copy(s_gc, pm[:, 0:4]), reads=[PM], writes=["s_gc"])
            for tau in range(4):
                R("vector", lambda e, u=tau: e.tensor_scalar(
                    out=w_bwT[:, u * D:(u + 1) * D], in0=w_bwT[:, u * D:(u + 1) * D],
                    scalar1=s_gc[:, u:u + 1], scalar2=None, op0=ALU.mult),
                  reads=["w_bwT", "s_gc"], writes=["w_bwT"])
            for k in range(16):
                R("vector", lambda e, kk=k: e.tensor_scalar(
                    out=w_wo[:, kk * D:(kk + 1) * D], in0=w_wo[:, kk * D:(kk + 1) * D],
                    scalar1=s_gnw[:, kk % 4:kk % 4 + 1], scalar2=None, op0=ALU.mult),
                  reads=["w_wo", "s_gnw"], writes=[f"w_wo{k}"])


        def emit_combine_pre():
            R("vector", lambda e: e.tensor_copy(b_bthi, b_bt), reads=["b_bt"], writes=["b_bthi"])
            R("vector", lambda e: e.tensor_sub(b_btlo, b_bt, b_bthi),
              reads=["b_bt", "b_bthi"], writes=["b_btlo"])
            for tau in range(8):
                mm(pm[:, tau * 8:(tau + 1) * 8], b_bthi[:, tau * 128:(tau + 1) * 128],
                   c_cmat, tau == 0, False, ["b_bthi", "c_cmat"], [PM])
                mm(pm[:, tau * 8:(tau + 1) * 8], b_btlo[:, tau * 128:(tau + 1) * 128],
                   c_cmat, False, False, ["b_btlo", "c_cmat"], [PM])
                mm(pm[:, tau * 8:(tau + 1) * 8], c_ones[0:1, :], s_wmrow,
                   False, tau == 7, ["c_ones", "s_wmrow"], [PM])
            R("scalar", lambda e: e.activation(out=b_w, in_=pm[:, 0:64], func=AF.Exp),
              reads=[PM], writes=["b_w"])
            R("vector", lambda e: e.memset(s_Sin[:, 4 * DV:8 * DV], 0.0), writes=["s_Sin"])

        COMB_ORDER = [x for pair in zip(range(NCORES // 2), range(NCORES // 2, NCORES))
                      for x in pair] if NCORES > 1 else [0]
        COMB_PG = [(pd, PD, 0), (pd, PD, 512), (pwo, PWO, 0), (pwo, PWO, 512)]

        def emit_combine_rank(idx):
            if idx >= NCORES:
                return
            r = COMB_ORDER[idx]
            tb = b_Tr[idx % 2]
            tbn = f"b_Tr_{idx % 2}"
            R("gpsimd" if r < 4 else "sync",
              lambda e, rr=r, d=tb: e.dma_start(
                  out=d, in_=ap["T_all"][rr * C:(rr + 1) * C, :]),
              reads=["d:Tall"], writes=[tbn],
              queue="qg" if r < 4 else "qs")
            for g in range(4):
                dg = b_dg[g]
                dgn = f"b_dg{g}"
                R("vector", lambda e, gg=g, rr=r, d=dg: e.tensor_scalar(
                    out=d, in0=c_identb, scalar1=b_w[:, gg * 8 + rr:gg * 8 + rr + 1],
                    scalar2=None, op0=ALU.mult),
                  reads=["c_identb", "b_w"], writes=[dgn])
                pg, pgn, off = COMB_PG[g]
                mm(pg[:, off:off + DV], dg, tb[:, g * DV:(g + 1) * DV],
                   idx == 0, idx == NCORES - 1, [dgn, tbn], [pgn])
            for g in range(4, 8):
                R("vector", lambda e, gg=g, rr=r, d=tb: e.scalar_tensor_tensor(
                    out=s_Sin[:, gg * DV:(gg + 1) * DV], in0=d[:, gg * DV:(gg + 1) * DV],
                    scalar=b_w[:, gg * 8 + rr:gg * 8 + rr + 1],
                    in1=s_Sin[:, gg * DV:(gg + 1) * DV],
                    op0=ALU.mult, op1=ALU.add),
                  reads=[tbn, "b_w", "s_Sin"], writes=["s_Sin"])

        def emit_combine_evac():
            for g in range(4):
                pg, pgn, off = COMB_PG[g]
                R("vector", lambda e, p=pg, o=off, gg=g: e.tensor_copy(
                    s_Sin[:, gg * DV:(gg + 1) * DV], p[:, o:o + DV]),
                  reads=[pgn], writes=["s_Sin"])



        def B0(i):
            for nm, key in [("xnT", "xnT_d"), ("o16", "o16_d"),
                            ("qgT", "qgT_d"), ("base", "base_d")]:
                d, dn = T(nm, i)
                R("sync", lambda e, cc=i, dd=d, k=key: e.dma_start(out=dd, in_=ap[k][cc]),
                  reads=["d:" + key], writes=[dn], queue="qs")

        def B1a(i):
            xnT, xnTn = T("xnT", i)
            sg, sgn = T("sg", i)
            for j in range(8):
                for nh in range(2):
                    mm(pgA[:, nh * 512:(nh + 1) * 512], xnT[:, j * C:(j + 1) * C],
                       w_g[:, j * 2 * D + nh * 512: j * 2 * D + (nh + 1) * 512],
                       j == 0, j == 7, [xnTn, f"w_gh{j // 4}"], [PGA])
            R("scalar", lambda e: e.activation(out=sg[:, 0:D], in_=pgA, func=AF.Silu),
              reads=[PGA], writes=[sgn])

        def B1b(i):
            xnT, xnTn = T("xnT", i)
            sg, sgn = T("sg", i)
            mid, midn = T("mid", i)
            for j in range(8):
                for nh in range(2):
                    mm(pgA[:, nh * 512:(nh + 1) * 512], xnT[:, j * C:(j + 1) * C],
                       w_g[:, j * 2 * D + D + nh * 512: j * 2 * D + D + (nh + 1) * 512],
                       j == 0, j == 7, [xnTn, f"w_gh{j // 4}"], [PGA])
            R("scalar", lambda e: e.activation(out=sg[:, D:2 * D], in_=pgA, func=AF.Silu),
              reads=[PGA], writes=[sgn])
            for j in range(8):
                mm(pm[:, 0:512], xnT[:, j * C:(j + 1) * C],
                   w_alay[:, j * 512:(j + 1) * 512], j == 0, j == 7,
                   [xnTn, "w_alay"], [PM])
            R("vector", lambda e: e.tensor_copy(mid, pm[:, 0:512]), reads=[PM], writes=[midn])

        def B1d(i):
            o16, o16n = T("o16", i)
            qgT, qgTn = T("qgT", i)
            ob, obn = T("ob", i)
            for rnd in range(2):
                for hi in range(2):
                    h = 2 * rnd + hi
                    for ji, j in enumerate((2 * h, 2 * h + 1)):
                        mm(pd[:, hi * DV:(hi + 1) * DV], qgT[:, j * C:(j + 1) * C],
                           s_Sin[:, j * DV:(j + 1) * DV], ji == 0, ji == 1,
                           [qgTn, "s_Sin"], [PD])
                for hi in range(2):
                    h = 2 * rnd + hi
                    R("vector", lambda e, h2=h, i2=hi: e.tensor_tensor(
                        ob[:, h2 * DV:(h2 + 1) * DV], o16[:, h2 * DV:(h2 + 1) * DV],
                        pd[:, i2 * DV:(i2 + 1) * DV], ALU.add),
                      reads=[o16n, PD], writes=[obn])

        def B1c(i):
            ob, obn = T("ob", i)
            rs, rsn = T("rs", i)
            sd, sdn = T("sd", i)
            rst, rstn = T("rst", i)
            junk, junkn = T("junk", i)
            sg, sgn = T("sg", i)
            t1, t1n = ob, obn
            for h in range(4):
                R("scalar", lambda e, hh=h: e.activation(
                    out=junk, in_=ob[:, hh * DV:(hh + 1) * DV], func=AF.Square,
                    accum_out=rs[:, hh:hh + 1]),
                  reads=[obn], writes=[junkn, rsn])
            R("vector", lambda e: e.tensor_scalar(out=rs, in0=rs, scalar1=1.0 / DV,
                                                  scalar2=EPS, op0=ALU.mult, op1=ALU.add),
              reads=[rsn], writes=[rsn])
            R("scalar", lambda e: e.activation(out=sd, in_=rs, func=AF.Ln),
              reads=[rsn], writes=[sdn])
            R("scalar", lambda e: e.activation(out=rst, in_=sd, func=AF.Exp, scale=-0.5),
              reads=[sdn], writes=[rstn])
            R("vector", lambda e: e.tensor_mul(t1, ob, sg),
              reads=[obn, sgn], writes=[t1n])
            for h in range(4):
                R("vector", lambda e, hh=h: e.tensor_scalar(
                    out=t1[:, hh * DV:(hh + 1) * DV], in0=t1[:, hh * DV:(hh + 1) * DV],
                    scalar1=rst[:, hh:hh + 1], scalar2=None, op0=ALU.mult),
                  reads=[t1n, rstn], writes=[t1n])

        def B2mid(i):
            mid, midn = T("mid", i)
            midT, midTn = T("midT", i)
            for j in range(4):
                tr(ptbB[:, j * C:(j + 1) * C], mid[:, j * C:(j + 1) * C], c_identb,
                   [midn, "c_identb"], [PTBB])
            R("scalar", lambda e: e.copy(midT, ptbB[:, 0:512]), reads=[PTBB], writes=[midTn])

        def B2og(i, half):
            t1, t1n = T("ob", i)
            ogT, ogTn = T("ogT", i)
            for j in range(8):
                tr(ptbB[:, j * C:(j + 1) * C],
                   t1[:, half * D + j * C: half * D + (j + 1) * C],
                   c_identb, [t1n, "c_identb"], [PTBB])
            R("scalar", lambda e, h=half: e.copy(ogT[:, h * D:(h + 1) * D], ptbB),
              reads=[PTBB], writes=[ogTn])

        def B2wo(i):
            ogT, ogTn = T("ogT", i)
            midT, midTn = T("midT", i)
            base, basen = T("base", i)
            for nh in range(2):
                sl = slice(nh * 512, (nh + 1) * 512)
                for k in range(16):
                    mm(pwo[:, sl], ogT[:, k * C:(k + 1) * C],
                       w_wo[:, k * D + nh * 512: k * D + (nh + 1) * 512],
                       k == 0, False, [ogTn, f"w_wo{k}"], [PWO])
                for j in range(4):
                    mm(pwo[:, sl], midT[:, j * C:(j + 1) * C],
                       w_bwT[:, j * D + nh * 512: j * D + (nh + 1) * 512],
                       False, j == 3, [midTn, "w_bwT"], [PWO])
            R("vector", lambda e: e.tensor_add(base, base, pwo),
              reads=[basen, PWO], writes=[basen])
            R("sync", lambda e, cc=i: e.dma_start(
                out=ap["y"][cc * C:(cc + 1) * C, :], in_=base),
              reads=[basen], writes=["d:y"], queue="qs")

        B0(0)
        B0(1)
        emit_gates_a()
        emit_combine_pre()
        for i in range(NCH + 3):
            if 0 <= i + 2 < NCH:
                B0(i + 2)
            if i == 1:
                emit_combine_rank(4)
                emit_combine_rank(5)
            if 0 <= i - 2 < NCH:
                B2mid(i - 2)
            if 0 <= i < NCH:
                B1a(i)
            if i == 0:
                emit_gates_b()
                emit_combine_rank(0)
                emit_combine_rank(1)
            if i == 1:
                emit_combine_rank(6)
                emit_combine_rank(7)
                emit_combine_evac()
            if 0 <= i - 2 < NCH:
                B2og(i - 2, 0)
            if 0 <= i < NCH:
                B1b(i)
            if i == 0:
                emit_gates_c()
                emit_combine_rank(2)
                emit_combine_rank(3)
            if 0 <= i - 2 < NCH:
                B2og(i - 2, 1)
            if 0 <= i - 1 < NCH:
                B1d(i - 1)
            if 0 <= i - 2 < NCH:
                B2wo(i - 2)
            if 0 <= i - 1 < NCH:
                B1c(i - 1)

        p.barrier()
        block = es.enter_context(nc.Block())
        p.finalize(es, block)
    return nc


def _prep_shared(inputs):
    con = _tile_consts()
    w = {}
    w["wq"] = np.ascontiguousarray(inputs["L0_attn_Wq"], np.float32)
    w["wk"] = np.ascontiguousarray(inputs["L0_attn_Wk"], np.float32)
    w["wv"] = np.ascontiguousarray(inputs["L0_attn_Wv"], np.float32)
    w["wgk1"] = np.ascontiguousarray(inputs["L0_attn_Wgk1"], np.float32)
    w["wgk2"] = np.ascontiguousarray(inputs["L0_attn_Wgk2"], np.float32)
    w["bgk2"] = np.ascontiguousarray(inputs["L0_attn_bgk2"], np.float32).reshape(1, D)
    w["wg"] = np.ascontiguousarray(inputs["L0_attn_Wg"], np.float32)
    w["wo"] = np.ascontiguousarray(inputs["L0_attn_Wo"], np.float32)
    w["alay"] = np.ascontiguousarray(
        np.asarray(inputs["L0_ffn_A"], np.float32).reshape(512, D).T)
    w["bwT"] = np.ascontiguousarray(
        np.asarray(inputs["L0_ffn_Bw"], np.float32).transpose(0, 2, 1).reshape(512, D))
    w["wrT"] = np.ascontiguousarray(np.asarray(inputs["L0_ffn_Wr"], np.float32).T)
    w["gnw"] = np.ascontiguousarray(
        np.asarray(inputs["L0_attn_g_norm_weight"], np.float32).reshape(4, 128).T)
    x = np.asarray(inputs["L0_attn_x"], np.float32).reshape(NCORES * TOK, D)
    w["x0"] = np.ascontiguousarray(x[0:1, :])
    return con, w, x


def kernel(**inputs):
    con, w, x = _prep_shared(inputs)

    nc_a = build_phase_a()
    in_a = []
    for cidx in range(NCORES):
        m = dict(
            x=np.ascontiguousarray(x[cidx * TOK:(cidx + 1) * TOK]),
            wq=w["wq"], wk=w["wk"], wv=w["wv"], wgk1=w["wgk1"], wgk2=w["wgk2"],
            bgk2=w["bgk2"],
            ctrib=con["trib"], ctrir=con["trir"], cidentb=con["identb"],
            conesb=con["onesb"], cmask=con["mask"], cidentf=con["identf"],
            cln16=np.full((C, 1), -LN16, np.float32),
            wg=w["wg"], wo=w["wo"], alay=w["alay"], bwT=w["bwT"],
        )
        in_a.append(m)
    res_a = run_bass_kernel_spmd(nc_a, in_a, core_ids=list(range(NCORES)))
    ra = res_a.results

    T_all = np.concatenate([np.asarray(ra[cidx]["T_d"]) for cidx in range(NCORES)], axis=0)
    btot_all = np.zeros((8, D), np.float32)
    for cidx in range(NCORES):
        btot_all[cidx] = np.asarray(ra[cidx]["btot_d"])[0]

    nc_b = build_phase_b()
    in_b = []
    for cidx in range(NCORES):
        m = dict(
            o16_d=np.asarray(ra[cidx]["o16_d"]),
            qgT_d=np.asarray(ra[cidx]["qgT_d"]),
            base_d=np.asarray(ra[cidx]["base_d"]),
            xnT_d=np.asarray(ra[cidx]["xnT_d"]),
            T_all=T_all, btot_all=btot_all,
            x0=w["x0"], wg=np.asarray(ra[cidx]["wgb"]), wo=np.asarray(ra[cidx]["wob"]),
            alay=np.asarray(ra[cidx]["alayb"]), bwT=np.asarray(ra[cidx]["bwTb"]),
            wrT=w["wrT"], gnw=w["gnw"],
            cmatx=_cmat(cidx),
            wmrow=np.where(np.arange(8) >= cidx, -1e4, 0.0).astype(np.float32)
                    .astype(BF).reshape(1, 8),
            conesb=con["onesb"],
            cidentb=con["identb"], cidentf=con["identf"], cesel=con["esel"],
        )
        in_b.append(m)
    res_b = run_bass_kernel_spmd(nc_b, in_b, core_ids=list(range(NCORES)))
    y = np.concatenate([np.asarray(res_b.results[cidx]["y"]) for cidx in range(NCORES)],
                       axis=0)
    return y.reshape(1, NCORES * TOK, D)
